# version 15 (final)
# speedup vs baseline: 1.2311x; 1.0114x over previous
import contextlib
import numpy as np
import concourse.bass as bass
import concourse.mybir as mybir
from concourse.bass_utils import run_bass_kernel_spmd

F32 = mybir.dt.float32
BF16 = mybir.dt.bfloat16
I32 = mybir.dt.int32
AF = mybir.ActivationFunctionType
OP = mybir.AluOpType
AX = mybir.AxisListType

D = 1024
T = 2048
NBATCH = 16
NCORES = 8
BPC = NBATCH // NCORES
EC = D // 128
NTT = T // 512
NTC = T // 128
I0 = 2816
IC0 = I0 // 128
HID = 2730
DH = 64
WIN = 512

SHIFT_L0 = 8.0
SHIFT_MHA = 8.0
SHIFT_LOC = 12.0

FP4_MIDS = (0.25, 0.75, 1.25, 1.75, 2.5, 3.5, 5.0)
FP4_INCS = (0.5, 0.5, 0.5, 0.5, 1.0, 1.0, 2.0)

TRACE_ARENA = False
DBG = {}


def _cody_waite_2pi():
    import struct
    two_pi = 6.283185307179586476925287
    def trunc_bits(x, bits):
        u = struct.unpack(">I", struct.pack(">f", np.float32(x)))[0]
        mask = (0xFFFFFFFF << (23 - bits)) & 0xFFFFFFFF
        return struct.unpack(">f", struct.pack(">I", u & mask))[0]
    c1 = trunc_bits(two_pi, 10)
    c2 = trunc_bits(two_pi - c1, 10)
    c3 = float(np.float32(two_pi - c1 - c2))
    return float(np.float32(c1)), float(np.float32(c2)), c3


CW1, CW2, CW3 = _cody_waite_2pi()


def build_consts():
    c = {}
    c["iden"] = np.eye(128, dtype=np.float32)
    c["ones_row"] = np.ones((1, 128), dtype=np.float32)
    c["ones_col"] = np.ones((128, 1), dtype=np.float32)
    c["iota_t"] = np.arange(T, dtype=np.float32).reshape(1, T)
    idx = np.arange(32, dtype=np.float32)
    c["iota_dh"] = np.tile(idx, 4).reshape(1, 128)
    P = np.zeros((128, 128), dtype=np.float32)
    for h in range(2):
        b = h * 64
        for d_ in range(32):
            P[b + d_, b + d_ + 32] = -1.0
            P[b + d_ + 32, b + d_] = 1.0
    c["rot_pi"] = P.T.copy()
    m = np.zeros((8, 128, 512), dtype=np.float32)
    o = (np.arange(8)[:, None, None] * 128 + np.arange(128)[None, :, None])
    i_ = np.arange(512)[None, None, :]
    m[:] = ((i_ >= o - 512) & (i_ <= o)).astype(np.float32)
    c["loc_mask"] = m.reshape(8 * 128, 512)
    e16 = np.zeros((4, 2, 128, 16), dtype=np.float32)
    for g in range(4):
        for s in range(2):
            e16[g, s, 0:64, 8 * s + 2 * g] = 1.0
            e16[g, s, 64:128, 8 * s + 2 * g + 1] = 1.0
    c["e16"] = e16.reshape(4 * 2 * 128, 16)
    eb = np.zeros((4, 2, 16, 128), dtype=np.float32)
    for g in range(4):
        for s in range(2):
            eb[g, s, 8 * s + 2 * g, 0:64] = 1.0
            eb[g, s, 8 * s + 2 * g + 1, 64:128] = 1.0
    c["ebr"] = eb.reshape(4 * 2 * 16, 128)
    return c


class Ev:
    __slots__ = ("sem", "val", "eng", "dma")
    def __init__(self, sem, val, eng, dma):
        self.sem, self.val, self.eng, self.dma = sem, val, eng, dma


class StopBuild(Exception):
    def __init__(self, evs):
        self.evs = evs


class Planner:
    ENGS = ("sp", "gp", "pe", "dve", "act")

    def __init__(self, nc, stack):
        self.nc = nc
        self.prog = {e: [] for e in self.ENGS}
        self.sems = {}
        names = ["s_sp", "s_gp", "s_pe", "s_dve", "s_act"]
        self.dma_groups = {}
        for stream in ("w", "x"):
            for eng in ("sp", "gp"):
                grp = [f"d{stream}{i}_{eng}" for i in range(4)]
                self.dma_groups[(stream, eng)] = grp
                names += grp
        for name in names:
            self.sems[name] = [stack.enter_context(nc.semaphore(name)), 0]
        self.rot = {k: 0 for k in self.dma_groups}
        self.marks = {}
        self.nops = 0

    def _wait(self, eng, ev):
        if ev is None:
            return
        if isinstance(ev, (list, tuple)):
            for e2 in ev:
                self._wait(eng, e2)
            return
        if ev.eng == eng and not ev.dma:
            return
        val = ev.val
        key = (eng, ev.sem)
        if self.marks.get(key, -1) >= val:
            return
        self.marks[key] = val
        sem_h = self.sems[ev.sem][0]
        self.prog[eng].append(lambda e: e.wait_ge(sem_h, val))

    def comp(self, eng, fn, reads=None, sig=False, tag=""):
        assert eng in ("pe", "dve", "act", "gp")
        self._wait(eng, reads)
        self.nops += 1
        if not sig:
            self.prog[eng].append(fn)
            return None
        semname = {"pe": "s_pe", "dve": "s_dve", "act": "s_act",
                   "gp": "s_gp"}[eng]
        sem = self.sems[semname]
        sem[1] += 1
        val = sem[1]
        sem_h = sem[0]
        if eng == "pe":
            def run(e):
                r = fn(e)
                bi = r[-1] if isinstance(r, (list, tuple)) else r
                bi.then_inc(sem_h, 1)
            self.prog[eng].append(run)
        else:
            self.prog[eng].append(fn)
            self.prog[eng].append(lambda e: e.drain().then_inc(sem_h, 1))
        return Ev(semname, val, eng, False)

    def dma(self, eng, fn, reads=None, stream="x", tag=""):
        assert eng in ("sp", "gp")
        self._wait(eng, reads)
        grp = self.dma_groups[(stream, eng)]
        i = self.rot[(stream, eng)]
        self.rot[(stream, eng)] = (i + 1) % len(grp)
        semname = grp[i]
        sem = self.sems[semname]
        sem_h = sem[0]
        prev = sem[1]
        self._wait(eng, Ev(semname, prev, "-", True))
        def run(e):
            bi = fn(e)
            bi.then_inc(sem_h, 16)
        self.prog[eng].append(run)
        sem[1] += 16
        self.nops += 1
        return Ev(semname, sem[1], eng, True)

    def barrier_wait(self, eng, ev):
        self._wait(eng, ev)

    def replay(self, blk):
        prog = self.prog
        blk.sync(lambda e: [fn(e) for fn in prog["sp"]])
        blk.gpsimd(lambda e: [fn(e) for fn in prog["gp"]])
        blk.tensor(lambda e: [fn(e) for fn in prog["pe"]])
        blk.vector(lambda e: [fn(e) for fn in prog["dve"]])
        blk.scalar(lambda e: [fn(e) for fn in prog["act"]])


class Banks:
    def __init__(self, psum):
        self.psum = psum
        self.state = [{"writer": None, "readers": []} for _ in psum]

    def acquire(self, i):
        s = self.state[i]
        deps = list(s["readers"]) if s["readers"] else (
            [s["writer"]] if s["writer"] else [])
        s["readers"] = []
        s["writer"] = None
        return deps

    def wrote(self, i, ev):
        self.state[i]["writer"] = ev

    def read(self, i, ev):
        if ev is not None:
            self.state[i]["readers"].append(ev)


class Region:
    __slots__ = ("arena", "off", "n", "deps", "name")
    def __init__(self, arena, off, n, deps, name):
        self.arena, self.off, self.n, self.deps, self.name = arena, off, n, deps, name

    def f32(self, c0=0, c1=None):
        c1 = self.n if c1 is None else c1
        assert 0 <= c0 <= c1 <= self.n, (self.name, c0, c1, self.n)
        return self.arena[:, self.off + c0:self.off + c1]

    def bf16(self, c0=0, c1=None):
        c1 = 2 * self.n if c1 is None else c1
        assert 0 <= c0 <= c1 <= 2 * self.n, (self.name, c0, c1, self.n)
        v = self.arena[:, self.off:self.off + self.n].bitcast(BF16)
        return v[:, c0:c1]


class Arena:
    def __init__(self, tensor, n):
        self.t = tensor
        self.free_list = [[0, n, []]]
        self.live = {}
        self.hiwater = 0

    def alloc(self, name, n_f32, align=16, side="low"):
        assert name not in self.live, name
        segs = self.free_list if side == "low" else list(reversed(self.free_list))
        for seg in segs:
            if side == "low":
                s = (seg[0] + align - 1) // align * align
                ok = seg[1] - s >= n_f32
            else:
                s = (seg[1] - n_f32) // align * align
                ok = s >= seg[0]
            if ok:
                deps = list(seg[2])
                seg_old = list(seg)
                self.free_list.remove(seg)
                if s > seg_old[0]:
                    self.free_list.append([seg_old[0], s, list(seg_old[2])])
                if s + n_f32 < seg_old[1]:
                    self.free_list.append([s + n_f32, seg_old[1], list(seg_old[2])])
                self.free_list.sort(key=lambda x: x[0])
                for lr in self.live.values():
                    assert s + n_f32 <= lr.off or s >= lr.off + lr.n, \
                        f"OVERLAP {name}@{s}+{n_f32} vs {lr.name}@{lr.off}+{lr.n}"
                r = Region(self.t, s, n_f32, deps, name)
                self.live[name] = r
                if TRACE_ARENA:
                    print(f"ARENA alloc {name}@{s}+{n_f32} ({side})")
                self.hiwater = max(self.hiwater, s + n_f32)
                return r
        raise RuntimeError(
            f"arena OOM for {name} ({n_f32} f32); free="
            f"{[(a, b) for a, b, _ in self.free_list]} live="
            f"{[(r.name, r.off, r.n) for r in self.live.values()]}")

    def free(self, r, events):
        if TRACE_ARENA:
            print(f"ARENA free {r.name}@{r.off}+{r.n}")
        for a, b_, _ in self.free_list:
            assert r.off + r.n <= a or r.off >= b_, \
                f"DOUBLE-FREE {r.name}@{r.off}+{r.n} overlaps free seg ({a},{b_})"
        assert self.live.pop(r.name, None) is not None, r.name
        evs = events if isinstance(events, list) else [events]
        flat = []
        for e in evs:
            if isinstance(e, (list, tuple)):
                flat += [x for x in e if x is not None]
            elif e is not None:
                flat.append(e)
        self.free_list.append([r.off, r.off + r.n, flat])
        self.free_list.sort(key=lambda x: x[0])
        merged = []
        for seg in self.free_list:
            if merged and merged[-1][1] == seg[0]:
                merged[-1][1] = seg[1]
                merged[-1][2] = merged[-1][2] + seg[2]
            else:
                merged.append(seg)
        self.free_list = merged


def kernel(**inputs):
    consts = build_consts()
    x_full = np.ascontiguousarray(inputs["L0_attn_x"], dtype=np.float32)
    nc = bass.Bass("TRN2", target_bir_lowering=False, debug=False)
    build_program(nc)
    in_maps = []
    for c in range(NCORES):
        m = {"x_in": x_full[c * BPC:(c + 1) * BPC]}
        for k, v in inputs.items():
            if k == "L0_attn_x":
                continue
            m[k] = np.ascontiguousarray(v, dtype=np.float32)
        for k, v in consts.items():
            m["c_" + k] = v
        in_maps.append(m)
    res = run_bass_kernel_spmd(nc, in_maps, core_ids=list(range(NCORES)))
    out = np.concatenate([r["out"] for r in res.results], axis=0)
    return out.astype(np.float32)


def build_program(nc: bass.Bass):
    with contextlib.ExitStack() as st:
        _build(nc, st)


def _build(nc: bass.Bass, st):
    x_in = nc.dram_tensor("x_in", [BPC, T, D], F32, kind="ExternalInput").ap()
    out = nc.dram_tensor("out", [BPC, T, D], F32, kind="ExternalOutput").ap()
    cap = {}
    for k, v in build_consts().items():
        cap[k] = nc.dram_tensor("c_" + k, list(v.shape), F32,
                                kind="ExternalInput").ap()
    W = {}
    wspecs = {
        "L0_attn_Wq": [D, D], "L0_attn_bq": [D],
        "L0_attn_Wk": [D, D], "L0_attn_bk": [D],
        "L0_attn_Wv": [D, D], "L0_attn_bv": [D],
        "L0_ffn_w_gate": [I0, D], "L0_ffn_w_up": [I0, D],
        "L0_ffn_w_down": [D, I0],
        "L1_attn_W_qkv": [3 * D, D], "L1_attn_b_qkv": [3 * D],
        "L1_attn_W_out": [D, D], "L1_attn_b_out": [D],
        "L1_ffn_wqkv": [2, D, 1536], "L1_ffn_q_scale": [2, DH],
        "L1_ffn_k_scale": [2, DH], "L1_ffn_wo": [2, 512, D],
        "L1_ffn_ln_g": [2, D], "L1_ffn_ln_b": [2, D],
        "L1_ffn_w1": [2, D, 2 * HID], "L1_ffn_b1": [2, 2 * HID],
        "L1_ffn_w2": [2, HID, D], "L1_ffn_b2": [2, D],
    }
    for k, shp in wspecs.items():
        W[k] = nc.dram_tensor(k, shp, F32, kind="ExternalInput").ap()

    scr = {
        "res": nc.dram_tensor("scr_res", [EC, 128, T], F32, kind="Internal").ap(),
        "att": nc.dram_tensor("scr_att", [EC, 128, T], BF16, kind="Internal").ap(),
        "xn1": nc.dram_tensor("scr_xn1", [EC, 128, T], F32, kind="Internal").ap(),
        "xq": nc.dram_tensor("scr_xq", [EC, 128, T], BF16, kind="Internal").ap(),
        "xn0": nc.dram_tensor("scr_xn0", [EC, 128, T], F32, kind="Internal").ap(),
        "qt1": nc.dram_tensor("scr_qt1", [EC, 128, T], BF16, kind="Internal").ap(),
        "wgq": nc.dram_tensor("scr_wgq", [EC, 128, I0], BF16, kind="Internal").ap(),
        "wuq": nc.dram_tensor("scr_wuq", [EC, 128, I0], BF16, kind="Internal").ap(),
        "wdq": nc.dram_tensor("scr_wdq", [IC0, 128, D], BF16, kind="Internal").ap(),
    }
    if DBG.get("stop"):
        scr["dbg"] = nc.dram_tensor("dbg", [EC, 128, T], F32,
                                    kind="ExternalOutput").ap()

    def sb(name, shape, dt):
        return st.enter_context(nc.sbuf_tensor(name, shape, dt))

    sbuf = {
        "CONST": sb("CONST", [128, 400], F32),
        "SMALL": sb("SMALL", [128, 1664], F32),
        "CONB": sb("CONB", [128, 5760], BF16),
    }
    rem = nc.sbuf_bytes_remaining
    rem = rem() if callable(rem) else rem
    arena_n = (rem - 64) // 4 // 16 * 16
    sbuf["ARENA"] = sb("ARENA", [128, arena_n], F32)

    p01 = st.enter_context(nc.psum_tensor("pb01", [128, 1024], F32))
    p23 = st.enter_context(nc.psum_tensor("pb23", [128, 1024], F32))
    psum = [p01[:, 0:512], p01[:, 512:1024], p23[:, 0:512], p23[:, 512:1024]]
    psum += [st.enter_context(nc.psum_tensor(f"pb{i}", [128, 512], F32))
             for i in range(4, 7)]
    psum_t = st.enter_context(nc.psum_tensor("pbt", [128, 1024], BF16))

    p = Planner(nc, st)
    banks = Banks(psum + [psum_t])
    arena = Arena(sbuf["ARENA"][:, :], arena_n)

    env = dict(nc=nc, p=p, banks=banks, sbuf=sbuf, psum=psum, psum_t=psum_t,
               psum_pair=[p01, p23],
               arena=arena, x_in=x_in, out=out, cap=cap, W=W, scr=scr)
    _forward(env)

    with nc.Block() as blk:
        p.replay(blk)


def _bcast_row(env, bank_i, row_ap, reads, n=512):
    p, banks = env["p"], env["banks"]
    deps = banks.acquire(bank_i)
    ones_row = env["ones_row"]
    ev = p.comp("pe", lambda e: e.matmul(
        env["psum"][bank_i][:, 0:n], ones_row, row_ap, start=True, stop=True),
        reads=[reads, env["ev_const"]] + deps, sig=True)
    banks.wrote(bank_i, ev)
    return ev


def _wt_transpose(env, nat_getter, nchunks_out, nchunks_in, wt_region,
                  ev_nat, tag):
    p, banks, psum_t = env["p"], env["banks"], env["psum_t"]
    out_evs = []
    for c in range(nchunks_out):
        bk = 7
        deps = banks.acquire(bk)
        nat = nat_getter(c)
        ev_tp = p.comp("pe", lambda e, nat=nat: [
            e.transpose(psum_t[:, j * 128:(j + 1) * 128],
                        nat[:, j * 128:(j + 1) * 128], env["idenb"])
            for j in range(min(8, nchunks_in))],
            reads=[ev_nat, env["ev_constb"]] + deps, sig=True)
        banks.wrote(bk, ev_tp)
        now = nchunks_in if nchunks_in <= 8 else 8
        ncols_out = nchunks_out * 128
        ev_ev = p.comp("dve", lambda e, c=c, now=now: e.tensor_copy(
            wt_region.bf16(0, now * ncols_out).rearrange(
                "p (j m) -> p j m", m=ncols_out)[:, :, c * 128:(c + 1) * 128],
            psum_t[:, 0:now * 128].rearrange("p (j m) -> p j m", m=128)),
            reads=[ev_tp] + wt_region.deps, sig=True)
        banks.read(bk, ev_ev)
        out_evs.append(ev_ev)
    return out_evs


def _forward(env):
    p = env["p"]
    sbuf = env["sbuf"]
    cap = env["cap"]
    CONST = sbuf["CONST"]
    CONB = sbuf["CONB"]

    ev1 = p.dma("sp", lambda e: e.dma_start(out=CONST[:, 0:128], in_=cap["iden"]))
    ev2 = p.dma("sp", lambda e: e.dma_start(out=CONST[:, 128:256], in_=cap["rot_pi"]))
    ev3 = p.dma("sp", lambda e: e.dma_start(out=CONST[:, 256:257], in_=cap["ones_col"]))
    ev4 = p.dma("sp", lambda e: e.dma_start(out=CONST[0:1, 257:385], in_=cap["ones_row"]))
    ev5 = p.dma("gp", lambda e: e.dma_start(out=CONB[:, 4096:4224], in_=cap["iden"]))
    ev5b = p.dma("gp", lambda e: e.dma_start(out=CONB[:, 4224:4225], in_=cap["ones_col"]))
    ev5c = p.dma("gp", lambda e: e.dma_start(out=CONB[0:1, 4225:4353], in_=cap["ones_row"]))
    ev6 = p.dma("gp", lambda e: e.dma_start(
        out=CONB[:, 0:4096].rearrange("p (c n) -> p c n", n=512),
        in_=cap["loc_mask"].rearrange("(c p) n -> p c n", p=128)))
    ev_e16 = p.dma("gp", lambda e: e.dma_start(
        out=CONB[:, 4480:4608].rearrange("p (g m) -> p g m", m=16),
        in_=cap["e16"].rearrange("(g p) m -> p g m", p=128)))
    ev_rpi = p.dma("gp", lambda e: e.dma_start(
        out=CONB[:, 4608:4736], in_=cap["rot_pi"]))
    env["ev_const"] = [ev1, ev2, ev3, ev4]
    env["ev_constb"] = [ev5, ev5b, ev5c, ev6, ev_e16, ev_rpi]
    env["iden"] = CONST[:, 0:128]
    env["rot_pi"] = CONST[:, 128:256]
    env["ones_col"] = CONST[:, 256:257]
    env["ones_row"] = CONST[0:1, 257:385]
    cells = {1e-5: 1656, -SHIFT_L0: 1657, -SHIFT_MHA: 1658, -SHIFT_LOC: 1659}
    SMALL = env["sbuf"]["SMALL"]
    ev_cells = p.comp("dve", lambda e: [
        e.memset(SMALL[:, col:col + 1], val) for val, col in cells.items()],
        sig=True)
    env["cell"] = {v: SMALL[:, c:c + 1] for v, c in cells.items()}
    env["rowchain"] = []
    env["ev_cells"] = ev_cells
    env["idenb"] = CONB[:, 4096:4224]
    env["ones_col_b"] = CONB[:, 4224:4225]
    env["ones_row_b"] = CONB[0:1, 4225:4353]
    env["e16_b"] = CONB[:, 4480:4608]
    env["rot_pi_b"] = CONB[:, 4608:4736]
    env["loc_mask"] = CONB[:, 0:4096]

    env["ebr_f"] = CONB[:, 4736:5760]
    env["ev_gconst"] = [p.dma("gp", lambda e: e.dma_start(
        out=CONB[0:16, 4736:5760].rearrange("p (g m) -> p g m", m=128),
        in_=cap["ebr"].rearrange("(g p) m -> p g m", p=16)))]

    store_evs = []
    nb = 1 if DBG.get("stop") else BPC
    try:
        for b in range(nb):
            ev = _batch_forward(env, b)
            store_evs += ev
    except StopBuild as sb:
        store_evs = sb.evs
    for ev in store_evs:
        p.barrier_wait("gp", ev)


def _batch_forward(env, b):
    arena, p = env["arena"], env["p"]
    res_r, ev_res = _load_x(env, b)
    ev_res = _layer0(env, b, res_r, ev_res)
    if DBG.get("stop") == "l0":
        evs = _dump_dbg(env, res_r, ev_res)
        return evs
    ev_res = _layer1(env, b, res_r, ev_res)
    evs = _store_out(env, b, env["final_r"], ev_res)
    arena.free(env["final_r"], evs)
    return evs


def _dump_xint(env, xf_ap, evs):
    p = env["p"]
    out = []
    for ec in range(EC):
        out.append(p.dma("sp", lambda e, ec=ec: e.dma_start(
            out=env["scr"]["dbg"][ec], in_=xf_ap(ec, 0, T)), reads=evs))
    return out


def _dump_dbg(env, res_r, ev_res):
    p = env["p"]
    evs = []
    for ec in range(EC):
        evs.append(p.dma("sp", lambda e, ec=ec: e.dma_start(
            out=env["scr"]["dbg"][ec], in_=res_r.f32(ec * T, (ec + 1) * T)),
            reads=ev_res))
    return evs


def _load_x(env, b):
    p, banks, arena = env["p"], env["banks"], env["arena"]
    x_in, psum = env["x_in"], env["psum"]
    res_r = arena.alloc("RES", EC * T)
    stg_r = arena.alloc("io_stg", 4 * 1024, side="high")
    slot_reader = {s: stg_r.deps + res_r.deps for s in range(4)}
    evs = []
    for tc in range(NTC):
        slot = tc % 4
        stg = stg_r.f32(slot * 1024, (slot + 1) * 1024)
        ev_ld = p.dma("sp", lambda e, tc=tc, stg=stg: e.dma_start(
            out=stg, in_=x_in[b, tc * 128:(tc + 1) * 128, :]),
            reads=slot_reader.get(slot))
        pe_evs = []
        for half in range(2):
            bk = (tc % 2) * 2 + half
            deps = banks.acquire(bk)
            ev_mm = p.comp("pe", lambda e, stg=stg, bk=bk, half=half: [
                e.transpose(env["psum"][bk][:, j * 128:(j + 1) * 128],
                            stg[:, (half * 4 + j) * 128:(half * 4 + j + 1) * 128],
                            env["iden"])
                for j in range(4)], reads=[ev_ld, env["ev_const"]] + deps,
                sig=True)
            banks.wrote(bk, ev_mm)
            pe_evs.append(ev_mm)
            ev_ev = p.comp("dve", lambda e, bk=bk, half=half, tc=tc: [
                e.tensor_copy(
                    res_r.f32((half * 4 + j) * T + tc * 128,
                              (half * 4 + j) * T + (tc + 1) * 128),
                    env["psum"][bk][:, j * 128:(j + 1) * 128])
                for j in range(4)], reads=[ev_mm], sig=True)
            banks.read(bk, ev_ev)
            evs.append(ev_ev)
        slot_reader[slot] = pe_evs
    arena.free(stg_r, [e for v in slot_reader.values()
                       for e in (v if isinstance(v, list) else [v])])
    return res_r, evs


def _store_out(env, b, src_r, ev_src):
    p, banks, arena = env["p"], env["banks"], env["arena"]
    out, psum = env["out"], env["psum"]
    stg_r = arena.alloc("io_stg_out", 4 * 1024, side="high")
    store_evs = []
    slot_store = {s: stg_r.deps for s in range(4)}
    for tc in range(NTC):
        slot = tc % 4
        stg = stg_r.f32(slot * 1024, (slot + 1) * 1024)
        evs_ev = []
        for half in range(2):
            bk = (tc % 2) * 2 + half
            deps = banks.acquire(bk)
            ev_mm = p.comp("pe", lambda e, bk=bk, half=half, tc=tc: [
                e.transpose(env["psum"][bk][:, j * 128:(j + 1) * 128],
                            src_r.f32((half * 4 + j) * T + tc * 128,
                                      (half * 4 + j) * T + (tc + 1) * 128),
                            env["iden"])
                for j in range(4)],
                reads=[ev_src] + deps + [env["ev_const"]], sig=True)
            banks.wrote(bk, ev_mm)
            dep_st = slot_store.get(slot)
            ev_ev = p.comp("dve", lambda e, bk=bk, half=half, stg=stg: [
                e.tensor_copy(stg[:, (half * 4 + j) * 128:(half * 4 + j + 1) * 128],
                              env["psum"][bk][:, j * 128:(j + 1) * 128])
                for j in range(4)],
                reads=[ev_mm, dep_st], sig=True)
            banks.read(bk, ev_ev)
            evs_ev.append(ev_ev)
        ev_st = p.dma("sp", lambda e, stg=stg, tc=tc: e.dma_start(
            out=out[b, tc * 128:(tc + 1) * 128, :], in_=stg),
            reads=evs_ev)
        slot_store[slot] = ev_st
        store_evs.append(ev_st)
    arena.free(stg_r, list(store_evs))
    return store_evs


def _layernorm_fast(env, src3, outs3, tag):
    p, banks, arena = env["p"], env["banks"], env["arena"]
    psum = env["psum"]
    SMALL = env["sbuf"]["SMALL"]
    out_evs = []
    sc_r = arena.alloc(f"lnf_{tag}", 4096 + 512, side="high")
    xb3 = sc_r.bf16(0, EC * 512).rearrange("p (c m) -> p c m", m=512)
    x23 = sc_r.bf16(EC * 512, 2 * EC * 512).rearrange(
        "p (c m) -> p c m", m=512)
    t3h = sc_r.f32(0, 2048).rearrange("p (c m) -> p c m", m=512)
    var = sc_r.f32(4096, 4096 + 512)[0:1, 0:512]
    prev_mm = list(sc_r.deps)
    prev_t = []
    for tt in range(NTT):
        src, ev_src = src3(tt)
        deps3 = banks.acquire(3)
        deps4 = banks.acquire(4)
        ev_cp = p.comp("dve", lambda e, src=src: e.tensor_copy(xb3, src),
                       reads=[ev_src] + prev_mm, sig=True)
        ev_sq = p.comp("act", lambda e, src=src: e.activation(
            x23, src, AF.Square), reads=[ev_src] + prev_mm, sig=True)
        ev_mm = p.comp("pe", lambda e: [
            mm for ec in range(EC) for mm in (
                e.matmul(psum[3][0:1, 0:512], env["ones_col_b"],
                         xb3[:, ec, :], start=(ec == 0), stop=(ec == EC - 1)),
                e.matmul(psum[4][0:1, 0:512], env["ones_col_b"],
                         x23[:, ec, :], start=(ec == 0), stop=(ec == EC - 1)))],
            reads=[ev_cp, ev_sq, env["ev_constb"]] + deps3 + deps4, sig=True)
        prev_mm = [ev_mm]
        banks.wrote(3, ev_mm)
        banks.wrote(4, ev_mm)
        mu = SMALL[0:1, 0:512]
        ev_mu = p.comp("dve", lambda e: e.tensor_scalar_mul(
            mu, psum[3][0:1, 0:512], 1.0 / D),
            reads=[ev_mm] + env["rowchain"][-4:], sig=True)
        mu2 = SMALL[0:1, 512:1024]
        ev_mu2 = p.comp("dve", lambda e: e.tensor_mul(mu2, mu, mu),
                        reads=[ev_mu], sig=True)
        ev_var = p.comp("dve", lambda e: e.scalar_tensor_tensor(
            var, psum[4][0:1, 0:512], 1.0 / D, mu2, OP.mult, OP.subtract),
            reads=[ev_mu2, ev_mm], sig=True)
        banks.read(4, ev_var)
        sd = SMALL[0:1, 512:1024]
        ev_sd = p.comp("act", lambda e: e.activation(
            sd, var[0:1, :], AF.Sqrt, bias=env["cell"][1e-5][0:1, :]),
            reads=[ev_var, ev_mu2, env["ev_cells"]], sig=True)
        rs = SMALL[0:1, 0:512]
        ev_rs = p.comp("dve", lambda e: e.reciprocal(rs, sd),
                       reads=[ev_sd, ev_var, ev_mu], sig=True)
        nm = SMALL[0:1, 512:1024]
        ev_nm = p.comp("dve", lambda e: e.scalar_tensor_tensor(
            nm, psum[3][0:1, 0:512], -1.0 / D, rs, OP.mult, OP.mult),
            reads=[ev_rs, ev_mm, ev_sd], sig=True)
        banks.read(3, ev_nm)
        ev_a = _bcast_row(env, 5, rs, [ev_rs, ev_nm])
        ev_b = _bcast_row(env, 6, nm, [ev_nm])
        env["rowchain"] = [ev_a, ev_b]
        a3 = psum[5][:, 0:512].unsqueeze(1).broadcast_to([128, 4, 512])
        b3 = psum[6][:, 0:512].unsqueeze(1).broadcast_to([128, 4, 512])
        kind, apfn = outs3[0]
        dst = apfn(tt)
        wevs = []
        for hh in range(2):
            ev_t = p.comp("dve", lambda e, src=src, hh=hh: e.tensor_tensor(
                t3h, src[:, hh * 4:(hh + 1) * 4, :], a3, OP.mult),
                reads=[ev_src, ev_a, ev_var, ev_mm] + prev_t, sig=True)
            banks.read(5, ev_t)
            ev_o = p.comp("dve", lambda e, dst=dst, hh=hh: e.tensor_tensor(
                dst[:, hh * 4:(hh + 1) * 4, :], t3h, b3, OP.add),
                reads=[ev_t, ev_b], sig=True)
            banks.read(6, ev_o)
            prev_t = [ev_o]
            wevs.append(ev_o)
            for kind2, apfn2 in outs3[1:]:
                dst2 = apfn2(tt)
                ev_o2 = p.comp("act", lambda e, dst2=dst2, dst=dst, hh=hh:
                               e.copy(dst2[:, hh * 4:(hh + 1) * 4, :],
                                      dst[:, hh * 4:(hh + 1) * 4, :]),
                               reads=[ev_o], sig=True)
                wevs.append(ev_o2)
        out_evs += wevs
    arena.free(sc_r, out_evs[-2 * len(outs3):] + prev_mm)
    return out_evs


def _layernorm(env, src_get, outs, tag, gamma_beta=None, gb_reads=None):
    p, banks, arena = env["p"], env["banks"], env["arena"]
    psum = env["psum"]
    SMALL = env["sbuf"]["SMALL"]
    out_evs = []
    sc_r = arena.alloc(f"ln_{tag}", EC * 512 + 1024, side="high")
    prev_mm = list(sc_r.deps)
    for tt in range(NTT):
        c0 = tt * 512
        deps3 = banks.acquire(3)
        deps4 = banks.acquire(4)
        cp_evs = []
        for ec in range(EC):
            src, ev_src = src_get(ec)
            xb_t = sc_r.bf16(ec * 1024, ec * 1024 + 512)
            x2_t = sc_r.bf16(ec * 1024 + 512, (ec + 1) * 1024)
            ev_cp = p.comp("dve", lambda e, src=src, c0=c0, xb_t=xb_t:
                           e.tensor_copy(xb_t, src[:, c0:c0 + 512]),
                           reads=[ev_src] + prev_mm, sig=True)
            ev_sq = p.comp("act", lambda e, src=src, c0=c0, x2_t=x2_t:
                           e.activation(x2_t, src[:, c0:c0 + 512], AF.Square),
                           reads=[ev_src] + prev_mm, sig=True)
            cp_evs += [ev_cp, ev_sq]
        ev_mm = None
        for ec in range(EC):
            xb_t = sc_r.bf16(ec * 1024, ec * 1024 + 512)
            x2_t = sc_r.bf16(ec * 1024 + 512, (ec + 1) * 1024)
            ev_mm = p.comp("pe", lambda e, ec=ec, xb_t=xb_t, x2_t=x2_t: [
                e.matmul(psum[3][0:1, 0:512], env["ones_col_b"], xb_t,
                         start=(ec == 0), stop=(ec == EC - 1)),
                e.matmul(psum[4][0:1, 0:512], env["ones_col_b"], x2_t,
                         start=(ec == 0), stop=(ec == EC - 1))],
                reads=cp_evs + [env["ev_constb"]] +
                      (deps3 + deps4 if ec == 0 else []),
                sig=(ec == EC - 1))
        prev_mm = [ev_mm]
        banks.wrote(3, ev_mm)
        banks.wrote(4, ev_mm)
        mu = SMALL[0:1, 0:512]
        ev_mu = p.comp("dve", lambda e: e.tensor_scalar_mul(
            mu, psum[3][0:1, 0:512], 1.0 / D),
            reads=[ev_mm] + env["rowchain"][-4:], sig=True)
        banks.read(3, ev_mu)
        mu2 = SMALL[0:1, 512:1024]
        ev_mu2 = p.comp("dve", lambda e: e.tensor_mul(mu2, mu, mu),
                        reads=[ev_mu], sig=True)
        var = sc_r.f32(EC * 512 + 512, EC * 512 + 1024)[0:1, 0:512]
        ev_var = p.comp("dve", lambda e: e.scalar_tensor_tensor(
            var, psum[4][0:1, 0:512], 1.0 / D, mu2, OP.mult, OP.subtract),
            reads=[ev_mu2, ev_mm], sig=True)
        banks.read(4, ev_var)
        sd = SMALL[0:1, 512:1024]
        ev_sd = p.comp("act", lambda e: e.activation(
            sd, var[0:1, :], AF.Sqrt, bias=env["cell"][1e-5][0:1, :]),
            reads=[ev_var, ev_mu2, env["ev_cells"]], sig=True)
        rs = SMALL[0:1, 0:512]
        ev_rs = p.comp("dve", lambda e: e.reciprocal(rs, sd),
                       reads=[ev_sd, ev_var, ev_mu], sig=True)
        nm = SMALL[0:1, 512:1024]
        ev_nm = p.comp("dve", lambda e: e.scalar_tensor_tensor(
            nm, psum[3][0:1, 0:512], -1.0 / D, rs, OP.mult, OP.mult),
            reads=[ev_rs, ev_mm, ev_sd], sig=True)
        banks.read(3, ev_nm)
        ev_a = _bcast_row(env, 5, rs, [ev_rs, ev_nm])
        ev_b = _bcast_row(env, 6, nm, [ev_nm])
        env["rowchain"] = [ev_a, ev_b]
        t_t = sc_r.f32(EC * 512, EC * 512 + 512)
        u_t = sc_r.f32(EC * 512 + 512, EC * 512 + 1024)
        last_t_reader = None
        last_u_reader = None
        for ec in range(EC):
            src, ev_src = src_get(ec)
            ev_t = p.comp("dve", lambda e, src=src, c0=c0: e.tensor_tensor(
                t_t, src[:, c0:c0 + 512], psum[5][:, 0:512], OP.mult),
                reads=[ev_src, ev_a, ev_var] +
                      ([last_t_reader] if last_t_reader else []), sig=True)
            banks.read(5, ev_t)
            kind, apfn = outs[0]
            dst = apfn(ec, tt)
            if gamma_beta is None:
                ev_o = p.comp("dve", lambda e, dst=dst: e.tensor_tensor(
                    dst, t_t, psum[6][:, 0:512], OP.add),
                    reads=[ev_t, ev_b], sig=True)
                banks.read(6, ev_o)
                last_t_reader = ev_o
            else:
                g_ap, b_ap = gamma_beta(ec)
                ev_u = p.comp("dve", lambda e: e.tensor_tensor(
                    u_t, t_t, psum[6][:, 0:512], OP.add),
                    reads=[ev_t, ev_b] +
                          ([last_u_reader] if last_u_reader else []), sig=True)
                banks.read(6, ev_u)
                ev_o = p.comp("act", lambda e, dst=dst, g_ap=g_ap,
                              b_ap=b_ap: e.activation(
                    dst, u_t, AF.Identity, bias=b_ap, scale=g_ap),
                    reads=[ev_u] + (gb_reads or []), sig=True)
                last_t_reader = ev_u
                last_u_reader = ev_o
            wevs = [ev_o]
            for kind2, apfn2 in outs[1:]:
                dst2 = apfn2(ec, tt)
                ev_o2 = p.comp("act", lambda e, dst2=dst2, dst=dst: e.copy(
                    dst2, dst), reads=[ev_o], sig=True)
                wevs.append(ev_o2)
            out_evs += wevs
    arena.free(sc_r, out_evs[-2 * len(outs) * EC:] + prev_mm)
    return out_evs


def _ladder(env, eng, q_ap, axn_ap, reads):
    p = env["p"]
    tmp = env["_lad_tmp"]
    ev = p.comp(eng, lambda e: e.tensor_single_scalar(
        q_ap, axn_ap, 0.25, OP.is_gt), reads=reads, sig=True)
    if eng == "gp":
        for mid, inc in ((0.75, 1.0), (1.25, 1.0), (1.75, 1.0),
                         (2.5, 2.0), (3.5, 2.0), (5.0, 4.0)):
            ev = p.comp(eng, lambda e, mid=mid, inc=inc: e.tensor_scalar(
                tmp, axn_ap, mid, inc, OP.is_gt, OP.mult),
                reads=[ev], sig=True)
            ev = p.comp(eng, lambda e: e.tensor_tensor(
                q_ap, q_ap, tmp, OP.add), reads=[ev], sig=True)
        return ev
    for mid in (0.75, 1.25, 1.75):
        ev = p.comp(eng, lambda e, mid=mid: e.scalar_tensor_tensor(
            q_ap, axn_ap, mid, q_ap, OP.is_gt, OP.add), reads=[ev], sig=True)
    evb = p.comp(eng, lambda e: e.tensor_single_scalar(
        tmp, axn_ap, 2.5, OP.is_gt), reads=[ev], sig=True)
    evb = p.comp(eng, lambda e: e.scalar_tensor_tensor(
        tmp, axn_ap, 3.5, tmp, OP.is_gt, OP.add), reads=[evb], sig=True)
    ev = p.comp(eng, lambda e: e.scalar_tensor_tensor(
        q_ap, tmp, 2.0, q_ap, OP.mult, OP.add), reads=[evb], sig=True)
    evc = p.comp(eng, lambda e: e.tensor_scalar(
        tmp, axn_ap, 5.0, 4.0, OP.is_gt, OP.mult), reads=[ev], sig=True)
    ev = p.comp(eng, lambda e: e.tensor_tensor(
        q_ap, q_ap, tmp, OP.add), reads=[evc], sig=True)
    return ev


def _quant_block_free(env, src_ap, n_cols, out_bf16_ap, scratch_r, reads,
                      eng="dve", src_is_f32=True):
    p = env["p"]
    nb = n_cols // 16
    nbq = (nb + 15) // 16 * 16
    sc = scratch_r.f32(0, nb)
    rcp = scratch_r.f32(nbq, nbq + nb)
    o = 2 * nbq
    if src_is_f32:
        q_ap = scratch_r.f32(o, o + n_cols)
        axn = scratch_r.f32(o + n_cols, o + 2 * n_cols)
        tmp = scratch_r.f32(o + 2 * n_cols, o + 3 * n_cols)
        sg = scratch_r.f32(o + 3 * n_cols, o + 4 * n_cols)
    else:
        q_ap = scratch_r.bf16(2 * o, 2 * o + n_cols)
        axn = scratch_r.bf16(2 * o + n_cols, 2 * o + 2 * n_cols)
        tmp = scratch_r.bf16(2 * o + 2 * n_cols, 2 * o + 3 * n_cols)
        sg = scratch_r.bf16(2 * o + 3 * n_cols, 2 * o + 4 * n_cols)
    src3 = src_ap.rearrange("p (b k) -> p b k", k=16)
    ev1 = p.comp("dve", lambda e: e.tensor_reduce(
        sc, src3, AX.X, OP.max, apply_absolute_value=True),
        reads=reads + scratch_r.deps, sig=True)
    ev2 = p.comp(eng, lambda e: e.tensor_scalar(
        sc, sc, 1.0 / 6.0, 1e-12, OP.mult, OP.max), reads=[ev1], sig=True)
    ev3 = p.comp("dve", lambda e: e.reciprocal(rcp, sc), reads=[ev2], sig=True)
    rb = rcp.unsqueeze(2).broadcast_to([128, nb, 16])
    ev_ax = p.comp("act", lambda e: e.activation(tmp, src_ap, AF.Abs),
                   reads=reads + scratch_r.deps, sig=True)
    ev4 = p.comp(eng, lambda e: e.tensor_tensor(
        axn.rearrange("p (b k) -> p b k", k=16),
        tmp.rearrange("p (b k) -> p b k", k=16), rb, OP.mult),
        reads=[ev3, ev_ax] + reads, sig=True)
    ev_sg = p.comp("act", lambda e: e.activation(sg, src_ap, AF.Sign),
                   reads=[ev_ax], sig=True)
    env["_lad_tmp"] = tmp
    ev5 = _ladder(env, eng, q_ap, axn, [ev4])
    if eng == "gp":
        ev6 = p.comp(eng, lambda e: e.tensor_single_scalar(
            q_ap, q_ap, 0.5, OP.mult), reads=[ev5], sig=True)
        ev6 = p.comp(eng, lambda e: e.tensor_tensor(
            q_ap, q_ap, sg, OP.mult), reads=[ev6, ev_sg], sig=True)
    else:
        ev6 = p.comp(eng, lambda e: e.scalar_tensor_tensor(
            q_ap, q_ap, 0.5, sg, OP.mult, OP.mult), reads=[ev5, ev_sg],
            sig=True)
    sb = sc.unsqueeze(2).broadcast_to([128, nb, 16])
    ev7 = p.comp(eng, lambda e: e.tensor_tensor(
        out_bf16_ap.rearrange("p (b k) -> p b k", k=16),
        q_ap.rearrange("p (b k) -> p b k", k=16), sb, OP.mult),
        reads=[ev6], sig=True)
    return ev7


def _prep_ffn0_weights(env):
    p, banks, arena = env["p"], env["banks"], env["arena"]
    W, scr, psum_t = env["W"], env["scr"], env["psum_t"]
    jobs = [
        ("L0_ffn_w_gate", I0 // 128, D, scr["wgq"]),
        ("L0_ffn_w_up", I0 // 128, D, scr["wuq"]),
        ("L0_ffn_w_down", D // 128, I0, scr["wdq"]),
    ]
    nat_r = arena.alloc("wp_nat", 2 * I0, side="high")
    qs_r = arena.alloc("wp_scratch", 4 * I0 + 768, side="high")
    wq_r = arena.alloc("wp_q", I0, side="high")
    strip_r = arena.alloc("wp_strip", I0, side="high")
    slot_ev = {0: nat_r.deps, 1: nat_r.deps}
    q_ev = {0: wq_r.deps + qs_r.deps, 1: wq_r.deps + qs_r.deps}
    strip_ev = {0: strip_r.deps, 1: strip_r.deps}
    evs = []
    for wname, nout, nin, dst in jobs:
        njc = nin // 128
        for c in range(nout):
            sl = c % 2
            nat = nat_r.f32(sl * I0, sl * I0 + nin)
            ev_ld = p.dma("sp", lambda e, wname=wname, c=c, nat=nat:
                          e.dma_start(out=nat, in_=W[wname][c * 128:(c + 1) * 128, :]),
                          reads=slot_ev[sl], stream="w")
            wq = wq_r.bf16(sl * I0, sl * I0 + nin)
            ev_q = _quant_block_free(env, nat, nin, wq, qs_r,
                                     [ev_ld] + q_ev[sl],
)
            slot_ev[sl] = [ev_q]
            q_ev[sl] = []
            strip = strip_r.bf16(sl * I0, sl * I0 + nin)
            tevs = []
            for jb in range(0, njc, 8):
                jn = min(8, njc - jb)
                bk = 7
                deps = banks.acquire(bk)
                ev_tp = p.comp("pe", lambda e, wq=wq, jb=jb, jn=jn: [
                    e.transpose(psum_t[:, j * 128:(j + 1) * 128],
                                wq[:, (jb + j) * 128:(jb + j + 1) * 128],
                                env["idenb"])
                    for j in range(jn)],
                    reads=[ev_q, env["ev_constb"]] + deps + strip_ev[sl],
                    sig=True)
                banks.wrote(bk, ev_tp)
                ev_ev = p.comp("dve", lambda e, strip=strip, jb=jb, jn=jn:
                               e.tensor_copy(
                    strip[:, jb * 128:(jb + jn) * 128],
                    psum_t[:, 0:jn * 128]), reads=[ev_tp], sig=True)
                banks.read(bk, ev_ev)
                tevs.append(ev_ev)
            ev_st = p.dma("sp", lambda e, strip=strip, dst=dst, c=c, njc=njc:
                          e.dma_start(
                out=dst[:, :, c * 128:(c + 1) * 128].rearrange(
                    "j p m -> p j m"),
                in_=strip.rearrange("p (j m) -> p j m", m=128)),
                reads=tevs, stream="w")
            strip_ev[sl] = [ev_st]
            q_ev[sl] = q_ev[sl] + tevs
            evs.append(ev_st)
    arena.free(nat_r, [v for vv in slot_ev.values() for v in vv])
    arena.free(qs_r, evs[-2:])
    arena.free(wq_r, evs[-2:])
    arena.free(strip_r, evs)
    env["ev_wprep"] = evs


def _cast_load_weight_T(env, wname, nout, nin, tag, reads=None):
    p, banks, arena, psum_t = env["p"], env["banks"], env["arena"], env["psum_t"]
    W = env["W"]
    njc = nin // 128
    ncols = nout * 128
    nh = min(nout, 4)
    nat_r = arena.alloc(f"wnat_{tag}", (nh * nin) // 2, side="high")
    wt_r = arena.alloc(f"wt_{tag}", (njc * ncols) // 2, side="high")
    evs = []
    ev_ld_prev = list(nat_r.deps)
    for cb in range(0, nout, nh):
        cn = min(nh, nout - cb)
        ev_ld = p.dma("gp", lambda e, cb=cb, cn=cn: e.dma_start(
            out=nat_r.bf16(0, cn * nin).rearrange("p (c m) -> p c m", m=nin),
            in_=W[wname][cb * 128:(cb + cn) * 128, :].rearrange(
                "(c p) m -> p c m", p=128)),
            reads=ev_ld_prev + (reads or []), stream="w")
        ev_ld_prev = []
        for ci in range(cn):
            c = cb + ci
            bk = 7
            deps = banks.acquire(bk)
            nat = nat_r.bf16(ci * nin, (ci + 1) * nin)
            for jb in range(0, njc, 8):
                jn = min(8, njc - jb)
                ev_tp = p.comp("pe", lambda e, nat=nat, jb=jb, jn=jn: [
                    e.transpose(psum_t[:, j * 128:(j + 1) * 128],
                                nat[:, (jb + j) * 128:(jb + j + 1) * 128],
                                env["idenb"])
                    for j in range(jn)],
                    reads=[ev_ld, env["ev_constb"]] + deps + wt_r.deps,
                    sig=True)
                banks.wrote(bk, ev_tp)
                ev_ld_prev.append(ev_tp)
                ev_ev = p.comp("dve", lambda e, c=c, jb=jb, jn=jn:
                               e.tensor_copy(
                    wt_r.bf16(0, njc * ncols).rearrange(
                        "p (j m) -> p j m", m=ncols)[:, jb:jb + jn,
                                                     c * 128:(c + 1) * 128],
                    psum_t[:, 0:jn * 128].rearrange("p (j m) -> p j m",
                                                    m=128)),
                    reads=[ev_tp], sig=True)
                banks.read(bk, ev_ev)
                evs.append(ev_ev)
    arena.free(nat_r, evs[-1:] + ev_ld_prev)
    return wt_r, evs


def _layer0(env, b, res_r, ev_res):
    p, banks, arena = env["p"], env["banks"], env["arena"]
    psum, scr, W = env["psum"], env["scr"], env["W"]
    SMALL = env["sbuf"]["SMALL"]

    ev_b1 = p.dma("sp", lambda e: e.dma_start(
        out=SMALL[:, 1024:1032], in_=W["L0_attn_bq"].rearrange("(c p) -> p c", p=128),
        allow_slow_non_contiguous=True), reads=ev_res)
    ev_b2 = p.dma("sp", lambda e: e.dma_start(
        out=SMALL[:, 1032:1040], in_=W["L0_attn_bk"].rearrange("(c p) -> p c", p=128),
        allow_slow_non_contiguous=True), reads=ev_res)
    rows_r = arena.alloc("l0_rows", 1024, side="high")
    bv_row = rows_r.f32(0, 1024)[0:1, :]
    ev_b3 = p.dma("sp", lambda e: e.dma_start(
        out=bv_row, in_=W["L0_attn_bv"].rearrange("(a d) -> a d", a=1)),
        reads=rows_r.deps + ev_res)
    env["_ev_bq"] = ev_b1
    env["_ev_bk"] = ev_b2
    env["_ev_bv"] = ev_b3

    xn_r = arena.alloc("xn0", EC * T)
    xnb_r = arena.alloc("xnb0", EC * T // 2)
    ln_evs = _layernorm_fast(
        env,
        lambda tt: (res_r.f32(0, EC * T).rearrange(
            "p (c m) -> p c m", m=T)[:, :, tt * 512:(tt + 1) * 512], ev_res),
        [("f32", lambda tt: xn_r.f32(0, EC * T).rearrange(
            "p (c m) -> p c m", m=T)[:, :, tt * 512:(tt + 1) * 512]),
         ("bf16", lambda tt: xnb_r.bf16(0, EC * T).rearrange(
             "p (c m) -> p c m", m=T)[:, :, tt * 512:(tt + 1) * 512])],
        f"l0b{b}")

    xnsp_evs = []
    for ec in range(EC):
        xnsp_evs.append(p.dma("sp", lambda e, ec=ec: e.dma_start(
            out=scr["xn0"][ec], in_=xn_r.f32(ec * T, (ec + 1) * T)),
            reads=ln_evs))
    arena.free(xn_r, xnsp_evs)

    qt_spill_evs = _project_l0(env, b, xnb_r, ln_evs)
    kt_r, vt_r, proj_evs = env["_l0_kt"], env["_l0_vt"], env["_l0_proj_evs"]
    arena.free(xnb_r, proj_evs)

    res_evs = _attn_l0(env, b, res_r, kt_r, vt_r, qt_spill_evs, ev_res)
    arena.free(kt_r, env["_l0_att_done"])
    arena.free(vt_r, env["_l0_att_done"])
    arena.free(rows_r, env["_l0_att_done"])

    if b == 0:
        _prep_ffn0_weights(env)
    HT = T // 2
    qs_r = arena.alloc("xq_scratch", 4 * HT + 768, side="high")
    xqc_r = arena.alloc("xq_chunk", T // 2, side="high")
    xnc_r = arena.alloc("xn_chunk", 2 * T, side="high")
    ev_spill_prev = xqc_r.deps
    xnc_prev = {0: xnc_r.deps, 1: xnc_r.deps}
    xq_evs = []
    ev7 = ev6 = None
    for ec in range(EC):
        csl = ec % 2
        xnc = xnc_r.f32(csl * T, (csl + 1) * T)
        ev_ld = p.dma("sp", lambda e, ec=ec, xnc=xnc: e.dma_start(
            out=xnc, in_=scr["xn0"][ec]), reads=xnc_prev[csl] + xnsp_evs)
        half_evs = []
        for hh in range(2):
            xt = qs_r.f32(0, HT)
            ev_t = p.comp("dve", lambda e, xnc=xnc, hh=hh: e.transpose(
                xt, xnc[:, hh * HT:(hh + 1) * HT]),
                reads=[ev_ld] + qs_r.deps +
                      ([ev7] if ev7 else []) + half_evs, sig=True)
            nc_ = HT // 32
            xt4 = xt.rearrange("p (c g k) -> p c g k", g=2, k=16)
            sc = qs_r.f32(HT, HT + 2 * nc_)
            rcp = qs_r.f32(HT + 128, HT + 128 + 2 * nc_)
            ev1 = p.comp("dve", lambda e: e.tensor_reduce(
                sc.rearrange("p (c g) -> p c g", g=2), xt4, AX.X, OP.max,
                apply_absolute_value=True), reads=[ev_t], sig=True)
            ev2 = p.comp("dve", lambda e: e.tensor_scalar(
                sc, sc, 1.0 / 6.0, 1e-12, OP.mult, OP.max), reads=[ev1],
                sig=True)
            ev3 = p.comp("dve", lambda e: e.reciprocal(rcp, sc), reads=[ev2],
                         sig=True)
            rb = rcp.rearrange("p (c g) -> p c g", g=2).unsqueeze(3
                ).broadcast_to([128, nc_, 2, 16])
            axn = qs_r.f32(HT + 256, 2 * HT + 256)
            ev_ax = p.comp("act", lambda e: e.activation(axn, xt, AF.Abs),
                           reads=[ev_t], sig=True)
            ev4 = p.comp("dve", lambda e: e.tensor_tensor(
                axn.rearrange("p (c g k) -> p c g k", g=2, k=16),
                axn.rearrange("p (c g k) -> p c g k", g=2, k=16),
                rb, OP.mult), reads=[ev3, ev_ax], sig=True)
            sg = qs_r.bf16(2 * (4 * HT + 256), 2 * (4 * HT + 256) + HT)
            ev_sg = p.comp("act", lambda e: e.activation(sg, xt, AF.Sign),
                           reads=[ev_t, ev_ax], sig=True)
            q_ap = qs_r.f32(2 * HT + 256, 3 * HT + 256)
            env["_lad_tmp"] = qs_r.f32(3 * HT + 256, 4 * HT + 256)
            ev5 = _ladder(env, "dve", q_ap, axn, [ev4])
            ev6 = p.comp("dve", lambda e: e.scalar_tensor_tensor(
                q_ap, q_ap, 0.5, sg, OP.mult, OP.mult),
                reads=[ev5, ev_sg], sig=True)
            sb = sc.rearrange("p (c g) -> p c g", g=2).unsqueeze(3
                ).broadcast_to([128, nc_, 2, 16])
            xqt = qs_r.bf16(0, HT)
            ev7 = p.comp("dve", lambda e: e.tensor_tensor(
                xqt.rearrange("p (c g k) -> p c g k", g=2, k=16),
                q_ap.rearrange("p (c g k) -> p c g k", g=2, k=16), sb,
                OP.mult), reads=[ev6, ev_t, ev_sg], sig=True)
            xqc = xqc_r.bf16(hh * HT, (hh + 1) * HT)
            ev8 = p.comp("dve", lambda e, xqc=xqc: e.transpose(xqc, xqt),
                         reads=[ev7] + ev_spill_prev, sig=True)
            half_evs.append(ev8)
        xnc_prev[csl] = half_evs
        ev_sp = p.dma("sp", lambda e, ec=ec: e.dma_start(
            out=scr["xq"][ec], in_=xqc_r.bf16(0, T)), reads=half_evs)
        ev_spill_prev = [ev_sp]
        xq_evs.append(ev_sp)
    arena.free(qs_r, [xq_evs[-1], ev7, ev6])
    arena.free(xqc_r, [xq_evs[-1]])
    arena.free(xnc_r, [xq_evs[-1], ev7])

    res_evs = _ffn0(env, b, res_r, res_evs, xq_evs)
    return res_evs


def _project_l0(env, b, xnb_r, ln_evs):
    p, banks, arena = env["p"], env["banks"], env["arena"]
    psum, scr, W = env["psum"], env["scr"], env["W"]
    SMALL = env["sbuf"]["SMALL"]

    def proj_fm(wname, bias_small_col, out_sink, tag):
        wt_r, wt_evs = _cast_load_weight_T(env, wname, EC, D, tag)
        sink_evs = []
        for dc in range(EC):
            for tt in range(NTT):
                bk = (dc * NTT + tt) % 2
                deps = banks.acquire(bk)
                ev_mm = p.comp("pe", lambda e, dc=dc, tt=tt, bk=bk, wt_r=wt_r: [
                    e.matmul(psum[bk][:, 0:512],
                             wt_r.bf16(0, EC * D).rearrange(
                                 "p (j m) -> p j m", m=D)[:, j,
                                                          dc * 128:(dc + 1) * 128],
                             xnb_r.bf16(j * T + tt * 512, j * T + tt * 512 + 512),
                             start=(j == 0), stop=(j == EC - 1))
                    for j in range(EC)],
                    reads=wt_evs + ln_evs + deps, sig=True)
                banks.wrote(bk, ev_mm)
                ev_s = out_sink(dc, tt, bk, ev_mm, bias_small_col)
                banks.read(bk, ev_s)
                sink_evs.append(ev_s)
        arena.free(wt_r, sink_evs)
        return sink_evs

    qstg_r = arena.alloc("q_stage", T // 2, side="high")
    qsp_evs = [None] * EC
    prev_sp = {0: qstg_r.deps}

    def q_sink(dc, tt, bk, ev_mm, bcol):
        dst = qstg_r.bf16(tt * 512, (tt + 1) * 512)
        ev = p.comp("act", lambda e, dst=dst, bk=bk, dc=dc: e.activation(
            dst, psum[bk][:, 0:512], AF.Identity,
            bias=SMALL[:, bcol + dc:bcol + dc + 1]),
            reads=[ev_mm, env["_ev_bq"]] + (prev_sp.get(0) or []), sig=True)
        if tt == NTT - 1:
            ev_sp = p.dma("sp", lambda e, dc=dc: e.dma_start(
                out=scr["qt1"][dc], in_=qstg_r.bf16(0, T)), reads=[ev])
            prev_sp[0] = [ev_sp]
            qsp_evs[dc] = ev_sp
        return ev

    proj_fm("L0_attn_Wq", 1024, q_sink, f"q0b{b}")
    arena.free(qstg_r, [e for e in qsp_evs if e])

    kt_r = arena.alloc("l0_kt", EC * T // 2)

    def k_sink(dc, tt, bk, ev_mm, bcol):
        dst = kt_r.bf16(dc * T + tt * 512, dc * T + tt * 512 + 512)
        return p.comp("act", lambda e, dst=dst, bk=bk, dc=dc: e.activation(
            dst, psum[bk][:, 0:512], AF.Identity,
            bias=SMALL[:, bcol + dc:bcol + dc + 1]),
            reads=[ev_mm, env["_ev_bk"]] +
                  (kt_r.deps if (dc == 0 and tt == 0) else []),
            sig=True)

    k_evs = proj_fm("L0_attn_Wk", 1032, k_sink, f"k0b{b}")

    vt_r = arena.alloc("l0_vt", NTC * 1024 // 2)
    wt_r, wt_evs = _cast_load_weight_T(env, "L0_attn_Wv", EC, D, f"v0b{b}")
    rows_bv = env["arena"].live["l0_rows"]
    bv_row = rows_bv.f32(0, 1024)[0:1, :]
    v_evs = []
    for tc in range(NTC):
        for n in range(2):
            bk = (tc * 2 + n) % 2 + 2
            deps = banks.acquire(bk)
            ev_mm = p.comp("pe", lambda e, tc=tc, n=n, bk=bk, wt_r=wt_r: [
                e.matmul(psum[bk][:, 0:512],
                         xnb_r.bf16(j * T + tc * 128, j * T + (tc + 1) * 128),
                         wt_r.bf16(0, EC * D).rearrange(
                             "p (j m) -> p j m", m=D)[:, j,
                                                      n * 512:(n + 1) * 512],
                         start=(j == 0), stop=False)
                for j in range(EC)] + [
                e.matmul(psum[bk][:, 0:512], env["ones_row"],
                         bv_row[:, n * 512:(n + 1) * 512],
                         start=False, stop=True)],
                reads=wt_evs + ln_evs + deps + [env["ev_const"],
                                                env.get("_ev_bv")], sig=True)
            banks.wrote(bk, ev_mm)
            dst = vt_r.bf16(tc * 1024 + n * 512, tc * 1024 + (n + 1) * 512)
            ev_s = p.comp("act", lambda e, dst=dst, bk=bk: e.copy(
                dst, psum[bk][:, 0:512]),
                reads=[ev_mm] + (vt_r.deps if tc == 0 and n == 0 else []),
                sig=True)
            banks.read(bk, ev_s)
            v_evs.append(ev_s)
    arena.free(wt_r, v_evs)

    env["_l0_kt"] = kt_r
    env["_l0_vt"] = vt_r
    env["_l0_proj_evs"] = k_evs + v_evs + [e for e in qsp_evs if e]
    return [e for e in qsp_evs if e]


def _attn_l0(env, b, res_r, kt_r, vt_r, qt_spill_evs, ev_res):
    p, banks, arena = env["p"], env["banks"], env["arena"]
    psum, scr = env["psum"], env["scr"]
    SMALL = env["sbuf"]["SMALL"]
    pt_r = arena.alloc("l0_pt", NTC * 512 // 2)
    qt_r = arena.alloc("l0_qtile", 2 * EC * 512 // 2, side="high")
    zn_r = arena.alloc("l0_zn", 512, side="high")
    rbs_r = arena.alloc("l0_rbs", 512, side="high")
    res_state = {}
    att_done = []
    qslot_ev = {0: qt_r.deps, 1: qt_r.deps}
    for tt in range(NTT):
        sl = tt % 2
        qtile = qt_r.bf16(sl * EC * 512, (sl + 1) * EC * 512)
        ev_q = p.dma("sp", lambda e, tt=tt, qtile=qtile: e.dma_start(
            out=qtile.rearrange("p (j m) -> p j m", m=512),
            in_=scr["qt1"][:, :, tt * 512:(tt + 1) * 512].rearrange(
                "j p m -> p j m")),
            reads=qslot_ev[sl] + qt_spill_evs)
        deps_sum = banks.acquire(2)
        exp_evs = []
        for tkc in range(NTC):
            bk = tkc % 2
            deps = banks.acquire(bk)
            ev_mm = p.comp("pe", lambda e, tkc=tkc, bk=bk, qtile=qtile: [
                e.matmul(psum[bk][:, 0:512],
                         kt_r.bf16(j * T + tkc * 128, j * T + (tkc + 1) * 128),
                         qtile[:, j * 512:(j + 1) * 512],
                         start=(j == 0), stop=(j == EC - 1))
                for j in range(EC)],
                reads=[ev_q] + deps, sig=True)
            banks.wrote(bk, ev_mm)
            dst = pt_r.bf16(tkc * 512, (tkc + 1) * 512)
            ev_e = p.comp("act", lambda e, dst=dst, bk=bk: e.activation(
                dst, psum[bk][:, 0:512], AF.Exp, bias=env["cell"][-SHIFT_L0],
                scale=1.0 / 32.0),
                reads=[ev_mm, env["ev_cells"]] +
                      (pt_r.deps if tt == 0 and tkc == 0 else []),
                sig=True)
            banks.read(bk, ev_e)
            exp_evs.append(ev_e)
        ev_sums = p.comp("pe", lambda e: [
            e.matmul(psum[2][0:1, 0:512], env["ones_col_b"],
                     pt_r.bf16(tkc * 512, (tkc + 1) * 512),
                     start=(tkc == 0), stop=(tkc == NTC - 1))
            for tkc in range(NTC)],
            reads=exp_evs + [env["ev_constb"]] + deps_sum, sig=True)
        banks.wrote(2, ev_sums)
        rsum = SMALL[0:1, 0:512]
        ev_r = p.comp("dve", lambda e: e.reciprocal(
            rsum, psum[2][0:1, 0:512]),
            reads=[ev_sums] + env["rowchain"][-4:], sig=True)
        banks.read(2, ev_r)
        pv_evs = []
        for dc in range(EC):
            bk = 3 + dc % 2
            deps = banks.acquire(bk)
            ev_mm = p.comp("pe", lambda e, dc=dc, bk=bk: [
                e.matmul(psum[bk][:, 0:512],
                         vt_r.bf16(tkc * 1024 + dc * 128,
                                   tkc * 1024 + (dc + 1) * 128),
                         pt_r.bf16(tkc * 512, (tkc + 1) * 512),
                         start=(tkc == 0), stop=(tkc == NTC - 1))
                for tkc in range(NTC)],
                reads=exp_evs + deps, sig=True)
            banks.wrote(bk, ev_mm)
            pv_evs.append(ev_mm)
            if dc == 0:
                ev_rb = _bcast_row(env, 5, rsum, [ev_r])
                env["rowchain"] = [ev_rb]
                ev_rbs = p.comp("dve", lambda e: e.tensor_copy(
                    rbs_r.f32(0, 512), psum[5][:, 0:512]),
                    reads=[ev_rb] + rbs_r.deps + zn_r.deps, sig=True)
                banks.read(5, ev_rbs)
            ev_zn = p.comp("dve", lambda e, bk=bk: e.tensor_tensor(
                zn_r.f32(0, 512), psum[bk][:, 0:512], rbs_r.f32(0, 512),
                OP.mult), reads=[ev_mm, ev_rbs] + zn_r.deps, sig=True)
            banks.read(bk, ev_zn)
            key = (dc, tt)
            prior = res_state.get(key, ev_res)
            ev_add = p.comp("dve", lambda e, dc=dc, tt=tt: e.tensor_tensor(
                res_r.f32(dc * T + tt * 512, dc * T + tt * 512 + 512),
                res_r.f32(dc * T + tt * 512, dc * T + tt * 512 + 512),
                zn_r.f32(0, 512), OP.add), reads=[ev_zn, prior], sig=True)
            res_state[key] = ev_add
            att_done.append(ev_add)
        qslot_ev[sl] = [att_done[-1]] + exp_evs[-1:]
    env["_l0_att_done"] = att_done
    env["_l0_res_state"] = res_state
    arena.free(pt_r, att_done[-EC:])
    arena.free(qt_r, att_done[-EC:])
    arena.free(zn_r, att_done[-1:])
    arena.free(rbs_r, att_done[-1:])
    return att_done


FFN0_ITILES = [(0, 512), (512, 512), (1024, 512), (1536, 512),
               (2048, 512), (2560, 256)]


def _ffn0(env, b, res_r, res_evs, xq_evs):
    p, banks, arena = env["p"], env["banks"], env["arena"]
    psum, psum_t, scr = env["psum"], env["psum_t"], env["scr"]
    res_state = env["_l0_res_state"]

    xq_r = arena.alloc("ffn_xq", EC * T // 2)
    ev_xq = p.dma("sp", lambda e: e.dma_start(
        out=xq_r.bf16(0, EC * T).rearrange("p (j m) -> p j m", m=T),
        in_=scr["xq"].rearrange("j p m -> p j m")),
        reads=xq_r.deps + xq_evs, stream="w")

    wst_r = arena.alloc("ffn_wst", 4 * 2048, side="high")
    wdq_r = arena.alloc("ffn_wdq", 2 * IC0 * 128 // 2, side="high")
    h_r = arena.alloc("ffn_h", 4 * 256, side="high")
    hq_r = arena.alloc("ffn_hq", 4 * 256, side="high")
    hqt_r = arena.alloc("ffn_hqt", IC0 * 512 // 2)
    gs_r = arena.alloc("ffn_gs", 4 * 256, side="high")
    qs_r = arena.alloc("ffn_qs", 2048, side="high")
    wslot_ev = {0: wst_r.deps + env["ev_wprep"], 1: wst_r.deps}
    dslot_ev = {0: wdq_r.deps + env["ev_wprep"], 1: wdq_r.deps}
    h_last = {i: h_r.deps for i in range(4)}
    hq_last = {0: hq_r.deps + qs_r.deps, 1: hq_r.deps, 2: hq_r.deps,
               3: hq_r.deps}
    gs_last = {i: gs_r.deps for i in range(4)}
    hqt_last = hqt_r.deps
    ev_qprev = None
    out_evs = []
    for tt in range(NTT):
        hqt_evs = []
        for it, (i0, iw) in enumerate(FFN0_ITILES):
            sl = it % 2
            wg = wst_r.bf16(sl * 8192, sl * 8192 + 4096)[:, 0:EC * iw]
            wu = wst_r.bf16(sl * 8192 + 4096, (sl + 1) * 8192)[:, 0:EC * iw]
            ev_w = [
                p.dma("sp", lambda e, i0=i0, iw=iw, wg=wg: e.dma_start(
                    out=wg.rearrange("p (j m) -> p j m", m=iw),
                    in_=scr["wgq"][:, :, i0:i0 + iw].rearrange("j p m -> p j m")),
                    reads=wslot_ev[sl], stream="w"),
                p.dma("sp", lambda e, i0=i0, iw=iw, wu=wu: e.dma_start(
                    out=wu.rearrange("p (j m) -> p j m", m=iw),
                    in_=scr["wuq"][:, :, i0:i0 + iw].rearrange("j p m -> p j m")),
                    stream="w"),
            ]
            wsl_evs = []
            for tcw in range(4):
                tc = tt * 4 + tcw
                bkg = tcw % 2
                bku = 2 + tcw % 2
                hsl4 = tcw
                depsg = banks.acquire(bkg)
                depsu = banks.acquire(bku)
                ev_g = p.comp("pe", lambda e, tc=tc, iw=iw, wg=wg, bkg=bkg: [
                    e.matmul(psum[bkg][:, 0:iw],
                             xq_r.bf16(j * T + tc * 128, j * T + (tc + 1) * 128),
                             wg[:, j * iw:(j + 1) * iw],
                             start=(j == 0), stop=(j == EC - 1))
                    for j in range(EC)], reads=ev_w + [ev_xq] + depsg,
                    sig=True)
                banks.wrote(bkg, ev_g)
                ev_u = p.comp("pe", lambda e, tc=tc, iw=iw, wu=wu, bku=bku: [
                    e.matmul(psum[bku][:, 0:iw],
                             xq_r.bf16(j * T + tc * 128, j * T + (tc + 1) * 128),
                             wu[:, j * iw:(j + 1) * iw],
                             start=(j == 0), stop=(j == EC - 1))
                    for j in range(EC)], reads=ev_w + [ev_xq] + depsu,
                    sig=True)
                banks.wrote(bku, ev_u)
                wsl_evs += [ev_g, ev_u]
                hsl = tcw
                gs = gs_r.bf16(hsl * 512, hsl * 512 + iw)
                ev_si = p.comp("act", lambda e, gs=gs, bkg=bkg, iw=iw:
                               e.activation(gs, psum[bkg][:, 0:iw], AF.Silu),
                               reads=[ev_g] + gs_last[hsl], sig=True)
                banks.read(bkg, ev_si)
                h_t = h_r.bf16(hsl * 512, hsl * 512 + iw)
                ev_h = p.comp("dve", lambda e, h_t=h_t, gs=gs, bku=bku, iw=iw:
                              e.tensor_tensor(h_t, gs, psum[bku][:, 0:iw],
                                              OP.mult),
                              reads=[ev_si, ev_u] + h_last[hsl], sig=True)
                banks.read(bku, ev_h)
                gs_last[hsl] = [ev_h]
                hq_t = hq_r.bf16(hsl * 512, hsl * 512 + iw)
                ev_q = _quant_block_free(
                    env, h_t, iw, hq_t, qs_r,
                    [ev_h] + hq_last[hsl] + ([ev_qprev] if ev_qprev else []),
                    src_is_f32=False)
                ev_qprev = ev_q
                h_last[hsl] = [ev_q]
                bk7 = 7
                deps7 = banks.acquire(bk7)
                njj = iw // 128
                ev_tp = p.comp("pe", lambda e, hq_t=hq_t, njj=njj: [
                    e.transpose(psum_t[:, jj * 128:(jj + 1) * 128],
                                hq_t[:, jj * 128:(jj + 1) * 128],
                                env["idenb"])
                    for jj in range(njj)],
                    reads=[ev_q, env["ev_constb"]] + deps7, sig=True)
                banks.wrote(bk7, ev_tp)
                ev_he = p.comp("dve", lambda e, i0=i0, tcw=tcw, njj=njj:
                               e.tensor_copy(
                    hqt_r.bf16(0, IC0 * 512).rearrange(
                        "p (ic m) -> p ic m", m=512)[
                        :, i0 // 128:i0 // 128 + njj,
                        tcw * 128:(tcw + 1) * 128],
                    psum_t[:, 0:njj * 128].rearrange("p (jj m) -> p jj m",
                                                     m=128)),
                    reads=[ev_tp] + (hqt_last if it == 0 and tcw == 0 else []),
                    sig=True)
                banks.read(bk7, ev_he)
                hq_last[hsl] = [ev_he]
                hqt_evs.append(ev_he)
            wslot_ev[sl] = wsl_evs
        for dc in range(EC):
            dsl = dc % 2
            wd = wdq_r.bf16(dsl * IC0 * 128, (dsl + 1) * IC0 * 128)
            ev_wd = p.dma("sp", lambda e, dc=dc, wd=wd: e.dma_start(
                out=wd.rearrange("p (ic m) -> p ic m", m=128),
                in_=scr["wdq"][:, :, dc * 128:(dc + 1) * 128].rearrange(
                    "ic p m -> p ic m")),
                reads=dslot_ev[dsl], stream="w")
            bk = 4 + dc % 2
            deps = banks.acquire(bk)
            ev_mm = p.comp("pe", lambda e, wd=wd, bk=bk: [
                e.matmul(psum[bk][:, 0:512],
                         wd[:, ic * 128:(ic + 1) * 128],
                         hqt_r.bf16(ic * 512, (ic + 1) * 512),
                         start=(ic == 0), stop=(ic == IC0 - 1))
                for ic in range(IC0)],
                reads=[ev_wd] + hqt_evs + deps, sig=True)
            banks.wrote(bk, ev_mm)
            dslot_ev[dsl] = [ev_mm]
            key = (dc, tt)
            prior = res_state.get(key)
            ev_add = p.comp("dve", lambda e, dc=dc, tt=tt, bk=bk:
                            e.tensor_tensor(
                res_r.f32(dc * T + tt * 512, dc * T + tt * 512 + 512),
                res_r.f32(dc * T + tt * 512, dc * T + tt * 512 + 512),
                psum[bk][:, 0:512], OP.add),
                reads=[ev_mm, prior], sig=True)
            banks.read(bk, ev_add)
            res_state[key] = ev_add
            out_evs.append(ev_add)
        hqt_last = hqt_evs
    for r, evs in ((xq_r, out_evs[-2:]), (wst_r, out_evs[-2:]),
                   (wdq_r, out_evs[-2:]), (h_r, out_evs[-2:]),
                   (hq_r, out_evs[-2:]), (hqt_r, out_evs[-2:]),
                   (gs_r, out_evs[-2:]), (qs_r, out_evs[-2:])):
        arena.free(r, evs)
    env["_l0_res_state"] = res_state
    return out_evs


def _cast_load_rows_T(env, wname, chunks, nin, tag, reads=None):
    p, banks, arena, psum_t = env["p"], env["banks"], env["arena"], env["psum_t"]
    W = env["W"]
    njc = nin // 128
    nout = len(chunks)
    ncols = nout * 128
    nat_r = arena.alloc(f"wnat_{tag}", nin // 2, side="high")
    wt_r = arena.alloc(f"wt_{tag}", (njc * ncols) // 2, side="high")
    evs = []
    ev_prev = list(nat_r.deps)
    for c, plan in enumerate(chunks):
        ev_lds = []
        for (r0, p0, nr) in plan:
            ev_lds.append(p.dma("gp", lambda e, r0=r0, p0=p0, nr=nr:
                                e.dma_start(
                out=nat_r.bf16(0, nin)[p0:p0 + nr, :],
                in_=W[wname][r0:r0 + nr, :]),
                reads=ev_prev + (reads or []), stream="w"))
            ev_prev = []
        tevs = []
        for jb in range(0, njc, 8):
            jn = min(8, njc - jb)
            bk = 7
            deps = banks.acquire(bk)
            ev_tp = p.comp("pe", lambda e, jb=jb, jn=jn: [
                e.transpose(env["psum_t"][:, j * 128:(j + 1) * 128],
                            nat_r.bf16(0, nin)[:, (jb + j) * 128:
                                               (jb + j + 1) * 128],
                            env["idenb"])
                for j in range(jn)],
                reads=ev_lds + [env["ev_constb"]] + deps + wt_r.deps,
                sig=True)
            banks.wrote(bk, ev_tp)
            tevs.append(ev_tp)
            ev_ev = p.comp("dve", lambda e, c=c, jb=jb, jn=jn: e.tensor_copy(
                wt_r.bf16(0, njc * ncols).rearrange(
                    "p (j m) -> p j m", m=ncols)[:, jb:jb + jn,
                                                 c * 128:(c + 1) * 128],
                env["psum_t"][:, 0:jn * 128].rearrange("p (j m) -> p j m",
                                                       m=128)),
                reads=[ev_tp], sig=True)
            banks.read(bk, ev_ev)
            evs.append(ev_ev)
        ev_prev = tevs
    arena.free(nat_r, evs[-1:] + ev_prev)
    return wt_r, evs


TZ = T + 1


def _layer1(env, b, res_r, ev_res):
    p, banks, arena = env["p"], env["banks"], env["arena"]
    psum, scr, W = env["psum"], env["scr"], env["W"]
    SMALL = env["sbuf"]["SMALL"]

    xint_r = arena.alloc("xint", EC * TZ + 8)
    xintb_r = arena.alloc("xintb", (EC * TZ) // 2 + 8)
    ev_z0 = p.comp("dve", lambda e, xr=xint_r, xbr=xintb_r: [
        e.memset(xr.f32(0, EC * TZ).rearrange("p (c z) -> p c z", z=TZ)
                 [:, :, 0:1], 0.0),
        e.memset(xbr.bf16(0, EC * TZ).rearrange("p (c z) -> p c z", z=TZ)
                 [:, :, 0:1], 0.0)],
        reads=xint_r.deps + xintb_r.deps, sig=True)
    ln_evs = _layernorm_fast(
        env,
        lambda tt: (res_r.f32(0, EC * T).rearrange(
            "p (c m) -> p c m", m=T)[:, :, tt * 512:(tt + 1) * 512], ev_res),
        [("f32", lambda tt, xr=xint_r: xr.f32(0, EC * TZ).rearrange(
            "p (c m) -> p c m", m=TZ)[:, :, 1 + tt * 512:1 + tt * 512 + 512]),
         ("bf16", lambda tt, xbr=xintb_r: xbr.bf16(0, EC * TZ).rearrange(
             "p (c m) -> p c m", m=TZ)[:, :, 1 + tt * 512:1 + tt * 512 + 512])],
        f"l1b{b}")
    ln_evs = ln_evs + [ev_z0]

    sp_evs = []
    for ec in range(EC):
        sp_evs.append(p.dma("sp", lambda e, ec=ec: e.dma_start(
            out=scr["res"][ec], in_=res_r.f32(ec * T, (ec + 1) * T)),
            reads=ev_res + ln_evs[-1:]))
    arena.free(res_r, sp_evs)
    spx_evs = []
    for ec in range(EC):
        spx_evs.append(p.dma("sp", lambda e, ec=ec, xr=xint_r: e.dma_start(
            out=scr["xn1"][ec], in_=xr.f32(ec * TZ + 1, ec * TZ + 1 + T)),
            reads=ln_evs))
    arena.free(xint_r, spx_evs)

    def xb_ap(ec, c0, c1, shift=False):
        o = ec * TZ + (0 if shift else 1)
        return xintb_r.bf16(o + c0, o + c1)

    att_evs = _mha(env, b, xb_ap, ln_evs, xintb_r)
    xint2_r, ffn_evs = _l1ffn(env, b, ln_evs + spx_evs, att_evs)

    fin_r = arena.alloc("final", EC * T)
    stg_r = arena.alloc("fin_stg", 2 * 2048, side="high")
    stgb_r = arena.alloc("fin_stgb", 2 * 1024, side="high")
    slot_ev = {0: stg_r.deps + stgb_r.deps + sp_evs + att_evs,
               1: stg_r.deps + stgb_r.deps + sp_evs + att_evs}
    fin_evs = []
    for ec in range(EC):
        sl = ec % 2
        xs = stg_r.f32(sl * 2048, sl * 2048 + 2048)
        ab = stgb_r.bf16(sl * 2048, sl * 2048 + 2048)
        ev_l1 = p.dma("sp", lambda e, ec=ec, xs=xs: e.dma_start(
            out=xs, in_=scr["res"][ec]), reads=slot_ev[sl])
        ev_l2 = p.dma("gp", lambda e, ec=ec, ab=ab: e.dma_start(
            out=ab, in_=scr["att"][ec]), reads=slot_ev[sl])
        ev_a1 = p.comp("dve", lambda e, ec=ec, xs=xs, ab=ab: e.tensor_tensor(
            fin_r.f32(ec * T, (ec + 1) * T), xs, ab, OP.add),
            reads=[ev_l1, ev_l2] + fin_r.deps, sig=True)
        ev_a2 = p.comp("dve", lambda e, ec=ec: e.tensor_tensor(
            fin_r.f32(ec * T, (ec + 1) * T),
            fin_r.f32(ec * T, (ec + 1) * T),
            xint2_r.f32(ec * TZ + 1, ec * TZ + 1 + T), OP.add),
            reads=[ev_a1] + ffn_evs, sig=True)
        slot_ev[sl] = [ev_a1, ev_a2]
        fin_evs.append(ev_a2)
    arena.free(xint2_r, fin_evs)
    arena.free(stg_r, fin_evs)
    arena.free(stgb_r, fin_evs)
    env["final_r"] = fin_r
    return fin_evs


def _mha(env, b, xb_ap, ln_evs, xintb_r):
    p, banks, arena = env["p"], env["banks"], env["arena"]
    psum, scr, W = env["psum"], env["scr"], env["W"]
    SMALL = env["sbuf"]["SMALL"]

    bev = []
    for g in range(8):
        for part, r0 in ((0, 384 * g), (64, 384 * g + 192)):
            bev.append(p.dma("sp", lambda e, part=part, r0=r0, g=g:
                             e.dma_start(
                out=SMALL[part:part + 64, 1040 + g:1041 + g],
                in_=W["L1_attn_b_qkv"][r0:r0 + 64].rearrange(
                    "(a d) -> d a", a=1)), reads=ln_evs))
            bev.append(p.dma("sp", lambda e, part=part, r0=r0 + 64, g=g:
                             e.dma_start(
                out=SMALL[part:part + 64, 1048 + g:1049 + g],
                in_=W["L1_attn_b_qkv"][r0:r0 + 64].rearrange(
                    "(a d) -> d a", a=1)), reads=ln_evs))
    rbs_r = arena.alloc("mha_rbs", 1024, side="high")
    bv_row = rbs_r.f32(0, 1024)[0:1, :]
    ev_bv = p.dma("sp", lambda e: e.dma_start(
        out=bv_row.rearrange("a (h d) -> a h d", d=64),
        in_=W["L1_attn_b_qkv"].rearrange("(h x d) -> x h d", x=3, d=64)
            [2:3, :, :]),
        reads=rbs_r.deps)
    bev.append(ev_bv)
    ev_bo = p.dma("sp", lambda e: e.dma_start(
        out=SMALL[:, 1056:1064],
        in_=W["L1_attn_b_out"].rearrange("(c p) -> p c", p=128),
        allow_slow_non_contiguous=True), reads=ln_evs)

    def qk_chunks(off):
        return [[(384 * g + off, 0, 64), (384 * g + 192 + off, 64, 64)]
                for g in range(8)]

    kt_r = arena.alloc("mha_kt", EC * T // 2)
    wt_r, wt_evs = _cast_load_rows_T(env, "L1_attn_W_qkv", qk_chunks(64), D,
                                     f"mk{b}")
    k_evs = []
    for g in range(8):
        for tt in range(NTT):
            bk = (g * NTT + tt) % 2
            deps = banks.acquire(bk)
            ev_mm = p.comp("pe", lambda e, g=g, tt=tt, bk=bk, wt_r=wt_r: [
                e.matmul(psum[bk][:, 0:512],
                         wt_r.bf16(0, EC * D).rearrange(
                             "p (j m) -> p j m", m=D)[:, j,
                                                      g * 128:(g + 1) * 128],
                         xb_ap(j, tt * 512, (tt + 1) * 512),
                         start=(j == 0), stop=(j == EC - 1))
                for j in range(EC)],
                reads=wt_evs + ln_evs + deps, sig=True)
            banks.wrote(bk, ev_mm)
            dst = kt_r.bf16(g * T + tt * 512, g * T + tt * 512 + 512)
            ev_s = p.comp("act", lambda e, dst=dst, bk=bk, g=g: e.activation(
                dst, psum[bk][:, 0:512], AF.Identity,
                bias=SMALL[:, 1048 + g:1049 + g]),
                reads=[ev_mm] + bev +
                      (kt_r.deps if g == 0 and tt == 0 else []),
                sig=True)
            banks.read(bk, ev_s)
            k_evs.append(ev_s)
    arena.free(wt_r, k_evs[-1:])

    wt_r, wt_evs = _cast_load_rows_T(env, "L1_attn_W_qkv", qk_chunks(0), D,
                                     f"mq{b}")
    qstg_r = arena.alloc("mha_qstg", T // 2, side="high")
    q_sp = []
    prev_sp = list(qstg_r.deps)
    for g in range(8):
        g_evs = []
        for tt in range(NTT):
            bk = (g * NTT + tt) % 2
            deps = banks.acquire(bk)
            ev_mm = p.comp("pe", lambda e, g=g, tt=tt, bk=bk, wt_r=wt_r: [
                e.matmul(psum[bk][:, 0:512],
                         wt_r.bf16(0, EC * D).rearrange(
                             "p (j m) -> p j m", m=D)[:, j,
                                                      g * 128:(g + 1) * 128],
                         xb_ap(j, tt * 512, (tt + 1) * 512),
                         start=(j == 0), stop=(j == EC - 1))
                for j in range(EC)],
                reads=wt_evs + ln_evs + deps, sig=True)
            banks.wrote(bk, ev_mm)
            dst = qstg_r.bf16(tt * 512, (tt + 1) * 512)
            ev_s = p.comp("act", lambda e, dst=dst, bk=bk, g=g: e.activation(
                dst, psum[bk][:, 0:512], AF.Identity,
                bias=SMALL[:, 1040 + g:1041 + g]),
                reads=[ev_mm] + bev + prev_sp, sig=True)
            banks.read(bk, ev_s)
            g_evs.append(ev_s)
        ev_spill = p.dma("sp", lambda e, g=g: e.dma_start(
            out=scr["qt1"][g], in_=qstg_r.bf16(0, T)), reads=g_evs)
        prev_sp = [ev_spill]
        q_sp.append(ev_spill)
    arena.free(wt_r, q_sp[-1:])
    arena.free(qstg_r, q_sp[-1:])

    def v_chunks():
        return [[(384 * g + 128, 0, 64), (384 * g + 320, 64, 64)]
                for g in range(8)]

    vt_r = arena.alloc("mha_vt", NTC * 1040 // 2)

    def vt_chunk(tc):
        return vt_r.bf16(tc * 1040, (tc + 1) * 1040)

    wt_r, wt_evs = _cast_load_rows_T(env, "L1_attn_W_qkv", v_chunks(), D,
                                     f"mv{b}")
    ev_ones = p.comp("dve", lambda e: e.memset(
        vt_r.bf16(0, NTC * 1040).rearrange(
            "p (c k) -> p c k", k=65)[:, :, 64:65], 1.0),
        reads=vt_r.deps, sig=True)
    v_evs = []
    for tc in range(NTC):
        for n in range(2):
            bk = (tc * 2 + n) % 2 + 2
            deps = banks.acquire(bk)
            ev_mm = p.comp("pe", lambda e, tc=tc, n=n, bk=bk, wt_r=wt_r: [
                e.matmul(psum[bk][:, 0:512],
                         xb_ap(j, tc * 128, (tc + 1) * 128),
                         wt_r.bf16(0, EC * D).rearrange(
                             "p (j m) -> p j m", m=D)[:, j,
                                                      n * 512:(n + 1) * 512],
                         start=(j == 0), stop=False)
                for j in range(EC)] + [
                e.matmul(psum[bk][:, 0:512], env["ones_row"],
                         bv_row[:, n * 512:(n + 1) * 512],
                         start=False, stop=True)],
                reads=wt_evs + ln_evs + deps + [env["ev_const"], ev_bv],
                sig=True)
            banks.wrote(bk, ev_mm)
            dst = vt_chunk(tc).rearrange(
                "p (h k) -> p h k", k=65)[:, n * 8:(n + 1) * 8, 0:64]
            ev_s = p.comp("act", lambda e, dst=dst, bk=bk: e.copy(
                dst, psum[bk][:, 0:512].rearrange(
                    "p (h k) -> p h k", k=64)),
                reads=[ev_mm, ev_ones], sig=True)
            banks.read(bk, ev_s)
            v_evs.append(ev_s)
    arena.free(wt_r, v_evs[-1:])
    arena.free(xintb_r, k_evs[-1:] + q_sp[-1:] + v_evs[-1:])

    wo_r, wo_evs = _cast_load_weight_T(env, "L1_attn_W_out", EC, D, f"mo{b}")
    att_r = arena.alloc("mha_att", EC * 512 // 2)
    qt_r = arena.alloc("mha_qtile", 2 * EC * 512 // 2, side="high")
    pt0_r = arena.alloc("mha_pt0", 2 * NTC * 512 // 2, side="high")
    pt1_r = arena.alloc("mha_pt1", 2 * NTC * 512 // 2, side="high")
    ptslot = {0: pt0_r, 1: pt1_r}
    ob_r = arena.alloc("mha_ostg", EC * 512 // 2, side="high")
    qslot_ev = {0: list(qt_r.deps), 1: list(qt_r.deps)}
    attsp_evs = []
    ost_prev = list(ob_r.deps)
    att_prev = {g: list(att_r.deps) for g in range(8)}
    pt_prev = {0: list(pt0_r.deps), 1: list(pt1_r.deps)}
    p7f = env["psum_t"][:, :].bitcast(F32)
    rsA = SMALL[64:65, 0:512]
    rsB = SMALL[64:65, 512:1024]
    ones64 = SMALL[64:65, 1592:1656]
    ev_o64 = p.comp("dve", lambda e: e.memset(ones64, 1.0),
                    reads=env["rowchain"][-4:], sig=True)
    env["ev_ones64"] = ev_o64
    jobs = [(tt, g) for tt in range(NTT) for g in range(8)]
    qtile_of = {}
    qtile_ev = {}
    win_evs = []

    def plan_scores(j):
        tt, g = jobs[j]
        sl = j % 2
        if g == 0:
            qsl = tt % 2
            qtile = qt_r.bf16(qsl * EC * 512, (qsl + 1) * EC * 512)
            qtile_of[tt] = qtile
            qtile_ev[tt] = p.dma("sp", lambda e, tt=tt, qtile=qtile:
                                 e.dma_start(
                out=qtile.rearrange("p (g m) -> p g m", m=512),
                in_=scr["qt1"][:, :, tt * 512:(tt + 1) * 512].rearrange(
                    "g p m -> p g m")),
                reads=qslot_ev[qsl] + q_sp)
        qtile = qtile_of[tt]
        ev_q = qtile_ev[tt]
        exp_evs = []
        for tkc in range(NTC):
            bkA = 2 * (tkc % 2)
            bkB = bkA + 1
            pair_t = env["psum_pair"][tkc % 2]
            depsA = banks.acquire(bkA)
            depsB = banks.acquire(bkB)
            ev_mm = p.comp("pe", lambda e, g=g, tkc=tkc, bkA=bkA,
                           bkB=bkB, qtile=qtile: [
                e.matmul(psum[bkA][:, 0:512],
                         kt_r.bf16(g * T + tkc * 128,
                                   g * T + (tkc + 1) * 128)[0:64, :],
                         qtile[0:64, g * 512:(g + 1) * 512],
                         start=True, stop=True, tile_position=(0, 0)),
                e.matmul(psum[bkB][:, 0:512],
                         kt_r.bf16(g * T + tkc * 128,
                                   g * T + (tkc + 1) * 128)[64:128, :],
                         qtile[64:128, g * 512:(g + 1) * 512],
                         start=True, stop=True, tile_position=(64, 0))],
                reads=[ev_q] + k_evs[-1:] + depsA + depsB, sig=True)
            banks.wrote(bkA, ev_mm)
            banks.wrote(bkB, ev_mm)
            dstAB = ptslot[sl].bf16(tkc * 1024, (tkc + 1) * 1024)
            ev_e = p.comp("act", lambda e, dstAB=dstAB, pair_t=pair_t:
                          e.activation(
                dstAB, pair_t[:, 0:1024], AF.Exp,
                bias=env["cell"][-SHIFT_MHA], scale=0.125),
                reads=[ev_mm, env["ev_cells"]] + pt_prev[sl], sig=True)
            banks.read(bkA, ev_e)
            banks.read(bkB, ev_e)
            exp_evs.append(ev_e)
        return (None, None, exp_evs, sl)

    def _plan_wo(tt):
        nonlocal win_evs, ost_prev
        ob_evs = []
        for dc in range(EC):
            bk = dc % 2
            deps = banks.acquire(bk)
            ev_mm = p.comp("pe", lambda e, dc=dc, bk=bk: [
                e.matmul(psum[bk][:, 0:512],
                         wo_r.bf16(0, EC * D).rearrange(
                             "p (j m) -> p j m", m=D)[:, j,
                                                      dc * 128:(dc + 1) * 128],
                         att_r.bf16(j * 512, (j + 1) * 512),
                         start=(j == 0), stop=(j == EC - 1))
                for j in range(EC)],
                reads=win_evs + wo_evs + deps, sig=True)
            banks.wrote(bk, ev_mm)
            dst = ob_r.bf16(dc * 512, (dc + 1) * 512)
            ev_s = p.comp("act", lambda e, dst=dst, bk=bk, dc=dc:
                          e.activation(
                dst, psum[bk][:, 0:512], AF.Identity,
                bias=SMALL[:, 1056 + dc:1057 + dc]),
                reads=[ev_mm, ev_bo] + ost_prev, sig=True)
            banks.read(bk, ev_s)
            ob_evs.append(ev_s)
        ev_asp = p.dma("gp", lambda e, tt=tt: e.dma_start(
            out=scr["att"][:, :, tt * 512:(tt + 1) * 512].rearrange(
                "c p m -> p c m"),
            in_=ob_r.bf16(0, EC * 512).rearrange("p (c m) -> p c m", m=512)),
            reads=ob_evs)
        ost_prev = [ev_asp]
        attsp_evs.append(ev_asp)
        win_evs = []

    def _finish(rec):
        g, tt = rec["g"], rec["tt"]
        rbb = rec["i"] % 4
        deps_rb = banks.acquire(rbb)
        ev_rb = p.comp("pe", lambda e, rbb=rbb: [
            e.matmul(psum[rbb][0:64, 0:512], ones64, rsA,
                     start=True, stop=True, tile_position=(64, 0)),
            e.matmul(psum[rbb][64:128, 0:512], ones64, rsB,
                     start=True, stop=True, tile_position=(64, 64))],
            reads=[rec["ev_rA"], rec["ev_rB"], ev_o64] + deps_rb, sig=True)
        banks.wrote(rbb, ev_rb)
        env["rowchain"] = [ev_rb]
        rec["ev_rb"] = ev_rb
        ev_rbe = p.comp("act", lambda e, rbb=rbb: e.copy(
            rbs_r.f32(0, 512), psum[rbb][:, 0:512]),
            reads=[ev_rb] + rbs_r.deps + env.get("_rbs_last", []), sig=True)
        banks.read(rbb, ev_rbe)
        dst = att_r.bf16(g * 512, (g + 1) * 512)
        ev_nA = p.comp("dve", lambda e, dst=dst, rec=rec: e.tensor_tensor(
            dst[0:64, :], psum[rec["pvA"]][0:64, 0:512],
            rbs_r.f32(0, 512)[0:64, :], OP.mult),
            reads=[rec["ev_pv"], ev_rbe] + att_prev[g], sig=True)
        ev_n = p.comp("dve", lambda e, dst=dst, rec=rec: e.tensor_tensor(
            dst[64:128, :], rec["pvB_ap"][0:64, 0:512],
            rbs_r.f32(0, 512)[64:128, :], OP.mult),
            reads=[rec["ev_pv"], ev_rbe, ev_nA], sig=True)
        banks.read(rec["pvA"], ev_nA)
        banks.read(rec["pvB"], ev_n)
        env["_rbs_last"] = [ev_n, ev_nA]
        att_prev[g] = [ev_n, ev_nA]
        win_evs.append(ev_n)
        if g == 7:
            qslot_ev[tt % 2] = [rec["ev_pv"]]
            _plan_wo(tt)

    prev = None
    nxt = plan_scores(0)
    for i in range(len(jobs)):
        tt, g = jobs[i]
        _pa, _pb, exp_evs, sl = nxt
        if i + 1 < len(jobs):
            nxt = plan_scores(i + 1)
        pvA, pvB = (4, 5) if i % 2 == 0 else (6, 7)
        pvB_ap = p7f if pvB == 7 else psum[pvB]
        depsA = banks.acquire(pvA)
        depsB = banks.acquire(pvB)
        ptr = ptslot[sl]
        ev_pv = p.comp("pe", lambda e, g=g, ptr=ptr, pvA=pvA,
                       pvB_ap=pvB_ap: [
            mm for tkc in range(NTC) for mm in (
                e.matmul(psum[pvA][0:65, 0:512],
                         vt_chunk(tkc)[:, (2 * g) * 65:(2 * g) * 65 + 65],
                         ptr.bf16(tkc * 1024, tkc * 1024 + 512),
                         start=(tkc == 0), stop=(tkc == NTC - 1),
                         tile_position=(0, 0)),
                e.matmul(pvB_ap[0:65, 0:512],
                         vt_chunk(tkc)[
                             :, (2 * g + 1) * 65:(2 * g + 1) * 65 + 65],
                         ptr.bf16(tkc * 1024 + 512, (tkc + 1) * 1024),
                         start=(tkc == 0), stop=(tkc == NTC - 1),
                         tile_position=(0, 0)))],
            reads=exp_evs + depsA + depsB + v_evs[-1:], sig=True)
        banks.wrote(pvA, ev_pv)
        banks.wrote(pvB, ev_pv)
        if prev is not None:
            _finish(prev)
        ev_rA = p.comp("dve", lambda e, pvA=pvA: e.reciprocal(
            rsA, psum[pvA][64:65, 0:512]),
            reads=[ev_pv] + ([prev["ev_rb"]] if prev else []) +
                  env["rowchain"][-2:], sig=True)
        ev_rB = p.comp("dve", lambda e, pvB_ap=pvB_ap: e.reciprocal(
            rsB, pvB_ap[64:65, 0:512]), reads=[ev_rA, ev_pv], sig=True)
        banks.read(pvA, ev_rA)
        banks.read(pvB, ev_rB)
        prev = dict(i=i, tt=tt, g=g, pvA=pvA, pvB=pvB, pvB_ap=pvB_ap,
                    ev_pv=ev_pv, ev_rA=ev_rA, ev_rB=ev_rB)
        pt_prev[sl] = [ev_pv] + exp_evs[-1:]
    _finish(prev)
    for r in (kt_r, vt_r, wo_r, att_r, qt_r, pt0_r, pt1_r, ob_r,
              rbs_r):
        arena.free(r, attsp_evs[-1:] + env.get("_rbs_last", []))
    return attsp_evs


INV2PI = float(np.float32(1.0 / 6.283185307179586))
PI_F = float(np.float32(3.1415926535897932))
HALFPI_F = float(np.float32(1.5707963267948966))
TWOPI_F = float(np.float32(6.283185307179586))


def _wrap_pm_pi(env, out, in_, shift, reads, tmp):
    p = env["p"]
    ev = p.comp("dve", lambda e: e.tensor_single_scalar(
        out, in_, shift, OP.add), reads=reads, sig=True)
    ev = p.comp("dve", lambda e: e.tensor_scalar(
        tmp, out, -PI_F, TWOPI_F, OP.is_lt, OP.mult), reads=[ev], sig=True)
    ev = p.comp("dve", lambda e: e.tensor_tensor(
        out, out, tmp, OP.add), reads=[ev], sig=True)
    ev = p.comp("dve", lambda e: e.tensor_scalar(
        tmp, out, PI_F, -TWOPI_F, OP.is_gt, OP.mult), reads=[ev], sig=True)
    ev = p.comp("dve", lambda e: e.tensor_tensor(
        out, out, tmp, OP.add), reads=[ev], sig=True)
    ev = p.comp("dve", lambda e: e.tensor_scalar(
        out, out, -3.1415925, 3.1415925, OP.max, OP.min), reads=[ev],
        sig=True)
    return ev


def _rotary_tables(env, l, tb_r, qs_col, ks_col, ev_qs, reads):
    p, banks, arena = env["p"], env["banks"], env["arena"]
    psum = env["psum"]
    SMALL = env["sbuf"]["SMALL"]
    cap = env["cap"]
    ts_r = arena.alloc(f"tabsc{l}", 7 * 512 + 2 * T + 768, side="high")
    t_row = ts_r.f32(0, T)[0:1, :]
    pw = ts_r.f32(T, 2 * T)[0:1, :]
    idh = ts_r.f32(2 * T, 2 * T + 128)[0:1, :]
    b10k = ts_r.f32(2 * T + 128, 2 * T + 256)[0:1, 0:128]
    invf = ts_r.f32(2 * T + 256, 2 * T + 384)[0:1, 0:128]
    expn = ts_r.f32(2 * T + 384, 2 * T + 512)[0:1, 0:128]
    lnb = ts_r.f32(2 * T + 512, 2 * T + 640)[0:1, 0:128]
    o = 2 * T + 768
    kf = ts_r.f32(o, o + 512)
    rr = ts_r.f32(o + 512, o + 1024)
    rw = ts_r.f32(o + 1024, o + 1536)
    snc = ts_r.f32(o + 1536, o + 2048)
    ki = ts_r.f32(o + 2048, o + 2560)
    lsc = ts_r.f32(o + 2560, o + 3072)
    isc = ts_r.f32(o + 3072, o + 3584)
    ev_t = p.dma("sp", lambda e: e.dma_start(out=t_row, in_=cap["iota_t"]),
                 reads=ts_r.deps + (reads or []))
    ev_i = p.dma("sp", lambda e: e.dma_start(out=idh, in_=cap["iota_dh"]),
                 reads=ts_r.deps)
    ev1 = p.comp("dve", lambda e: [
        e.memset(b10k, 10000.0),
        e.tensor_scalar_mul(expn, idh, -1.0 / 32.0)],
        reads=[ev_i] + ts_r.deps, sig=True)
    ev1b = p.comp("act", lambda e: e.activation(
        b10k[0:1, 0:1], b10k[0:1, 0:1], AF.Ln), reads=[ev1], sig=True)
    ev1c = p.comp("dve", lambda e: e.tensor_scalar(
        expn, expn, b10k[0:1, 0:1], None, OP.mult), reads=[ev1b], sig=True)
    ev2 = p.comp("act", lambda e: e.activation(invf, expn, AF.Exp),
                 reads=[ev1c], sig=True)
    ev3 = p.comp("dve", lambda e: e.tensor_scalar(
        expn, idh, 2.0, 25.6, OP.mult, OP.add), reads=[ev2], sig=True)
    ev4 = p.comp("act", lambda e: e.activation(
        lnb, expn, AF.Ln, scale=float(np.float32(1.0 / 89.6))),
        reads=[ev3], sig=True)
    ev5 = p.comp("dve", lambda e: e.tensor_scalar(
        pw, t_row, 1.0 / 256.0, -4.0, OP.mult, OP.add),
        reads=[ev_t], sig=True)
    tabs = {"CSq": 0, "SSq": T // 2, "CSk": T, "SSk": 3 * T // 2}
    out_evs = []
    ev_prev = None
    for tt in range(NTT):
        deps6 = banks.acquire(6)
        ev_f = p.comp("pe", lambda e, tt=tt: e.matmul(
            psum[6][:, 0:512], invf, t_row[:, tt * 512:(tt + 1) * 512],
            start=True, stop=True), reads=[ev2, ev_t, env["ev_const"]] + deps6,
            sig=True)
        banks.wrote(6, ev_f)
        deps5 = banks.acquire(5)
        ev_l = p.comp("pe", lambda e, tt=tt: e.matmul(
            psum[5][:, 0:512], lnb, pw[:, tt * 512:(tt + 1) * 512],
            start=True, stop=True), reads=[ev4, ev5, env["ev_const"]] + deps5,
            sig=True)
        banks.wrote(5, ev_l)
        ki_i32 = ki.bitcast(I32)
        ev_k1 = p.comp("dve", lambda e: e.tensor_scalar(
            ki_i32, psum[6][:, 0:512], INV2PI, 0.5, OP.mult, OP.add),
            reads=[ev_f] + ([ev_prev] if ev_prev else []), sig=True)
        ev_k2 = p.comp("dve", lambda e: e.tensor_copy(kf, ki_i32),
                       reads=[ev_k1], sig=True)
        ev_r = p.comp("dve", lambda e: e.scalar_tensor_tensor(
            rr, kf, -CW1, psum[6][:, 0:512], OP.mult, OP.add),
            reads=[ev_k2, ev_f], sig=True)
        banks.read(6, ev_r)
        ev_r = p.comp("dve", lambda e: e.scalar_tensor_tensor(
            rr, kf, -CW2, rr, OP.mult, OP.add), reads=[ev_r], sig=True)
        ev_r = p.comp("dve", lambda e: e.scalar_tensor_tensor(
            rr, kf, -CW3, rr, OP.mult, OP.add), reads=[ev_r], sig=True)
        ev_c1 = _wrap_pm_pi(env, rw, rr, 0.0, [ev_r], kf)
        ev_s1 = p.comp("act", lambda e: e.activation(snc, rw, AF.Sin),
                       reads=[ev_c1], sig=True)
        ev_sc = p.comp("act", lambda e: e.activation(
            lsc, psum[5][:, 0:512], AF.Exp), reads=[ev_l, ev_s1], sig=True)
        ev_isc = p.comp("act", lambda e: e.activation(
            isc, psum[5][:, 0:512], AF.Exp, scale=-1.0),
            reads=[ev_l, ev_sc], sig=True)
        banks.read(5, ev_isc)
        banks.read(5, ev_sc)
        ev_m1 = p.comp("dve", lambda e: e.tensor_tensor(
            rw, snc, lsc, OP.mult), reads=[ev_s1, ev_sc], sig=True)
        ev_m2 = p.comp("dve", lambda e, tt=tt: e.tensor_scalar(
            tb_r.bf16(tabs["SSq"] * 2 + tt * 512,
                      tabs["SSq"] * 2 + (tt + 1) * 512),
            rw, SMALL[:, qs_col + 1:qs_col + 2], None, OP.mult),
            reads=[ev_m1, ev_qs] + (tb_r.deps if tt == 0 else []), sig=True)
        ev_m3 = p.comp("dve", lambda e: e.tensor_tensor(
            rw, snc, isc, OP.mult), reads=[ev_m2, ev_isc], sig=True)
        ev_m4 = p.comp("dve", lambda e, tt=tt: e.tensor_scalar(
            tb_r.bf16(tabs["SSk"] * 2 + tt * 512,
                      tabs["SSk"] * 2 + (tt + 1) * 512),
            rw, SMALL[:, ks_col + 1:ks_col + 2], None, OP.mult),
            reads=[ev_m3, ev_qs], sig=True)
        ev_c2 = _wrap_pm_pi(env, rw, rr, HALFPI_F, [ev_m4, ev_r], kf)
        ev_s2 = p.comp("act", lambda e: e.activation(snc, rw, AF.Sin),
                       reads=[ev_c2, ev_m4], sig=True)
        ev_m5 = p.comp("dve", lambda e: e.tensor_tensor(
            rw, snc, lsc, OP.mult), reads=[ev_s2, ev_sc], sig=True)
        ev_m6 = p.comp("dve", lambda e, tt=tt: e.tensor_scalar(
            tb_r.bf16(tabs["CSq"] * 2 + tt * 512,
                      tabs["CSq"] * 2 + (tt + 1) * 512),
            rw, SMALL[:, qs_col:qs_col + 1], None, OP.mult),
            reads=[ev_m5, ev_qs], sig=True)
        ev_m7 = p.comp("dve", lambda e: e.tensor_tensor(
            rw, snc, isc, OP.mult), reads=[ev_m6, ev_isc], sig=True)
        ev_m8 = p.comp("dve", lambda e, tt=tt: e.tensor_scalar(
            tb_r.bf16(tabs["CSk"] * 2 + tt * 512,
                      tabs["CSk"] * 2 + (tt + 1) * 512),
            rw, SMALL[:, ks_col:ks_col + 1], None, OP.mult),
            reads=[ev_m7, ev_qs], sig=True)
        ev_prev = ev_m8
        out_evs += [ev_m2, ev_m4, ev_m6, ev_m8]
    arena.free(ts_r, out_evs[-4:])
    return tabs, out_evs


def _l1ffn(env, b, ln_evs, att_evs):
    p, banks, arena = env["p"], env["banks"], env["arena"]
    psum, psum_t, scr, W = env["psum"], env["psum_t"], env["scr"], env["W"]
    SMALL = env["sbuf"]["SMALL"]
    xint_r = arena.alloc("xint2", EC * TZ + 8)
    ev_z0 = p.comp("dve", lambda e: e.memset(
        xint_r.f32(0, EC * TZ).rearrange("p (c z) -> p c z", z=TZ)
        [:, :, 0:1], 0.0), reads=xint_r.deps, sig=True)
    ld_evs = [ev_z0]
    for ec in range(EC):
        ld_evs.append(p.dma("sp", lambda e, ec=ec: e.dma_start(
            out=xint_r.f32(ec * TZ + 1, ec * TZ + 1 + T),
            in_=scr["xn1"][ec]), reads=ln_evs + xint_r.deps))
    state = ld_evs

    def xf_ap(ec, c0, c1, shift=False):
        o = ec * TZ + (0 if shift else 1)
        return xint_r.f32(o + c0, o + c1)

    for l in range(2):
        state = _l1ffn_layer(env, b, l, xint_r, xf_ap, state)
    return xint_r, state


def _l1ffn_layer(env, b, l, xint_r, xf_ap, state):
    p, banks, arena = env["p"], env["banks"], env["arena"]
    psum, psum_t, scr, W = env["psum"], env["psum_t"], env["scr"], env["W"]
    SMALL = env["sbuf"]["SMALL"]
    NG = 4

    bev = []
    for (col, wname) in ((1100, "L1_ffn_q_scale"), (1102, "L1_ffn_k_scale")):
        for rep in range(2):
            bev.append(p.dma("sp", lambda e, col=col, wname=wname, rep=rep:
                             e.dma_start(
                out=SMALL[rep * 64:(rep + 1) * 64, col:col + 1],
                in_=W[wname][l].rearrange("(a d) -> d a", a=1)),
                reads=state))
            bev.append(p.dma("sp", lambda e, col=col, wname=wname, rep=rep:
                             e.dma_start(
                out=SMALL[rep * 64:rep * 64 + 32, col + 1:col + 2],
                in_=W[wname][l, 32:64].rearrange("(a d) -> d a", a=1)),
                reads=state))
            bev.append(p.dma("sp", lambda e, col=col, wname=wname, rep=rep:
                             e.dma_start(
                out=SMALL[rep * 64 + 32:rep * 64 + 64, col + 1:col + 2],
                in_=W[wname][l, 0:32].rearrange("(a d) -> d a", a=1)),
                reads=state))
    bev.append(p.dma("sp", lambda e: e.dma_start(
        out=SMALL[:, 1104:1112],
        in_=W["L1_ffn_ln_g"][l].rearrange("(c p) -> p c", p=128),
        allow_slow_non_contiguous=True), reads=state))
    bev.append(p.dma("sp", lambda e: e.dma_start(
        out=SMALL[:, 1112:1120],
        in_=W["L1_ffn_ln_b"][l].rearrange("(c p) -> p c", p=128),
        allow_slow_non_contiguous=True), reads=state))
    bev.append(p.dma("sp", lambda e: e.dma_start(
        out=SMALL[:, 1120:1141],
        in_=W["L1_ffn_b1"][l, 0:2688].rearrange("(c p) -> p c", p=128),
        allow_slow_non_contiguous=True), reads=state))
    bev.append(p.dma("sp", lambda e: e.dma_start(
        out=SMALL[0:42, 1141:1142],
        in_=W["L1_ffn_b1"][l, 2688:2730].rearrange("(a d) -> d a", a=1)), reads=state))
    bev.append(p.dma("sp", lambda e: e.dma_start(
        out=SMALL[:, 1142:1163],
        in_=W["L1_ffn_b1"][l, 2730:5418].rearrange("(c p) -> p c", p=128),
        allow_slow_non_contiguous=True), reads=state))
    bev.append(p.dma("sp", lambda e: e.dma_start(
        out=SMALL[0:42, 1163:1164],
        in_=W["L1_ffn_b1"][l, 5418:5460].rearrange("(a d) -> d a", a=1)), reads=state))
    bev.append(p.dma("sp", lambda e: e.dma_start(
        out=SMALL[:, 1164:1172],
        in_=W["L1_ffn_b2"][l].rearrange("(c p) -> p c", p=128),
        allow_slow_non_contiguous=True), reads=state))
    ev_qs = bev[:12]

    tb_r = arena.alloc(f"tabs{l}", 2 * T)
    tabs, tab_evs = _rotary_tables(env, l, tb_r, 1100, 1102, bev, state)

    xb_r = arena.alloc(f"ixb{l}", (EC * TZ) // 2 + 8)
    xb_evs = [p.comp("dve", lambda e: e.memset(
        xb_r.bf16(0, EC * TZ).rearrange("p (c z) -> p c z", z=TZ)
        [:, :, 0:1], 0.0), reads=state + xb_r.deps, sig=True)]
    for ec in range(EC):
        xb_evs.append(p.comp("act", lambda e, ec=ec: e.copy(
            xb_r.bf16(ec * TZ + 1, ec * TZ + 1 + T),
            xf_ap(ec, 0, T)), reads=state + xb_r.deps, sig=True))
    state = state + xb_evs

    def xb_ap(ec, c0, c1, shift=False):
        o = ec * TZ + (0 if shift else 1)
        return xb_r.bf16(o + c0, o + c1)

    def tab(name, tt):
        o = tabs[name] * 2
        return tb_r.bf16(o + tt * 512, o + (tt + 1) * 512)

    wq_r = arena.alloc("iq_w", 2048, side="high")
    qr_r = arena.alloc("iq_q", NG * T // 2)
    kr_r = arena.alloc("iq_k", NG * T // 2)
    vt_r = arena.alloc("iq_v", NTC * 520 // 2)
    ev_vones = p.comp("dve", lambda e: e.memset(
        vt_r.bf16(0, NTC * 520).rearrange(
            "p (c k) -> p c k", k=65)[:, :, 64:65], 1.0),
        reads=vt_r.deps, sig=True)
    proj_evs = []
    wslot = [list(wq_r.deps)]

    def proj(part, sink_r, tag):
        ev_w = p.dma("gp", lambda e: e.dma_start(
            out=wq_r.bf16(0, EC * 512).rearrange("p (j m) -> p j m", m=512),
            in_=W["L1_ffn_wqkv"][l][:, part * 512:(part + 1) * 512].rearrange(
                "(j p) m -> p j m", p=128)),
            reads=wslot[0] + state, stream="w")
        evs = []
        if part < 2:
            for g in range(NG):
                for tt in range(NTT):
                    bk = (g * NTT + tt) % 2
                    deps = banks.acquire(bk)
                    ev_mm = p.comp("pe", lambda e, g=g, tt=tt, bk=bk: [
                        e.matmul(psum[bk][:, 0:512],
                                 wq_r.bf16(0, EC * 512).rearrange(
                                     "p (j m) -> p j m", m=512)[
                                     :, j, g * 128:(g + 1) * 128],
                                 xb_ap(j, tt * 512, (tt + 1) * 512,
                                       shift=(j >= 4)),
                                 start=(j == 0), stop=(j == EC - 1))
                        for j in range(EC)],
                        reads=[ev_w] + state + deps, sig=True)
                    banks.wrote(bk, ev_mm)
                    dst = sink_r.bf16(g * T + tt * 512, g * T + tt * 512 + 512)
                    ev_s = p.comp("act", lambda e, dst=dst, bk=bk: e.copy(
                        dst, psum[bk][:, 0:512]),
                        reads=[ev_mm] + (sink_r.deps if g == 0 and tt == 0
                                         else []), sig=True)
                    banks.read(bk, ev_s)
                    evs.append(ev_s)
        else:
            for tc in range(NTC):
                bk = tc % 2
                deps = banks.acquire(bk)
                ev_mm = p.comp("pe", lambda e, tc=tc, bk=bk: [
                    e.matmul(psum[bk][:, 0:512],
                             xb_ap(j, tc * 128, (tc + 1) * 128, shift=(j >= 4)),
                             wq_r.bf16(0, EC * 512).rearrange(
                                 "p (j m) -> p j m", m=512)[:, j, :],
                             start=(j == 0), stop=(j == EC - 1))
                    for j in range(EC)],
                    reads=[ev_w] + state + deps, sig=True)
                banks.wrote(bk, ev_mm)
                dst = sink_r.bf16(tc * 520, (tc + 1) * 520).rearrange(
                    "p (h k) -> p h k", k=65)[:, :, 0:64]
                ev_s = p.comp("act", lambda e, dst=dst, bk=bk: e.copy(
                    dst, psum[bk][:, 0:512].rearrange(
                        "p (h k) -> p h k", k=64)),
                    reads=[ev_mm, ev_vones] + (sink_r.deps if tc == 0 else []),
                    sig=True)
                banks.read(bk, ev_s)
                evs.append(ev_s)
        wslot[0] = evs[-2:]
        return evs

    q_evs = proj(0, qr_r, "q")
    k_evs = proj(1, kr_r, "k")
    v_evs = proj(2, vt_r, "v")
    proj_evs = q_evs + k_evs + v_evs
    arena.free(xb_r, proj_evs[-1:] + q_evs[-1:] + k_evs[-1:])

    ws_r = arena.alloc("iq_s", 1024, side="high")
    rot_evs = []
    nrm = SMALL[0:16, 0:512]
    for tt in range(NTT):
        deps4 = banks.acquire(4)
        ev_nm = None
        sq_evs = []
        for s, src_r in ((0, qr_r), (1, kr_r)):
            for g in range(NG):
                q2 = ws_r.bf16(0, 512)
                ev_sq = p.comp("act", lambda e, src_r=src_r, g=g, tt=tt,
                               q2=q2: e.activation(
                    q2, src_r.bf16(g * T + tt * 512, g * T + tt * 512 + 512),
                    AF.Square),
                    reads=proj_evs[-1:] + ws_r.deps +
                          ([ev_nm] if ev_nm else []), sig=True)
                ev_nm = p.comp("pe", lambda e, g=g, s=s, q2=q2: e.matmul(
                    psum[4][0:16, 0:512],
                    env["e16_b"][:, (g * 2 + s) * 16:(g * 2 + s) * 16 + 16],
                    q2, start=(s == 0 and g == 0),
                    stop=(s == 1 and g == NG - 1)),
                    reads=[ev_sq, env["ev_constb"]] +
                          (deps4 if s == 0 and g == 0 else []),
                    sig=True)
                sq_evs.append(ev_sq)
        banks.wrote(4, ev_nm)
        ev_sr = p.comp("act", lambda e: e.activation(
            nrm, psum[4][0:16, 0:512], AF.Sqrt),
            reads=[ev_nm] + env["rowchain"][-4:], sig=True)
        banks.read(4, ev_sr)
        ev_cl = p.comp("dve", lambda e: e.tensor_scalar(
            nrm, nrm, 1e-12, None, OP.max), reads=[ev_sr], sig=True)
        ev_rc = p.comp("dve", lambda e: e.reciprocal(nrm, nrm),
                       reads=[ev_cl], sig=True)
        nrmb = SMALL[0:16, 1172:1428].bitcast(BF16)
        ev_rc = p.comp("dve", lambda e: e.tensor_copy(nrmb, nrm),
                       reads=[ev_rc], sig=True)
        for s, src_r in ((0, qr_r), (1, kr_r)):
            csn = "CSq" if s == 0 else "CSk"
            ssn = "SSq" if s == 0 else "SSk"
            for g in range(NG):
                deps5 = banks.acquire(5)
                ev_rb = p.comp("pe", lambda e, g=g, s=s: e.matmul(
                    psum[5][:, 0:512],
                    env["ebr_f"][0:16, (g * 2 + s) * 128:
                                 (g * 2 + s) * 128 + 128],
                    nrmb, start=True, stop=True),
                    reads=[ev_rc, env["ev_gconst"]] + deps5, sig=True)
                banks.wrote(5, ev_rb)
                env["rowchain"] = env["rowchain"][-3:] + [ev_rb]
                src = src_r.bf16(g * T + tt * 512, g * T + tt * 512 + 512)
                qn = ws_r.bf16(512, 1024)
                ev_qn = p.comp("dve", lambda e, src=src, qn=qn: e.tensor_tensor(
                    qn, src, psum[5][:, 0:512], OP.mult),
                    reads=[ev_rb] + sq_evs + rot_evs[-1:], sig=True)
                banks.read(5, ev_qn)
                deps6 = banks.acquire(6)
                ev_rh = p.comp("pe", lambda e, qn=qn: e.matmul(
                    psum[6][:, 0:512], env["rot_pi_b"], qn,
                    start=True, stop=True),
                    reads=[ev_qn, env["ev_constb"]] + deps6, sig=True)
                banks.wrote(6, ev_rh)
                t1 = ws_r.bf16(1024, 1536)
                ev_t1 = p.comp("dve", lambda e, t1=t1, ssn=ssn, tt=tt:
                               e.tensor_tensor(
                    t1, psum[6][:, 0:512], tab(ssn, tt), OP.mult),
                    reads=[ev_rh] + tab_evs[-1:] + rot_evs[-1:], sig=True)
                banks.read(6, ev_t1)
                t2 = ws_r.bf16(1536, 2048)
                ev_t2 = p.comp("dve", lambda e, t2=t2, qn=qn, csn=csn, tt=tt:
                               e.tensor_tensor(
                    t2, qn, tab(csn, tt), OP.mult),
                    reads=[ev_qn, ev_t1], sig=True)
                ev_o = p.comp("dve", lambda e, src=src, t1=t1, t2=t2:
                              e.tensor_tensor(src, t1, t2, OP.add),
                              reads=[ev_t2], sig=True)
                rot_evs.append(ev_o)

    wo_r = arena.alloc("iwo", 2048, side="high")
    ev_wo = p.dma("gp", lambda e: e.dma_start(
        out=wo_r.bf16(0, 4 * D).rearrange("p (j m) -> p j m", m=D),
        in_=W["L1_ffn_wo"][l].rearrange("(j p) m -> p j m", p=128)),
        reads=wo_r.deps + state, stream="w")
    pt_r = arena.alloc("ipt", 2 * 8 * 1024 // 2, side="high")
    ot_r = arena.alloc("iot", NG * 512 // 2, side="high")
    rbs_r = arena.alloc(f"irbs{l}", 512, side="high")
    res_upd = []
    pt_prev = {0: list(pt_r.deps), 1: list(pt_r.deps)}
    ot_prev = {g: list(ot_r.deps) for g in range(NG)}
    jobs = [(w, g) for w in range(NTT) for g in range(NG)]

    def plan_scores_loc(jid):
        w, g = jobs[jid]
        sl = jid % 2
        base = sl * 8 * 1024
        kc0 = 4 * (w - 1)
        tkcs = [c for c in range(8) if kc0 + c >= 0]
        exp_evs = []
        for ci, c in enumerate(tkcs):
            kc = kc0 + c
            bkA = 2 * (ci % 2)
            bkB = bkA + 1
            pair_t = env["psum_pair"][ci % 2]
            depsA = banks.acquire(bkA)
            depsB = banks.acquire(bkB)
            ev_mm = p.comp("pe", lambda e, g=g, w=w, kc=kc, bkA=bkA,
                           bkB=bkB: [
                e.matmul(psum[bkA][:, 0:512],
                         kr_r.bf16(g * T + kc * 128,
                                   g * T + (kc + 1) * 128)[0:64, :],
                         qr_r.bf16(g * T + w * 512,
                                   g * T + (w + 1) * 512)[0:64, :],
                         start=True, stop=True, tile_position=(0, 0)),
                e.matmul(psum[bkB][:, 0:512],
                         kr_r.bf16(g * T + kc * 128,
                                   g * T + (kc + 1) * 128)[64:128, :],
                         qr_r.bf16(g * T + w * 512,
                                   g * T + (w + 1) * 512)[64:128, :],
                         start=True, stop=True, tile_position=(64, 0))],
                reads=rot_evs[-1:] + depsA + depsB, sig=True)
            banks.wrote(bkA, ev_mm)
            banks.wrote(bkB, ev_mm)
            dstAB = pt_r.bf16(base + ci * 1024, base + (ci + 1) * 1024)
            ev_e = p.comp("act", lambda e, dstAB=dstAB, pair_t=pair_t:
                          e.activation(
                dstAB, pair_t[:, 0:1024], AF.Exp,
                bias=env["cell"][-SHIFT_LOC], scale=8.0),
                reads=[ev_mm, env["ev_cells"]] + pt_prev[sl], sig=True)
            banks.read(bkA, ev_e)
            banks.read(bkB, ev_e)
            dstAB = pt_r.bf16(base + ci * 1024,
                              base + (ci + 1) * 1024).rearrange(
                "p (n m) -> p n m", n=2)
            mka = env["loc_mask"][:, c * 512:(c + 1) * 512]
            ev_mk = p.comp("dve", lambda e, dstAB=dstAB, mka=mka:
                           e.tensor_tensor(
                dstAB, dstAB,
                mka.unsqueeze(1).broadcast_to([128, 2, 512]), OP.mult),
                reads=[ev_e, env["ev_constb"]], sig=True)
            exp_evs.append(ev_mk)
        return (sl, base, tuple(tkcs), exp_evs)

    rsA = SMALL[64:65, 0:512]
    rsB = SMALL[64:65, 512:1024]
    ones64 = SMALL[64:65, 1592:1656]
    p7f = env["psum_t"][:, :].bitcast(F32)
    win_evs = []
    res_upd = res_upd

    def _ifinish(rec):
        nonlocal win_evs
        g, w = rec["g"], rec["w"]
        rbb = rec["i"] % 4
        deps_rb = banks.acquire(rbb)
        ev_rb = p.comp("pe", lambda e, rbb=rbb: [
            e.matmul(psum[rbb][0:64, 0:512], ones64, rsA,
                     start=True, stop=True, tile_position=(64, 0)),
            e.matmul(psum[rbb][64:128, 0:512], ones64, rsB,
                     start=True, stop=True, tile_position=(64, 64))],
            reads=[rec["ev_rA"], rec["ev_rB"], env["ev_ones64"]] + deps_rb,
            sig=True)
        banks.wrote(rbb, ev_rb)
        env["rowchain"] = [ev_rb]
        rec["ev_rb"] = ev_rb
        ev_rbe = p.comp("act", lambda e, rbb=rbb: e.copy(
            rbs_r.f32(0, 512), psum[rbb][:, 0:512]),
            reads=[ev_rb] + rbs_r.deps + env.get("_irbs_last", []), sig=True)
        banks.read(rbb, ev_rbe)
        dst = ot_r.bf16(g * 512, (g + 1) * 512)
        ev_nA = p.comp("dve", lambda e, dst=dst, rec=rec: e.tensor_tensor(
            dst[0:64, :], psum[rec["pvA"]][0:64, 0:512],
            rbs_r.f32(0, 512)[0:64, :], OP.mult),
            reads=[rec["ev_pv"], ev_rbe] + ot_prev[g], sig=True)
        ev_n = p.comp("dve", lambda e, dst=dst, rec=rec: e.tensor_tensor(
            dst[64:128, :], rec["pvB_ap"][0:64, 0:512],
            rbs_r.f32(0, 512)[64:128, :], OP.mult),
            reads=[rec["ev_pv"], ev_rbe, ev_nA], sig=True)
        banks.read(rec["pvA"], ev_nA)
        banks.read(rec["pvB"], ev_n)
        env["_irbs_last"] = [ev_n, ev_nA]
        ot_prev[g] = [ev_n, ev_nA]
        win_evs.append(ev_n)
        if g == NG - 1:
            for dc in range(EC):
                bk = dc % 2
                deps = banks.acquire(bk)
                ev_mm = p.comp("pe", lambda e, dc=dc, bk=bk: [
                    e.matmul(psum[bk][:, 0:512],
                             wo_r.bf16(0, 4 * D).rearrange(
                                 "p (j m) -> p j m", m=D)[
                                 :, j, dc * 128:(dc + 1) * 128],
                             ot_r.bf16(j * 512, (j + 1) * 512),
                             start=(j == 0), stop=(j == 3))
                    for j in range(4)],
                    reads=win_evs + [ev_wo] + deps, sig=True)
                banks.wrote(bk, ev_mm)
                ev_add = p.comp("dve", lambda e, dc=dc, w=w, bk=bk:
                                e.tensor_tensor(
                    xf_ap(dc, w * 512, (w + 1) * 512),
                    xf_ap(dc, w * 512, (w + 1) * 512),
                    psum[bk][:, 0:512], OP.add),
                    reads=[ev_mm] + state, sig=True)
                banks.read(bk, ev_add)
                res_upd.append(ev_add)
            win_evs = []

    prev = None
    nxt = plan_scores_loc(0)
    for jid, (w, g) in enumerate(jobs):
        sl, base, tkcs, exp_evs = nxt
        if jid + 1 < len(jobs):
            nxt = plan_scores_loc(jid + 1)
        kc0 = 4 * (w - 1)
        pvA, pvB = (4, 5) if jid % 2 == 0 else (6, 7)
        pvB_ap = p7f if pvB == 7 else psum[pvB]
        depsA = banks.acquire(pvA)
        depsB = banks.acquire(pvB)
        ev_pv = p.comp("pe", lambda e, g=g, base=base, tkcs=tkcs, kc0=kc0,
                       pvA=pvA, pvB_ap=pvB_ap: [
            mm for ci, c in enumerate(tkcs) for mm in (
                e.matmul(psum[pvA][0:65, 0:512],
                         vt_r.bf16((kc0 + c) * 520 + (2 * g) * 65,
                                   (kc0 + c) * 520 + (2 * g) * 65 + 65),
                         pt_r.bf16(base + ci * 1024, base + ci * 1024 + 512),
                         start=(ci == 0), stop=(ci == len(tkcs) - 1),
                         tile_position=(0, 0)),
                e.matmul(pvB_ap[0:65, 0:512],
                         vt_r.bf16((kc0 + c) * 520 + (2 * g + 1) * 65,
                                   (kc0 + c) * 520 + (2 * g + 1) * 65 + 65),
                         pt_r.bf16(base + ci * 1024 + 512,
                                   base + (ci + 1) * 1024),
                         start=(ci == 0), stop=(ci == len(tkcs) - 1),
                         tile_position=(0, 0)))],
            reads=exp_evs + depsA + depsB + v_evs[-1:] + [ev_vones],
            sig=True)
        banks.wrote(pvA, ev_pv)
        banks.wrote(pvB, ev_pv)
        if prev is not None:
            _ifinish(prev)
        ev_rA = p.comp("dve", lambda e, pvA=pvA: e.reciprocal(
            rsA, psum[pvA][64:65, 0:512]),
            reads=[ev_pv] + ([prev["ev_rb"]] if prev else []) +
                  env["rowchain"][-2:], sig=True)
        ev_rB = p.comp("dve", lambda e, pvB_ap=pvB_ap: e.reciprocal(
            rsB, pvB_ap[64:65, 0:512]), reads=[ev_rA, ev_pv], sig=True)
        banks.read(pvA, ev_rA)
        banks.read(pvB, ev_rB)
        prev = dict(i=jid, w=w, g=g, pvA=pvA, pvB=pvB, pvB_ap=pvB_ap,
                    ev_pv=ev_pv, ev_rA=ev_rA, ev_rB=ev_rB)
        pt_prev[sl] = [ev_pv] + exp_evs[-1:]
    _ifinish(prev)
    state = state + res_upd
    if DBG.get("stop") == f"iatt{l}":
        raise StopBuild(_dump_xint(env, xf_ap, state))
    for r in (wq_r, qr_r, kr_r, vt_r, ws_r, wo_r, pt_r, ot_r, tb_r, rbs_r):
        arena.free(r, res_upd[-2:] + pt_prev[0] + pt_prev[1] +
                   env.get("_irbs_last", []))

    xh_r = arena.alloc("ixh", EC * T // 2)
    ln2 = _layernorm(
        env,
        lambda ec: (xf_ap(ec, 0, T, shift=(ec >= 4)), state),
        [("bf16", lambda ec, tt: xh_r.bf16(ec * T + tt * 512,
                                           ec * T + tt * 512 + 512))],
        f"iln{l}b{b}",
        gamma_beta=lambda ec: (SMALL[:, 1104 + ec:1105 + ec],
                               SMALL[:, 1112 + ec:1113 + ec]),
        gb_reads=bev)
    w1_r = arena.alloc("iw1", 2 * 2048, side="high")
    w2_r = arena.alloc("iw2", 2 * 2048, side="high")
    h1_r = arena.alloc("ih1", 22 * 512 // 2, side="high")
    ag_r = arena.alloc("iag", 512, side="high")
    w1slot = {0: list(w1_r.deps), 1: list(w1_r.deps)}
    w2slot = {0: list(w2_r.deps), 1: list(w2_r.deps)}
    h1_prev = {0: list(h1_r.deps) + list(ag_r.deps),
               1: list(h1_r.deps)}
    ffn_out = []
    CH = [(i * 128, 128) for i in range(21)] + [(2688, 42)]
    for tt in range(NTT):
        h_evs = []
        for pc, (r0, nr) in enumerate(CH):
            sl = pc % 2
            w1a = w1_r.bf16(sl * 4096, sl * 4096 + 2048)[:, 0:EC * nr]
            w1g = w1_r.bf16(sl * 4096 + 2048, (sl + 1) * 4096)[:, 0:EC * nr]
            ev_w1 = [
                p.dma("gp", lambda e, r0=r0, nr=nr, w1a=w1a: e.dma_start(
                    out=w1a.rearrange("p (j m) -> p j m", m=nr),
                    in_=W["L1_ffn_w1"][l][:, r0:r0 + nr].rearrange(
                        "(j p) m -> p j m", p=128)),
                    reads=w1slot[sl], stream="w"),
                p.dma("gp", lambda e, r0=r0, nr=nr, w1g=w1g: e.dma_start(
                    out=w1g.rearrange("p (j m) -> p j m", m=nr),
                    in_=W["L1_ffn_w1"][l][:, 2730 + r0:2730 + r0 + nr]
                        .rearrange("(j p) m -> p j m", p=128)),
                    stream="w"),
            ]
            bka = pc % 2
            bkg = 2 + pc % 2
            depsa = banks.acquire(bka)
            depsg = banks.acquire(bkg)
            ev_a = p.comp("pe", lambda e, tt=tt, nr=nr, w1a=w1a, bka=bka: [
                e.matmul(psum[bka][0:nr, 0:512],
                         w1a[:, j * nr:(j + 1) * nr],
                         xh_r.bf16(j * T + tt * 512, j * T + tt * 512 + 512),
                         start=(j == 0), stop=(j == EC - 1))
                for j in range(EC)],
                reads=ev_w1 + ln2 + depsa, sig=True)
            banks.wrote(bka, ev_a)
            ev_g = p.comp("pe", lambda e, tt=tt, nr=nr, w1g=w1g, bkg=bkg: [
                e.matmul(psum[bkg][0:nr, 0:512],
                         w1g[:, j * nr:(j + 1) * nr],
                         xh_r.bf16(j * T + tt * 512, j * T + tt * 512 + 512),
                         start=(j == 0), stop=(j == EC - 1))
                for j in range(EC)],
                reads=ev_w1 + ln2 + depsg, sig=True)
            banks.wrote(bkg, ev_g)
            w1slot[sl] = [ev_a, ev_g]
            acol = 1120 + pc if nr == 128 else 1141
            gcol = 1142 + pc if nr == 128 else 1163
            ap_t = ag_r.bf16(0, 512)[0:nr, :]
            ev_ap = p.comp("act", lambda e, ap_t=ap_t, bka=bka, nr=nr,
                           acol=acol: e.activation(
                ap_t, psum[bka][0:nr, 0:512], AF.Identity,
                bias=SMALL[0:nr, acol:acol + 1]),
                reads=[ev_a] + bev + h1_prev[0], sig=True)
            banks.read(bka, ev_ap)
            gp_t = ag_r.bf16(512, 1024)[0:nr, :]
            ev_gp = p.comp("act", lambda e, gp_t=gp_t, bkg=bkg, nr=nr,
                           gcol=gcol: e.activation(
                gp_t, psum[bkg][0:nr, 0:512], AF.Gelu,
                bias=SMALL[0:nr, gcol:gcol + 1]),
                reads=[ev_g] + bev + h1_prev[0], sig=True)
            banks.read(bkg, ev_gp)
            dst = h1_r.bf16(pc * 512, pc * 512 + 512)[0:nr, :]
            ev_h = p.comp("dve", lambda e, dst=dst, ap_t=ap_t, gp_t=gp_t:
                          e.tensor_tensor(dst, ap_t, gp_t, OP.mult),
                          reads=[ev_ap, ev_gp] + h1_prev[1], sig=True)
            h1_prev[0] = [ev_h]
            h_evs.append(ev_h)
        h1_prev[1] = h_evs[-2:]
        for dc in range(EC):
            sl = dc % 2
            w2s = w2_r.bf16(sl * 4096, sl * 4096 + 2816)
            ev_w2a = p.dma("gp", lambda e, dc=dc, w2s=w2s: e.dma_start(
                out=w2s[:, 0:21 * 128].rearrange("p (c m) -> p c m", m=128),
                in_=W["L1_ffn_w2"][l, 0:2688, dc * 128:(dc + 1) * 128]
                    .rearrange("(c p) m -> p c m", p=128)),
                reads=w2slot[sl], stream="w")
            ev_w2b = p.dma("gp", lambda e, dc=dc, w2s=w2s: e.dma_start(
                out=w2s[0:42, 21 * 128:22 * 128],
                in_=W["L1_ffn_w2"][l, 2688:2730, dc * 128:(dc + 1) * 128]),
                stream="w")
            ev_w2 = [ev_w2a, ev_w2b]
            bk = 4 + dc % 2
            deps = banks.acquire(bk)
            ev_mm = p.comp("pe", lambda e, dc=dc, bk=bk, w2s=w2s: [
                e.matmul(psum[bk][:, 0:512],
                         w2s[0:nr2, cc * 128:cc * 128 + 128],
                         h1_r.bf16(cc * 512, cc * 512 + 512)[0:nr2, :],
                         start=(cc == 0), stop=(cc == 21))
                for cc, (r02, nr2) in enumerate(CH)],
                reads=ev_w2 + h_evs + deps, sig=True)
            banks.wrote(bk, ev_mm)
            w2slot[sl] = [ev_mm]
            ev_add = p.comp("dve", lambda e, dc=dc, tt=tt, bk=bk:
                            e.scalar_tensor_tensor(
                xf_ap(dc, tt * 512, (tt + 1) * 512),
                psum[bk][:, 0:512], SMALL[:, 1164 + dc:1165 + dc],
                xf_ap(dc, tt * 512, (tt + 1) * 512), OP.add, OP.add),
                reads=[ev_mm] + state + bev, sig=True)
            banks.read(bk, ev_add)
            ffn_out.append(ev_add)
        h1_prev[1] = ffn_out[-2:]
    state = state + ln2 + ffn_out
    if DBG.get("stop") == f"iffn{l}":
        raise StopBuild(_dump_xint(env, xf_ap, state))
    for r in (xh_r, w1_r, w2_r, h1_r, ag_r):
        arena.free(r, ffn_out[-2:])
    return state


# version 16
# speedup vs baseline: 1.2406x; 1.0077x over previous
import contextlib
import numpy as np
import concourse.bass as bass
import concourse.mybir as mybir
from concourse.bass_utils import run_bass_kernel_spmd

F32 = mybir.dt.float32
BF16 = mybir.dt.bfloat16
I32 = mybir.dt.int32
AF = mybir.ActivationFunctionType
OP = mybir.AluOpType
AX = mybir.AxisListType

D = 1024
T = 2048
NBATCH = 16
NCORES = 8
BPC = NBATCH // NCORES
EC = D // 128
NTT = T // 512
NTC = T // 128
I0 = 2816
IC0 = I0 // 128
HID = 2730
DH = 64
WIN = 512

SHIFT_L0 = 8.0
SHIFT_MHA = 8.0
SHIFT_LOC = 12.0

FP4_MIDS = (0.25, 0.75, 1.25, 1.75, 2.5, 3.5, 5.0)
FP4_INCS = (0.5, 0.5, 0.5, 0.5, 1.0, 1.0, 2.0)

TRACE_ARENA = False
DBG = {}


def _cody_waite_2pi():
    import struct
    two_pi = 6.283185307179586476925287
    def trunc_bits(x, bits):
        u = struct.unpack(">I", struct.pack(">f", np.float32(x)))[0]
        mask = (0xFFFFFFFF << (23 - bits)) & 0xFFFFFFFF
        return struct.unpack(">f", struct.pack(">I", u & mask))[0]
    c1 = trunc_bits(two_pi, 10)
    c2 = trunc_bits(two_pi - c1, 10)
    c3 = float(np.float32(two_pi - c1 - c2))
    return float(np.float32(c1)), float(np.float32(c2)), c3


CW1, CW2, CW3 = _cody_waite_2pi()


def build_consts():
    c = {}
    c["iden"] = np.eye(128, dtype=np.float32)
    c["ones_row"] = np.ones((1, 128), dtype=np.float32)
    c["ones_col"] = np.ones((128, 1), dtype=np.float32)
    c["iota_t"] = np.arange(T, dtype=np.float32).reshape(1, T)
    idx = np.arange(32, dtype=np.float32)
    c["iota_dh"] = np.tile(idx, 4).reshape(1, 128)
    P = np.zeros((128, 128), dtype=np.float32)
    for h in range(2):
        b = h * 64
        for d_ in range(32):
            P[b + d_, b + d_ + 32] = -1.0
            P[b + d_ + 32, b + d_] = 1.0
    c["rot_pi"] = P.T.copy()
    m = np.zeros((8, 128, 512), dtype=np.float32)
    o = (np.arange(8)[:, None, None] * 128 + np.arange(128)[None, :, None])
    i_ = np.arange(512)[None, None, :]
    m[:] = ((i_ >= o - 512) & (i_ <= o)).astype(np.float32)
    c["loc_mask"] = m.reshape(8 * 128, 512)
    e16 = np.zeros((4, 2, 128, 16), dtype=np.float32)
    for g in range(4):
        for s in range(2):
            e16[g, s, 0:64, 8 * s + 2 * g] = 1.0
            e16[g, s, 64:128, 8 * s + 2 * g + 1] = 1.0
    c["e16"] = e16.reshape(4 * 2 * 128, 16)
    eb = np.zeros((4, 2, 16, 128), dtype=np.float32)
    for g in range(4):
        for s in range(2):
            eb[g, s, 8 * s + 2 * g, 0:64] = 1.0
            eb[g, s, 8 * s + 2 * g + 1, 64:128] = 1.0
    c["ebr"] = eb.reshape(4 * 2 * 16, 128)
    return c


class Ev:
    __slots__ = ("sem", "val", "eng", "dma")
    def __init__(self, sem, val, eng, dma):
        self.sem, self.val, self.eng, self.dma = sem, val, eng, dma


class StopBuild(Exception):
    def __init__(self, evs):
        self.evs = evs


class Planner:
    ENGS = ("sp", "gp", "pe", "dve", "act")

    def __init__(self, nc, stack):
        self.nc = nc
        self.prog = {e: [] for e in self.ENGS}
        self.sems = {}
        names = ["s_sp", "s_gp", "s_pe", "s_dve", "s_act"]
        self.dma_groups = {}
        for stream in ("w", "x"):
            for eng in ("sp", "gp"):
                grp = [f"d{stream}{i}_{eng}" for i in range(4)]
                self.dma_groups[(stream, eng)] = grp
                names += grp
        for name in names:
            self.sems[name] = [stack.enter_context(nc.semaphore(name)), 0]
        self.rot = {k: 0 for k in self.dma_groups}
        self.marks = {}
        self.nops = 0

    def _wait(self, eng, ev):
        if ev is None:
            return
        if isinstance(ev, (list, tuple)):
            for e2 in ev:
                self._wait(eng, e2)
            return
        if ev.eng == eng and not ev.dma:
            return
        val = ev.val
        key = (eng, ev.sem)
        if self.marks.get(key, -1) >= val:
            return
        self.marks[key] = val
        sem_h = self.sems[ev.sem][0]
        self.prog[eng].append(lambda e: e.wait_ge(sem_h, val))

    def comp(self, eng, fn, reads=None, sig=False, tag=""):
        assert eng in ("pe", "dve", "act", "gp")
        self._wait(eng, reads)
        self.nops += 1
        if not sig:
            self.prog[eng].append(fn)
            return None
        semname = {"pe": "s_pe", "dve": "s_dve", "act": "s_act",
                   "gp": "s_gp"}[eng]
        sem = self.sems[semname]
        sem[1] += 1
        val = sem[1]
        sem_h = sem[0]
        if eng == "pe":
            def run(e):
                r = fn(e)
                bi = r[-1] if isinstance(r, (list, tuple)) else r
                bi.then_inc(sem_h, 1)
            self.prog[eng].append(run)
        else:
            self.prog[eng].append(fn)
            self.prog[eng].append(lambda e: e.drain().then_inc(sem_h, 1))
        return Ev(semname, val, eng, False)

    def dma(self, eng, fn, reads=None, stream="x", tag=""):
        assert eng in ("sp", "gp")
        self._wait(eng, reads)
        grp = self.dma_groups[(stream, eng)]
        i = self.rot[(stream, eng)]
        self.rot[(stream, eng)] = (i + 1) % len(grp)
        semname = grp[i]
        sem = self.sems[semname]
        sem_h = sem[0]
        prev = sem[1]
        self._wait(eng, Ev(semname, prev, "-", True))
        def run(e):
            bi = fn(e)
            bi.then_inc(sem_h, 16)
        self.prog[eng].append(run)
        sem[1] += 16
        self.nops += 1
        return Ev(semname, sem[1], eng, True)

    def barrier_wait(self, eng, ev):
        self._wait(eng, ev)

    def replay(self, blk):
        prog = self.prog
        blk.sync(lambda e: [fn(e) for fn in prog["sp"]])
        blk.gpsimd(lambda e: [fn(e) for fn in prog["gp"]])
        blk.tensor(lambda e: [fn(e) for fn in prog["pe"]])
        blk.vector(lambda e: [fn(e) for fn in prog["dve"]])
        blk.scalar(lambda e: [fn(e) for fn in prog["act"]])


class Banks:
    def __init__(self, psum):
        self.psum = psum
        self.state = [{"writer": None, "readers": []} for _ in psum]

    def acquire(self, i):
        s = self.state[i]
        deps = list(s["readers"]) if s["readers"] else (
            [s["writer"]] if s["writer"] else [])
        s["readers"] = []
        s["writer"] = None
        return deps

    def wrote(self, i, ev):
        self.state[i]["writer"] = ev

    def read(self, i, ev):
        if ev is not None:
            self.state[i]["readers"].append(ev)


class Region:
    __slots__ = ("arena", "off", "n", "deps", "name")
    def __init__(self, arena, off, n, deps, name):
        self.arena, self.off, self.n, self.deps, self.name = arena, off, n, deps, name

    def f32(self, c0=0, c1=None):
        c1 = self.n if c1 is None else c1
        assert 0 <= c0 <= c1 <= self.n, (self.name, c0, c1, self.n)
        return self.arena[:, self.off + c0:self.off + c1]

    def bf16(self, c0=0, c1=None):
        c1 = 2 * self.n if c1 is None else c1
        assert 0 <= c0 <= c1 <= 2 * self.n, (self.name, c0, c1, self.n)
        v = self.arena[:, self.off:self.off + self.n].bitcast(BF16)
        return v[:, c0:c1]


class Arena:
    def __init__(self, tensor, n):
        self.t = tensor
        self.free_list = [[0, n, []]]
        self.live = {}
        self.hiwater = 0

    def alloc(self, name, n_f32, align=16, side="low"):
        assert name not in self.live, name
        segs = self.free_list if side == "low" else list(reversed(self.free_list))
        for seg in segs:
            if side == "low":
                s = (seg[0] + align - 1) // align * align
                ok = seg[1] - s >= n_f32
            else:
                s = (seg[1] - n_f32) // align * align
                ok = s >= seg[0]
            if ok:
                deps = list(seg[2])
                seg_old = list(seg)
                self.free_list.remove(seg)
                if s > seg_old[0]:
                    self.free_list.append([seg_old[0], s, list(seg_old[2])])
                if s + n_f32 < seg_old[1]:
                    self.free_list.append([s + n_f32, seg_old[1], list(seg_old[2])])
                self.free_list.sort(key=lambda x: x[0])
                for lr in self.live.values():
                    assert s + n_f32 <= lr.off or s >= lr.off + lr.n, \
                        f"OVERLAP {name}@{s}+{n_f32} vs {lr.name}@{lr.off}+{lr.n}"
                r = Region(self.t, s, n_f32, deps, name)
                self.live[name] = r
                if TRACE_ARENA:
                    print(f"ARENA alloc {name}@{s}+{n_f32} ({side})")
                self.hiwater = max(self.hiwater, s + n_f32)
                return r
        raise RuntimeError(
            f"arena OOM for {name} ({n_f32} f32); free="
            f"{[(a, b) for a, b, _ in self.free_list]} live="
            f"{[(r.name, r.off, r.n) for r in self.live.values()]}")

    def free(self, r, events):
        if TRACE_ARENA:
            print(f"ARENA free {r.name}@{r.off}+{r.n}")
        for a, b_, _ in self.free_list:
            assert r.off + r.n <= a or r.off >= b_, \
                f"DOUBLE-FREE {r.name}@{r.off}+{r.n} overlaps free seg ({a},{b_})"
        assert self.live.pop(r.name, None) is not None, r.name
        evs = events if isinstance(events, list) else [events]
        flat = []
        for e in evs:
            if isinstance(e, (list, tuple)):
                flat += [x for x in e if x is not None]
            elif e is not None:
                flat.append(e)
        self.free_list.append([r.off, r.off + r.n, flat])
        self.free_list.sort(key=lambda x: x[0])
        merged = []
        for seg in self.free_list:
            if merged and merged[-1][1] == seg[0]:
                merged[-1][1] = seg[1]
                merged[-1][2] = merged[-1][2] + seg[2]
            else:
                merged.append(seg)
        self.free_list = merged


def kernel(**inputs):
    consts = build_consts()
    x_full = np.ascontiguousarray(inputs["L0_attn_x"], dtype=np.float32)
    nc = bass.Bass("TRN2", target_bir_lowering=False, debug=False)
    build_program(nc)
    in_maps = []
    for c in range(NCORES):
        m = {"x_in": x_full[c * BPC:(c + 1) * BPC]}
        for k, v in inputs.items():
            if k == "L0_attn_x":
                continue
            m[k] = np.ascontiguousarray(v, dtype=np.float32)
        for k, v in consts.items():
            m["c_" + k] = v
        in_maps.append(m)
    res = run_bass_kernel_spmd(nc, in_maps, core_ids=list(range(NCORES)))
    out = np.concatenate([r["out"] for r in res.results], axis=0)
    return out.astype(np.float32)


def build_program(nc: bass.Bass):
    with contextlib.ExitStack() as st:
        _build(nc, st)


def _build(nc: bass.Bass, st):
    x_in = nc.dram_tensor("x_in", [BPC, T, D], F32, kind="ExternalInput").ap()
    out = nc.dram_tensor("out", [BPC, T, D], F32, kind="ExternalOutput").ap()
    cap = {}
    for k, v in build_consts().items():
        cap[k] = nc.dram_tensor("c_" + k, list(v.shape), F32,
                                kind="ExternalInput").ap()
    W = {}
    wspecs = {
        "L0_attn_Wq": [D, D], "L0_attn_bq": [D],
        "L0_attn_Wk": [D, D], "L0_attn_bk": [D],
        "L0_attn_Wv": [D, D], "L0_attn_bv": [D],
        "L0_ffn_w_gate": [I0, D], "L0_ffn_w_up": [I0, D],
        "L0_ffn_w_down": [D, I0],
        "L1_attn_W_qkv": [3 * D, D], "L1_attn_b_qkv": [3 * D],
        "L1_attn_W_out": [D, D], "L1_attn_b_out": [D],
        "L1_ffn_wqkv": [2, D, 1536], "L1_ffn_q_scale": [2, DH],
        "L1_ffn_k_scale": [2, DH], "L1_ffn_wo": [2, 512, D],
        "L1_ffn_ln_g": [2, D], "L1_ffn_ln_b": [2, D],
        "L1_ffn_w1": [2, D, 2 * HID], "L1_ffn_b1": [2, 2 * HID],
        "L1_ffn_w2": [2, HID, D], "L1_ffn_b2": [2, D],
    }
    for k, shp in wspecs.items():
        W[k] = nc.dram_tensor(k, shp, F32, kind="ExternalInput").ap()

    scr = {
        "res": nc.dram_tensor("scr_res", [EC, 128, T], F32, kind="Internal").ap(),
        "att": nc.dram_tensor("scr_att", [EC, 128, T], BF16, kind="Internal").ap(),
        "xn1": nc.dram_tensor("scr_xn1", [EC, 128, T], F32, kind="Internal").ap(),
        "xq": nc.dram_tensor("scr_xq", [EC, 128, T], BF16, kind="Internal").ap(),
        "xn0": nc.dram_tensor("scr_xn0", [EC, 128, T], F32, kind="Internal").ap(),
        "qt1": nc.dram_tensor("scr_qt1", [EC, 128, T], BF16, kind="Internal").ap(),
        "wgq": nc.dram_tensor("scr_wgq", [EC, 128, I0], BF16, kind="Internal").ap(),
        "wuq": nc.dram_tensor("scr_wuq", [EC, 128, I0], BF16, kind="Internal").ap(),
        "wdq": nc.dram_tensor("scr_wdq", [IC0, 128, D], BF16, kind="Internal").ap(),
    }
    if DBG.get("stop"):
        scr["dbg"] = nc.dram_tensor("dbg", [EC, 128, T], F32,
                                    kind="ExternalOutput").ap()

    def sb(name, shape, dt):
        return st.enter_context(nc.sbuf_tensor(name, shape, dt))

    sbuf = {
        "CONST": sb("CONST", [128, 400], F32),
        "SMALL": sb("SMALL", [128, 1664], F32),
        "CONB": sb("CONB", [128, 5760], BF16),
    }
    rem = nc.sbuf_bytes_remaining
    rem = rem() if callable(rem) else rem
    arena_n = (rem - 64) // 4 // 16 * 16
    sbuf["ARENA"] = sb("ARENA", [128, arena_n], F32)

    p01 = st.enter_context(nc.psum_tensor("pb01", [128, 1024], F32))
    p23 = st.enter_context(nc.psum_tensor("pb23", [128, 1024], F32))
    psum = [p01[:, 0:512], p01[:, 512:1024], p23[:, 0:512], p23[:, 512:1024]]
    psum += [st.enter_context(nc.psum_tensor(f"pb{i}", [128, 512], F32))
             for i in range(4, 7)]
    psum_t = st.enter_context(nc.psum_tensor("pbt", [128, 1024], BF16))

    p = Planner(nc, st)
    banks = Banks(psum + [psum_t])
    arena = Arena(sbuf["ARENA"][:, :], arena_n)

    env = dict(nc=nc, p=p, banks=banks, sbuf=sbuf, psum=psum, psum_t=psum_t,
               psum_pair=[p01, p23],
               arena=arena, x_in=x_in, out=out, cap=cap, W=W, scr=scr)
    _forward(env)

    with nc.Block() as blk:
        p.replay(blk)


def _bcast_row(env, bank_i, row_ap, reads, n=512):
    p, banks = env["p"], env["banks"]
    deps = banks.acquire(bank_i)
    ones_row = env["ones_row"]
    ev = p.comp("pe", lambda e: e.matmul(
        env["psum"][bank_i][:, 0:n], ones_row, row_ap, start=True, stop=True),
        reads=[reads, env["ev_const"]] + deps, sig=True)
    banks.wrote(bank_i, ev)
    return ev


def _wt_transpose(env, nat_getter, nchunks_out, nchunks_in, wt_region,
                  ev_nat, tag):
    p, banks, psum_t = env["p"], env["banks"], env["psum_t"]
    out_evs = []
    for c in range(nchunks_out):
        bk = 7
        deps = banks.acquire(bk)
        nat = nat_getter(c)
        ev_tp = p.comp("pe", lambda e, nat=nat: [
            e.transpose(psum_t[:, j * 128:(j + 1) * 128],
                        nat[:, j * 128:(j + 1) * 128], env["idenb"])
            for j in range(min(8, nchunks_in))],
            reads=[ev_nat, env["ev_constb"]] + deps, sig=True)
        banks.wrote(bk, ev_tp)
        now = nchunks_in if nchunks_in <= 8 else 8
        ncols_out = nchunks_out * 128
        ev_ev = p.comp("dve", lambda e, c=c, now=now: e.tensor_copy(
            wt_region.bf16(0, now * ncols_out).rearrange(
                "p (j m) -> p j m", m=ncols_out)[:, :, c * 128:(c + 1) * 128],
            psum_t[:, 0:now * 128].rearrange("p (j m) -> p j m", m=128)),
            reads=[ev_tp] + wt_region.deps, sig=True)
        banks.read(bk, ev_ev)
        out_evs.append(ev_ev)
    return out_evs


def _forward(env):
    p = env["p"]
    sbuf = env["sbuf"]
    cap = env["cap"]
    CONST = sbuf["CONST"]
    CONB = sbuf["CONB"]

    ev1 = p.dma("sp", lambda e: e.dma_start(out=CONST[:, 0:128], in_=cap["iden"]))
    ev2 = p.dma("sp", lambda e: e.dma_start(out=CONST[:, 128:256], in_=cap["rot_pi"]))
    ev3 = p.dma("sp", lambda e: e.dma_start(out=CONST[:, 256:257], in_=cap["ones_col"]))
    ev4 = p.dma("sp", lambda e: e.dma_start(out=CONST[0:1, 257:385], in_=cap["ones_row"]))
    ev5 = p.dma("gp", lambda e: e.dma_start(out=CONB[:, 4096:4224], in_=cap["iden"]))
    ev5b = p.dma("gp", lambda e: e.dma_start(out=CONB[:, 4224:4225], in_=cap["ones_col"]))
    ev5c = p.dma("gp", lambda e: e.dma_start(out=CONB[0:1, 4225:4353], in_=cap["ones_row"]))
    ev6 = p.dma("gp", lambda e: e.dma_start(
        out=CONB[:, 0:4096].rearrange("p (c n) -> p c n", n=512),
        in_=cap["loc_mask"].rearrange("(c p) n -> p c n", p=128)))
    ev_e16 = p.dma("gp", lambda e: e.dma_start(
        out=CONB[:, 4480:4608].rearrange("p (g m) -> p g m", m=16),
        in_=cap["e16"].rearrange("(g p) m -> p g m", p=128)))
    ev_rpi = p.dma("gp", lambda e: e.dma_start(
        out=CONB[:, 4608:4736], in_=cap["rot_pi"]))
    env["ev_const"] = [ev1, ev2, ev3, ev4]
    env["ev_constb"] = [ev5, ev5b, ev5c, ev6, ev_e16, ev_rpi]
    env["iden"] = CONST[:, 0:128]
    env["rot_pi"] = CONST[:, 128:256]
    env["ones_col"] = CONST[:, 256:257]
    env["ones_row"] = CONST[0:1, 257:385]
    cells = {1e-5: 1656, -SHIFT_L0: 1657, -SHIFT_MHA: 1658, -SHIFT_LOC: 1659}
    SMALL = env["sbuf"]["SMALL"]
    ev_cells = p.comp("dve", lambda e: [
        e.memset(SMALL[:, col:col + 1], val) for val, col in cells.items()],
        sig=True)
    env["cell"] = {v: SMALL[:, c:c + 1] for v, c in cells.items()}
    env["rowchain"] = []
    env["ev_cells"] = ev_cells
    env["idenb"] = CONB[:, 4096:4224]
    env["ones_col_b"] = CONB[:, 4224:4225]
    env["ones_row_b"] = CONB[0:1, 4225:4353]
    env["e16_b"] = CONB[:, 4480:4608]
    env["rot_pi_b"] = CONB[:, 4608:4736]
    env["loc_mask"] = CONB[:, 0:4096]

    env["ebr_f"] = CONB[:, 4736:5760]
    env["ev_gconst"] = [p.dma("gp", lambda e: e.dma_start(
        out=CONB[0:16, 4736:5760].rearrange("p (g m) -> p g m", m=128),
        in_=cap["ebr"].rearrange("(g p) m -> p g m", p=16)))]

    store_evs = []
    nb = 1 if DBG.get("stop") else BPC
    try:
        for b in range(nb):
            ev = _batch_forward(env, b)
            store_evs += ev
    except StopBuild as sb:
        store_evs = sb.evs
    for ev in store_evs:
        p.barrier_wait("gp", ev)


def _batch_forward(env, b):
    arena, p = env["arena"], env["p"]
    res_r, ev_res = _load_x(env, b)
    ev_res = _layer0(env, b, res_r, ev_res)
    if DBG.get("stop") == "l0":
        evs = _dump_dbg(env, res_r, ev_res)
        return evs
    ev_res = _layer1(env, b, res_r, ev_res)
    evs = _store_out(env, b, env["final_r"], ev_res)
    arena.free(env["final_r"], evs)
    return evs


def _dump_xint(env, xf_ap, evs):
    p = env["p"]
    out = []
    for ec in range(EC):
        out.append(p.dma("sp", lambda e, ec=ec: e.dma_start(
            out=env["scr"]["dbg"][ec], in_=xf_ap(ec, 0, T)), reads=evs))
    return out


def _dump_dbg(env, res_r, ev_res):
    p = env["p"]
    evs = []
    for ec in range(EC):
        evs.append(p.dma("sp", lambda e, ec=ec: e.dma_start(
            out=env["scr"]["dbg"][ec], in_=res_r.f32(ec * T, (ec + 1) * T)),
            reads=ev_res))
    return evs


def _load_x(env, b):
    p, banks, arena = env["p"], env["banks"], env["arena"]
    x_in, psum = env["x_in"], env["psum"]
    res_r = arena.alloc("RES", EC * T)
    stg_r = arena.alloc("io_stg", 4 * 1024, side="high")
    slot_reader = {s: stg_r.deps + res_r.deps for s in range(4)}
    evs = []
    for tc in range(NTC):
        slot = tc % 4
        stg = stg_r.f32(slot * 1024, (slot + 1) * 1024)
        ev_ld = p.dma("sp", lambda e, tc=tc, stg=stg: e.dma_start(
            out=stg, in_=x_in[b, tc * 128:(tc + 1) * 128, :]),
            reads=slot_reader.get(slot))
        pe_evs = []
        for half in range(2):
            bk = (tc % 2) * 2 + half
            deps = banks.acquire(bk)
            ev_mm = p.comp("pe", lambda e, stg=stg, bk=bk, half=half: [
                e.transpose(env["psum"][bk][:, j * 128:(j + 1) * 128],
                            stg[:, (half * 4 + j) * 128:(half * 4 + j + 1) * 128],
                            env["iden"])
                for j in range(4)], reads=[ev_ld, env["ev_const"]] + deps,
                sig=True)
            banks.wrote(bk, ev_mm)
            pe_evs.append(ev_mm)
            ev_ev = p.comp("dve", lambda e, bk=bk, half=half, tc=tc: [
                e.tensor_copy(
                    res_r.f32((half * 4 + j) * T + tc * 128,
                              (half * 4 + j) * T + (tc + 1) * 128),
                    env["psum"][bk][:, j * 128:(j + 1) * 128])
                for j in range(4)], reads=[ev_mm], sig=True)
            banks.read(bk, ev_ev)
            evs.append(ev_ev)
        slot_reader[slot] = pe_evs
    arena.free(stg_r, [e for v in slot_reader.values()
                       for e in (v if isinstance(v, list) else [v])])
    return res_r, evs


def _store_out(env, b, src_r, ev_src):
    p, banks, arena = env["p"], env["banks"], env["arena"]
    out, psum = env["out"], env["psum"]
    stg_r = arena.alloc("io_stg_out", 4 * 1024, side="high")
    store_evs = []
    slot_store = {s: stg_r.deps for s in range(4)}
    for tc in range(NTC):
        slot = tc % 4
        stg = stg_r.f32(slot * 1024, (slot + 1) * 1024)
        evs_ev = []
        for half in range(2):
            bk = (tc % 2) * 2 + half
            deps = banks.acquire(bk)
            ev_mm = p.comp("pe", lambda e, bk=bk, half=half, tc=tc: [
                e.transpose(env["psum"][bk][:, j * 128:(j + 1) * 128],
                            src_r.f32((half * 4 + j) * T + tc * 128,
                                      (half * 4 + j) * T + (tc + 1) * 128),
                            env["iden"])
                for j in range(4)],
                reads=[ev_src] + deps + [env["ev_const"]], sig=True)
            banks.wrote(bk, ev_mm)
            dep_st = slot_store.get(slot)
            ev_ev = p.comp("dve", lambda e, bk=bk, half=half, stg=stg: [
                e.tensor_copy(stg[:, (half * 4 + j) * 128:(half * 4 + j + 1) * 128],
                              env["psum"][bk][:, j * 128:(j + 1) * 128])
                for j in range(4)],
                reads=[ev_mm, dep_st], sig=True)
            banks.read(bk, ev_ev)
            evs_ev.append(ev_ev)
        ev_st = p.dma("sp", lambda e, stg=stg, tc=tc: e.dma_start(
            out=out[b, tc * 128:(tc + 1) * 128, :], in_=stg),
            reads=evs_ev)
        slot_store[slot] = ev_st
        store_evs.append(ev_st)
    arena.free(stg_r, list(store_evs))
    return store_evs


def _layernorm_fast(env, src3, outs3, tag):
    p, banks, arena = env["p"], env["banks"], env["arena"]
    psum = env["psum"]
    SMALL = env["sbuf"]["SMALL"]
    out_evs = []
    sc_r = arena.alloc(f"lnf_{tag}", 4096 + 512, side="high")
    xb3 = sc_r.bf16(0, EC * 512).rearrange("p (c m) -> p c m", m=512)
    x23 = sc_r.bf16(EC * 512, 2 * EC * 512).rearrange(
        "p (c m) -> p c m", m=512)
    t3h = sc_r.f32(0, 2048).rearrange("p (c m) -> p c m", m=512)
    var = sc_r.f32(4096, 4096 + 512)[0:1, 0:512]
    prev_mm = list(sc_r.deps)
    prev_t = []
    for tt in range(NTT):
        src, ev_src = src3(tt)
        deps3 = banks.acquire(3)
        deps4 = banks.acquire(4)
        ev_cp = p.comp("dve", lambda e, src=src: e.tensor_copy(xb3, src),
                       reads=[ev_src] + prev_mm, sig=True)
        ev_sq = p.comp("act", lambda e, src=src: e.activation(
            x23, src, AF.Square), reads=[ev_src] + prev_mm, sig=True)
        ev_mm = p.comp("pe", lambda e: [
            mm for ec in range(EC) for mm in (
                e.matmul(psum[3][0:1, 0:512], env["ones_col_b"],
                         xb3[:, ec, :], start=(ec == 0), stop=(ec == EC - 1)),
                e.matmul(psum[4][0:1, 0:512], env["ones_col_b"],
                         x23[:, ec, :], start=(ec == 0), stop=(ec == EC - 1)))],
            reads=[ev_cp, ev_sq, env["ev_constb"]] + deps3 + deps4, sig=True)
        prev_mm = [ev_mm]
        banks.wrote(3, ev_mm)
        banks.wrote(4, ev_mm)
        mu = SMALL[0:1, 0:512]
        ev_mu = p.comp("dve", lambda e: e.tensor_scalar_mul(
            mu, psum[3][0:1, 0:512], 1.0 / D),
            reads=[ev_mm] + env["rowchain"][-4:], sig=True)
        mu2 = SMALL[0:1, 512:1024]
        ev_mu2 = p.comp("dve", lambda e: e.tensor_mul(mu2, mu, mu),
                        reads=[ev_mu], sig=True)
        ev_var = p.comp("dve", lambda e: e.scalar_tensor_tensor(
            var, psum[4][0:1, 0:512], 1.0 / D, mu2, OP.mult, OP.subtract),
            reads=[ev_mu2, ev_mm], sig=True)
        banks.read(4, ev_var)
        sd = SMALL[0:1, 512:1024]
        ev_sd = p.comp("act", lambda e: e.activation(
            sd, var[0:1, :], AF.Sqrt, bias=env["cell"][1e-5][0:1, :]),
            reads=[ev_var, ev_mu2, env["ev_cells"]], sig=True)
        rs = SMALL[0:1, 0:512]
        ev_rs = p.comp("dve", lambda e: e.reciprocal(rs, sd),
                       reads=[ev_sd, ev_var, ev_mu], sig=True)
        nm = SMALL[0:1, 512:1024]
        ev_nm = p.comp("dve", lambda e: e.scalar_tensor_tensor(
            nm, psum[3][0:1, 0:512], -1.0 / D, rs, OP.mult, OP.mult),
            reads=[ev_rs, ev_mm, ev_sd], sig=True)
        banks.read(3, ev_nm)
        ev_a = _bcast_row(env, 5, rs, [ev_rs, ev_nm])
        ev_b = _bcast_row(env, 6, nm, [ev_nm])
        env["rowchain"] = [ev_a, ev_b]
        a3 = psum[5][:, 0:512].unsqueeze(1).broadcast_to([128, 4, 512])
        b3 = psum[6][:, 0:512].unsqueeze(1).broadcast_to([128, 4, 512])
        kind, apfn = outs3[0]
        dst = apfn(tt)
        wevs = []
        for hh in range(2):
            ev_t = p.comp("dve", lambda e, src=src, hh=hh: e.tensor_tensor(
                t3h, src[:, hh * 4:(hh + 1) * 4, :], a3, OP.mult),
                reads=[ev_src, ev_a, ev_var, ev_mm] + prev_t, sig=True)
            banks.read(5, ev_t)
            ev_o = p.comp("dve", lambda e, dst=dst, hh=hh: e.tensor_tensor(
                dst[:, hh * 4:(hh + 1) * 4, :], t3h, b3, OP.add),
                reads=[ev_t, ev_b], sig=True)
            banks.read(6, ev_o)
            prev_t = [ev_o]
            wevs.append(ev_o)
            for kind2, apfn2 in outs3[1:]:
                dst2 = apfn2(tt)
                ev_o2 = p.comp("act", lambda e, dst2=dst2, dst=dst, hh=hh:
                               e.copy(dst2[:, hh * 4:(hh + 1) * 4, :],
                                      dst[:, hh * 4:(hh + 1) * 4, :]),
                               reads=[ev_o], sig=True)
                wevs.append(ev_o2)
        out_evs += wevs
    arena.free(sc_r, out_evs[-2 * len(outs3):] + prev_mm)
    return out_evs


def _layernorm(env, src_get, outs, tag, gamma_beta=None, gb_reads=None):
    p, banks, arena = env["p"], env["banks"], env["arena"]
    psum = env["psum"]
    SMALL = env["sbuf"]["SMALL"]
    out_evs = []
    sc_r = arena.alloc(f"ln_{tag}", EC * 512 + 1024, side="high")
    prev_mm = list(sc_r.deps)
    for tt in range(NTT):
        c0 = tt * 512
        deps3 = banks.acquire(3)
        deps4 = banks.acquire(4)
        cp_evs = []
        for ec in range(EC):
            src, ev_src = src_get(ec)
            xb_t = sc_r.bf16(ec * 1024, ec * 1024 + 512)
            x2_t = sc_r.bf16(ec * 1024 + 512, (ec + 1) * 1024)
            ev_cp = p.comp("dve", lambda e, src=src, c0=c0, xb_t=xb_t:
                           e.tensor_copy(xb_t, src[:, c0:c0 + 512]),
                           reads=[ev_src] + prev_mm, sig=True)
            ev_sq = p.comp("act", lambda e, src=src, c0=c0, x2_t=x2_t:
                           e.activation(x2_t, src[:, c0:c0 + 512], AF.Square),
                           reads=[ev_src] + prev_mm, sig=True)
            cp_evs += [ev_cp, ev_sq]
        ev_mm = None
        for ec in range(EC):
            xb_t = sc_r.bf16(ec * 1024, ec * 1024 + 512)
            x2_t = sc_r.bf16(ec * 1024 + 512, (ec + 1) * 1024)
            ev_mm = p.comp("pe", lambda e, ec=ec, xb_t=xb_t, x2_t=x2_t: [
                e.matmul(psum[3][0:1, 0:512], env["ones_col_b"], xb_t,
                         start=(ec == 0), stop=(ec == EC - 1)),
                e.matmul(psum[4][0:1, 0:512], env["ones_col_b"], x2_t,
                         start=(ec == 0), stop=(ec == EC - 1))],
                reads=cp_evs + [env["ev_constb"]] +
                      (deps3 + deps4 if ec == 0 else []),
                sig=(ec == EC - 1))
        prev_mm = [ev_mm]
        banks.wrote(3, ev_mm)
        banks.wrote(4, ev_mm)
        mu = SMALL[0:1, 0:512]
        ev_mu = p.comp("dve", lambda e: e.tensor_scalar_mul(
            mu, psum[3][0:1, 0:512], 1.0 / D),
            reads=[ev_mm] + env["rowchain"][-4:], sig=True)
        banks.read(3, ev_mu)
        mu2 = SMALL[0:1, 512:1024]
        ev_mu2 = p.comp("dve", lambda e: e.tensor_mul(mu2, mu, mu),
                        reads=[ev_mu], sig=True)
        var = sc_r.f32(EC * 512 + 512, EC * 512 + 1024)[0:1, 0:512]
        ev_var = p.comp("dve", lambda e: e.scalar_tensor_tensor(
            var, psum[4][0:1, 0:512], 1.0 / D, mu2, OP.mult, OP.subtract),
            reads=[ev_mu2, ev_mm], sig=True)
        banks.read(4, ev_var)
        sd = SMALL[0:1, 512:1024]
        ev_sd = p.comp("act", lambda e: e.activation(
            sd, var[0:1, :], AF.Sqrt, bias=env["cell"][1e-5][0:1, :]),
            reads=[ev_var, ev_mu2, env["ev_cells"]], sig=True)
        rs = SMALL[0:1, 0:512]
        ev_rs = p.comp("dve", lambda e: e.reciprocal(rs, sd),
                       reads=[ev_sd, ev_var, ev_mu], sig=True)
        nm = SMALL[0:1, 512:1024]
        ev_nm = p.comp("dve", lambda e: e.scalar_tensor_tensor(
            nm, psum[3][0:1, 0:512], -1.0 / D, rs, OP.mult, OP.mult),
            reads=[ev_rs, ev_mm, ev_sd], sig=True)
        banks.read(3, ev_nm)
        ev_a = _bcast_row(env, 5, rs, [ev_rs, ev_nm])
        ev_b = _bcast_row(env, 6, nm, [ev_nm])
        env["rowchain"] = [ev_a, ev_b]
        t_t = sc_r.f32(EC * 512, EC * 512 + 512)
        u_t = sc_r.f32(EC * 512 + 512, EC * 512 + 1024)
        last_t_reader = None
        last_u_reader = None
        for ec in range(EC):
            src, ev_src = src_get(ec)
            ev_t = p.comp("dve", lambda e, src=src, c0=c0: e.tensor_tensor(
                t_t, src[:, c0:c0 + 512], psum[5][:, 0:512], OP.mult),
                reads=[ev_src, ev_a, ev_var] +
                      ([last_t_reader] if last_t_reader else []), sig=True)
            banks.read(5, ev_t)
            kind, apfn = outs[0]
            dst = apfn(ec, tt)
            if gamma_beta is None:
                ev_o = p.comp("dve", lambda e, dst=dst: e.tensor_tensor(
                    dst, t_t, psum[6][:, 0:512], OP.add),
                    reads=[ev_t, ev_b], sig=True)
                banks.read(6, ev_o)
                last_t_reader = ev_o
            else:
                g_ap, b_ap = gamma_beta(ec)
                ev_u = p.comp("dve", lambda e: e.tensor_tensor(
                    u_t, t_t, psum[6][:, 0:512], OP.add),
                    reads=[ev_t, ev_b] +
                          ([last_u_reader] if last_u_reader else []), sig=True)
                banks.read(6, ev_u)
                ev_o = p.comp("act", lambda e, dst=dst, g_ap=g_ap,
                              b_ap=b_ap: e.activation(
                    dst, u_t, AF.Identity, bias=b_ap, scale=g_ap),
                    reads=[ev_u] + (gb_reads or []), sig=True)
                last_t_reader = ev_u
                last_u_reader = ev_o
            wevs = [ev_o]
            for kind2, apfn2 in outs[1:]:
                dst2 = apfn2(ec, tt)
                ev_o2 = p.comp("act", lambda e, dst2=dst2, dst=dst: e.copy(
                    dst2, dst), reads=[ev_o], sig=True)
                wevs.append(ev_o2)
            out_evs += wevs
    arena.free(sc_r, out_evs[-2 * len(outs) * EC:] + prev_mm)
    return out_evs


def _ladder(env, eng, q_ap, axn_ap, reads):
    p = env["p"]
    tmp = env["_lad_tmp"]
    ev = p.comp(eng, lambda e: e.tensor_single_scalar(
        q_ap, axn_ap, 0.25, OP.is_gt), reads=reads, sig=True)
    if eng == "gp":
        for mid, inc in ((0.75, 1.0), (1.25, 1.0), (1.75, 1.0),
                         (2.5, 2.0), (3.5, 2.0), (5.0, 4.0)):
            ev = p.comp(eng, lambda e, mid=mid, inc=inc: e.tensor_scalar(
                tmp, axn_ap, mid, inc, OP.is_gt, OP.mult),
                reads=[ev], sig=True)
            ev = p.comp(eng, lambda e: e.tensor_tensor(
                q_ap, q_ap, tmp, OP.add), reads=[ev], sig=True)
        return ev
    for mid in (0.75, 1.25, 1.75):
        ev = p.comp(eng, lambda e, mid=mid: e.scalar_tensor_tensor(
            q_ap, axn_ap, mid, q_ap, OP.is_gt, OP.add), reads=[ev], sig=True)
    evb = p.comp(eng, lambda e: e.tensor_single_scalar(
        tmp, axn_ap, 2.5, OP.is_gt), reads=[ev], sig=True)
    evb = p.comp(eng, lambda e: e.scalar_tensor_tensor(
        tmp, axn_ap, 3.5, tmp, OP.is_gt, OP.add), reads=[evb], sig=True)
    ev = p.comp(eng, lambda e: e.scalar_tensor_tensor(
        q_ap, tmp, 2.0, q_ap, OP.mult, OP.add), reads=[evb], sig=True)
    evc = p.comp(eng, lambda e: e.tensor_scalar(
        tmp, axn_ap, 5.0, 4.0, OP.is_gt, OP.mult), reads=[ev], sig=True)
    ev = p.comp(eng, lambda e: e.tensor_tensor(
        q_ap, q_ap, tmp, OP.add), reads=[evc], sig=True)
    return ev


def _quant_block_free(env, src_ap, n_cols, out_bf16_ap, scratch_r, reads,
                      eng="dve", src_is_f32=True):
    p = env["p"]
    nb = n_cols // 16
    nbq = (nb + 15) // 16 * 16
    sc = scratch_r.f32(0, nb)
    rcp = scratch_r.f32(nbq, nbq + nb)
    o = 2 * nbq
    if src_is_f32:
        q_ap = scratch_r.f32(o, o + n_cols)
        axn = scratch_r.f32(o + n_cols, o + 2 * n_cols)
        tmp = scratch_r.f32(o + 2 * n_cols, o + 3 * n_cols)
        sg = scratch_r.f32(o + 3 * n_cols, o + 4 * n_cols)
    else:
        q_ap = scratch_r.bf16(2 * o, 2 * o + n_cols)
        axn = scratch_r.bf16(2 * o + n_cols, 2 * o + 2 * n_cols)
        tmp = scratch_r.bf16(2 * o + 2 * n_cols, 2 * o + 3 * n_cols)
        sg = scratch_r.bf16(2 * o + 3 * n_cols, 2 * o + 4 * n_cols)
    src3 = src_ap.rearrange("p (b k) -> p b k", k=16)
    ev1 = p.comp("dve", lambda e: e.tensor_reduce(
        sc, src3, AX.X, OP.max, apply_absolute_value=True),
        reads=reads + scratch_r.deps, sig=True)
    ev2 = p.comp(eng, lambda e: e.tensor_scalar(
        sc, sc, 1.0 / 6.0, 1e-12, OP.mult, OP.max), reads=[ev1], sig=True)
    ev3 = p.comp("dve", lambda e: e.reciprocal(rcp, sc), reads=[ev2], sig=True)
    rb = rcp.unsqueeze(2).broadcast_to([128, nb, 16])
    ev_ax = p.comp("act", lambda e: e.activation(tmp, src_ap, AF.Abs),
                   reads=reads + scratch_r.deps, sig=True)
    ev_sg = p.comp("act", lambda e: e.activation(sg, src_ap, AF.Sign),
                   reads=[ev_ax], sig=True)
    if src_is_f32 and n_cols % 32 == 0:
        hb = nb // 2
        hc = n_cols // 2
        halves = [(slice(0, hb), slice(0, hc)), (slice(hb, nb),
                                                 slice(hc, n_cols))]

        def st2(fn, rds):
            return p.comp(eng, lambda e: [fn(e, b_, c_)
                                          for b_, c_ in halves],
                          reads=rds, sig=True)

        ev = st2(lambda e, b_, c_: e.tensor_tensor(
            axn[:, c_].rearrange("p (b k) -> p b k", k=16),
            tmp[:, c_].rearrange("p (b k) -> p b k", k=16),
            rcp[:, b_].unsqueeze(2).broadcast_to([128, hb, 16]), OP.mult),
            [ev3, ev_ax] + reads)
        ev = st2(lambda e, b_, c_: e.tensor_single_scalar(
            q_ap[:, c_], axn[:, c_], 0.25, OP.is_gt), [ev])
        for mid in (0.75, 1.25, 1.75):
            ev = st2(lambda e, b_, c_, mid=mid: e.scalar_tensor_tensor(
                q_ap[:, c_], axn[:, c_], mid, q_ap[:, c_], OP.is_gt, OP.add),
                [ev])
        ev = st2(lambda e, b_, c_: e.tensor_single_scalar(
            tmp[:, c_], axn[:, c_], 2.5, OP.is_gt), [ev])
        ev = st2(lambda e, b_, c_: e.scalar_tensor_tensor(
            tmp[:, c_], axn[:, c_], 3.5, tmp[:, c_], OP.is_gt, OP.add), [ev])
        ev = st2(lambda e, b_, c_: e.scalar_tensor_tensor(
            q_ap[:, c_], tmp[:, c_], 2.0, q_ap[:, c_], OP.mult, OP.add),
            [ev])
        ev = st2(lambda e, b_, c_: e.tensor_scalar(
            tmp[:, c_], axn[:, c_], 5.0, 4.0, OP.is_gt, OP.mult), [ev])
        ev = st2(lambda e, b_, c_: e.tensor_tensor(
            q_ap[:, c_], q_ap[:, c_], tmp[:, c_], OP.add), [ev])
        ev = st2(lambda e, b_, c_: e.scalar_tensor_tensor(
            q_ap[:, c_], q_ap[:, c_], 0.5, sg[:, c_], OP.mult, OP.mult),
            [ev, ev_sg])
        ev7 = st2(lambda e, b_, c_: e.tensor_tensor(
            out_bf16_ap[:, c_].rearrange("p (b k) -> p b k", k=16),
            q_ap[:, c_].rearrange("p (b k) -> p b k", k=16),
            sc[:, b_].unsqueeze(2).broadcast_to([128, hb, 16]), OP.mult),
            [ev])
        return ev7
    ev4 = p.comp(eng, lambda e: e.tensor_tensor(
        axn.rearrange("p (b k) -> p b k", k=16),
        tmp.rearrange("p (b k) -> p b k", k=16), rb, OP.mult),
        reads=[ev3, ev_ax] + reads, sig=True)
    env["_lad_tmp"] = tmp
    ev5 = _ladder(env, eng, q_ap, axn, [ev4])
    ev6 = p.comp(eng, lambda e: e.scalar_tensor_tensor(
        q_ap, q_ap, 0.5, sg, OP.mult, OP.mult), reads=[ev5, ev_sg],
        sig=True)
    sb = sc.unsqueeze(2).broadcast_to([128, nb, 16])
    ev7 = p.comp(eng, lambda e: e.tensor_tensor(
        out_bf16_ap.rearrange("p (b k) -> p b k", k=16),
        q_ap.rearrange("p (b k) -> p b k", k=16), sb, OP.mult),
        reads=[ev6], sig=True)
    return ev7


def _prep_ffn0_weights(env):
    p, banks, arena = env["p"], env["banks"], env["arena"]
    W, scr, psum_t = env["W"], env["scr"], env["psum_t"]
    jobs = [
        ("L0_ffn_w_gate", I0 // 128, D, scr["wgq"]),
        ("L0_ffn_w_up", I0 // 128, D, scr["wuq"]),
        ("L0_ffn_w_down", D // 128, I0, scr["wdq"]),
    ]
    nat_r = arena.alloc("wp_nat", 2 * I0, side="high")
    qs_r = arena.alloc("wp_scratch", 4 * I0 + 768, side="high")
    wq_r = arena.alloc("wp_q", I0, side="high")
    strip_r = arena.alloc("wp_strip", I0, side="high")
    slot_ev = {0: nat_r.deps, 1: nat_r.deps}
    q_ev = {0: wq_r.deps + qs_r.deps, 1: wq_r.deps + qs_r.deps}
    strip_ev = {0: strip_r.deps, 1: strip_r.deps}
    evs = []
    for wname, nout, nin, dst in jobs:
        njc = nin // 128
        for c in range(nout):
            sl = c % 2
            nat = nat_r.f32(sl * I0, sl * I0 + nin)
            ev_ld = p.dma("sp", lambda e, wname=wname, c=c, nat=nat:
                          e.dma_start(out=nat, in_=W[wname][c * 128:(c + 1) * 128, :]),
                          reads=slot_ev[sl], stream="w")
            wq = wq_r.bf16(sl * I0, sl * I0 + nin)
            ev_q = _quant_block_free(env, nat, nin, wq, qs_r,
                                     [ev_ld] + q_ev[sl],
)
            slot_ev[sl] = [ev_q]
            q_ev[sl] = []
            strip = strip_r.bf16(sl * I0, sl * I0 + nin)
            tevs = []
            for jb in range(0, njc, 8):
                jn = min(8, njc - jb)
                bk = 7
                deps = banks.acquire(bk)
                ev_tp = p.comp("pe", lambda e, wq=wq, jb=jb, jn=jn: [
                    e.transpose(psum_t[:, j * 128:(j + 1) * 128],
                                wq[:, (jb + j) * 128:(jb + j + 1) * 128],
                                env["idenb"])
                    for j in range(jn)],
                    reads=[ev_q, env["ev_constb"]] + deps + strip_ev[sl],
                    sig=True)
                banks.wrote(bk, ev_tp)
                ev_ev = p.comp("dve", lambda e, strip=strip, jb=jb, jn=jn:
                               e.tensor_copy(
                    strip[:, jb * 128:(jb + jn) * 128],
                    psum_t[:, 0:jn * 128]), reads=[ev_tp], sig=True)
                banks.read(bk, ev_ev)
                tevs.append(ev_ev)
            ev_st = p.dma("sp", lambda e, strip=strip, dst=dst, c=c, njc=njc:
                          e.dma_start(
                out=dst[:, :, c * 128:(c + 1) * 128].rearrange(
                    "j p m -> p j m"),
                in_=strip.rearrange("p (j m) -> p j m", m=128)),
                reads=tevs, stream="w")
            strip_ev[sl] = [ev_st]
            q_ev[sl] = q_ev[sl] + tevs
            evs.append(ev_st)
    arena.free(nat_r, [v for vv in slot_ev.values() for v in vv])
    arena.free(qs_r, evs[-2:])
    arena.free(wq_r, evs[-2:])
    arena.free(strip_r, evs)
    env["ev_wprep"] = evs


def _cast_load_weight_T(env, wname, nout, nin, tag, reads=None):
    p, banks, arena, psum_t = env["p"], env["banks"], env["arena"], env["psum_t"]
    W = env["W"]
    njc = nin // 128
    ncols = nout * 128
    nh = min(nout, 4)
    nat_r = arena.alloc(f"wnat_{tag}", (nh * nin) // 2, side="high")
    wt_r = arena.alloc(f"wt_{tag}", (njc * ncols) // 2, side="high")
    evs = []
    ev_ld_prev = list(nat_r.deps)
    for cb in range(0, nout, nh):
        cn = min(nh, nout - cb)
        ev_ld = p.dma("gp", lambda e, cb=cb, cn=cn: e.dma_start(
            out=nat_r.bf16(0, cn * nin).rearrange("p (c m) -> p c m", m=nin),
            in_=W[wname][cb * 128:(cb + cn) * 128, :].rearrange(
                "(c p) m -> p c m", p=128)),
            reads=ev_ld_prev + (reads or []), stream="w")
        ev_ld_prev = []
        for ci in range(cn):
            c = cb + ci
            bk = 7
            deps = banks.acquire(bk)
            nat = nat_r.bf16(ci * nin, (ci + 1) * nin)
            for jb in range(0, njc, 8):
                jn = min(8, njc - jb)
                ev_tp = p.comp("pe", lambda e, nat=nat, jb=jb, jn=jn: [
                    e.transpose(psum_t[:, j * 128:(j + 1) * 128],
                                nat[:, (jb + j) * 128:(jb + j + 1) * 128],
                                env["idenb"])
                    for j in range(jn)],
                    reads=[ev_ld, env["ev_constb"]] + deps + wt_r.deps,
                    sig=True)
                banks.wrote(bk, ev_tp)
                ev_ld_prev.append(ev_tp)
                ev_ev = p.comp("dve", lambda e, c=c, jb=jb, jn=jn:
                               e.tensor_copy(
                    wt_r.bf16(0, njc * ncols).rearrange(
                        "p (j m) -> p j m", m=ncols)[:, jb:jb + jn,
                                                     c * 128:(c + 1) * 128],
                    psum_t[:, 0:jn * 128].rearrange("p (j m) -> p j m",
                                                    m=128)),
                    reads=[ev_tp], sig=True)
                banks.read(bk, ev_ev)
                evs.append(ev_ev)
    arena.free(nat_r, evs[-1:] + ev_ld_prev)
    return wt_r, evs


def _layer0(env, b, res_r, ev_res):
    p, banks, arena = env["p"], env["banks"], env["arena"]
    psum, scr, W = env["psum"], env["scr"], env["W"]
    SMALL = env["sbuf"]["SMALL"]

    ev_b1 = p.dma("sp", lambda e: e.dma_start(
        out=SMALL[:, 1024:1032], in_=W["L0_attn_bq"].rearrange("(c p) -> p c", p=128),
        allow_slow_non_contiguous=True), reads=ev_res)
    ev_b2 = p.dma("sp", lambda e: e.dma_start(
        out=SMALL[:, 1032:1040], in_=W["L0_attn_bk"].rearrange("(c p) -> p c", p=128),
        allow_slow_non_contiguous=True), reads=ev_res)
    rows_r = arena.alloc("l0_rows", 1024, side="high")
    bv_row = rows_r.f32(0, 1024)[0:1, :]
    ev_b3 = p.dma("sp", lambda e: e.dma_start(
        out=bv_row, in_=W["L0_attn_bv"].rearrange("(a d) -> a d", a=1)),
        reads=rows_r.deps + ev_res)
    env["_ev_bq"] = ev_b1
    env["_ev_bk"] = ev_b2
    env["_ev_bv"] = ev_b3

    xn_r = arena.alloc("xn0", EC * T)
    xnb_r = arena.alloc("xnb0", EC * T // 2)
    ln_evs = _layernorm_fast(
        env,
        lambda tt: (res_r.f32(0, EC * T).rearrange(
            "p (c m) -> p c m", m=T)[:, :, tt * 512:(tt + 1) * 512], ev_res),
        [("f32", lambda tt: xn_r.f32(0, EC * T).rearrange(
            "p (c m) -> p c m", m=T)[:, :, tt * 512:(tt + 1) * 512]),
         ("bf16", lambda tt: xnb_r.bf16(0, EC * T).rearrange(
             "p (c m) -> p c m", m=T)[:, :, tt * 512:(tt + 1) * 512])],
        f"l0b{b}")

    xnsp_evs = []
    for ec in range(EC):
        xnsp_evs.append(p.dma("sp", lambda e, ec=ec: e.dma_start(
            out=scr["xn0"][ec], in_=xn_r.f32(ec * T, (ec + 1) * T)),
            reads=ln_evs))
    arena.free(xn_r, xnsp_evs)

    qt_spill_evs = _project_l0(env, b, xnb_r, ln_evs)
    kt_r, vt_r, proj_evs = env["_l0_kt"], env["_l0_vt"], env["_l0_proj_evs"]
    arena.free(xnb_r, proj_evs)

    res_evs = _attn_l0(env, b, res_r, kt_r, vt_r, qt_spill_evs, ev_res)
    arena.free(kt_r, env["_l0_att_done"])
    arena.free(vt_r, env["_l0_att_done"])
    arena.free(rows_r, env["_l0_att_done"])

    if b == 0:
        _prep_ffn0_weights(env)
    HT = T // 2
    qs_r = arena.alloc("xq_scratch", 4 * HT + 768, side="high")
    xqc_r = arena.alloc("xq_chunk", T // 2, side="high")
    xnc_r = arena.alloc("xn_chunk", 2 * T, side="high")
    ev_spill_prev = xqc_r.deps
    xnc_prev = {0: xnc_r.deps, 1: xnc_r.deps}
    xq_evs = []
    ev7 = ev6 = None
    for ec in range(EC):
        csl = ec % 2
        xnc = xnc_r.f32(csl * T, (csl + 1) * T)
        ev_ld = p.dma("sp", lambda e, ec=ec, xnc=xnc: e.dma_start(
            out=xnc, in_=scr["xn0"][ec]), reads=xnc_prev[csl] + xnsp_evs)
        half_evs = []
        for hh in range(2):
            xt = qs_r.f32(0, HT)
            ev_t = p.comp("dve", lambda e, xnc=xnc, hh=hh: e.transpose(
                xt, xnc[:, hh * HT:(hh + 1) * HT]),
                reads=[ev_ld] + qs_r.deps +
                      ([ev7] if ev7 else []) + half_evs, sig=True)
            nc_ = HT // 32
            xt4 = xt.rearrange("p (c g k) -> p c g k", g=2, k=16)
            sc = qs_r.f32(HT, HT + 2 * nc_)
            rcp = qs_r.f32(HT + 128, HT + 128 + 2 * nc_)
            ev1 = p.comp("dve", lambda e: e.tensor_reduce(
                sc.rearrange("p (c g) -> p c g", g=2), xt4, AX.X, OP.max,
                apply_absolute_value=True), reads=[ev_t], sig=True)
            ev2 = p.comp("dve", lambda e: e.tensor_scalar(
                sc, sc, 1.0 / 6.0, 1e-12, OP.mult, OP.max), reads=[ev1],
                sig=True)
            ev3 = p.comp("dve", lambda e: e.reciprocal(rcp, sc), reads=[ev2],
                         sig=True)
            rb = rcp.rearrange("p (c g) -> p c g", g=2).unsqueeze(3
                ).broadcast_to([128, nc_, 2, 16])
            axn = qs_r.f32(HT + 256, 2 * HT + 256)
            ev_ax = p.comp("act", lambda e: e.activation(axn, xt, AF.Abs),
                           reads=[ev_t], sig=True)
            ev4 = p.comp("dve", lambda e: e.tensor_tensor(
                axn.rearrange("p (c g k) -> p c g k", g=2, k=16),
                axn.rearrange("p (c g k) -> p c g k", g=2, k=16),
                rb, OP.mult), reads=[ev3, ev_ax], sig=True)
            sg = qs_r.bf16(2 * (4 * HT + 256), 2 * (4 * HT + 256) + HT)
            ev_sg = p.comp("act", lambda e: e.activation(sg, xt, AF.Sign),
                           reads=[ev_t, ev_ax], sig=True)
            q_ap = qs_r.f32(2 * HT + 256, 3 * HT + 256)
            env["_lad_tmp"] = qs_r.f32(3 * HT + 256, 4 * HT + 256)
            ev5 = _ladder(env, "dve", q_ap, axn, [ev4])
            ev6 = p.comp("dve", lambda e: e.scalar_tensor_tensor(
                q_ap, q_ap, 0.5, sg, OP.mult, OP.mult),
                reads=[ev5, ev_sg], sig=True)
            sb = sc.rearrange("p (c g) -> p c g", g=2).unsqueeze(3
                ).broadcast_to([128, nc_, 2, 16])
            xqt = qs_r.bf16(0, HT)
            ev7 = p.comp("dve", lambda e: e.tensor_tensor(
                xqt.rearrange("p (c g k) -> p c g k", g=2, k=16),
                q_ap.rearrange("p (c g k) -> p c g k", g=2, k=16), sb,
                OP.mult), reads=[ev6, ev_t, ev_sg], sig=True)
            xqc = xqc_r.bf16(hh * HT, (hh + 1) * HT)
            ev8 = p.comp("dve", lambda e, xqc=xqc: e.transpose(xqc, xqt),
                         reads=[ev7] + ev_spill_prev, sig=True)
            half_evs.append(ev8)
        xnc_prev[csl] = half_evs
        ev_sp = p.dma("sp", lambda e, ec=ec: e.dma_start(
            out=scr["xq"][ec], in_=xqc_r.bf16(0, T)), reads=half_evs)
        ev_spill_prev = [ev_sp]
        xq_evs.append(ev_sp)
    arena.free(qs_r, [xq_evs[-1], ev7, ev6])
    arena.free(xqc_r, [xq_evs[-1]])
    arena.free(xnc_r, [xq_evs[-1], ev7])

    res_evs = _ffn0(env, b, res_r, res_evs, xq_evs)
    return res_evs


def _project_l0(env, b, xnb_r, ln_evs):
    p, banks, arena = env["p"], env["banks"], env["arena"]
    psum, scr, W = env["psum"], env["scr"], env["W"]
    SMALL = env["sbuf"]["SMALL"]

    def proj_fm(wname, bias_small_col, out_sink, tag):
        wt_r, wt_evs = _cast_load_weight_T(env, wname, EC, D, tag)
        sink_evs = []
        for dc in range(EC):
            for tt in range(NTT):
                bk = (dc * NTT + tt) % 2
                deps = banks.acquire(bk)
                ev_mm = p.comp("pe", lambda e, dc=dc, tt=tt, bk=bk, wt_r=wt_r: [
                    e.matmul(psum[bk][:, 0:512],
                             wt_r.bf16(0, EC * D).rearrange(
                                 "p (j m) -> p j m", m=D)[:, j,
                                                          dc * 128:(dc + 1) * 128],
                             xnb_r.bf16(j * T + tt * 512, j * T + tt * 512 + 512),
                             start=(j == 0), stop=(j == EC - 1))
                    for j in range(EC)],
                    reads=wt_evs + ln_evs + deps, sig=True)
                banks.wrote(bk, ev_mm)
                ev_s = out_sink(dc, tt, bk, ev_mm, bias_small_col)
                banks.read(bk, ev_s)
                sink_evs.append(ev_s)
        arena.free(wt_r, sink_evs)
        return sink_evs

    qstg_r = arena.alloc("q_stage", T // 2, side="high")
    qsp_evs = [None] * EC
    prev_sp = {0: qstg_r.deps}

    def q_sink(dc, tt, bk, ev_mm, bcol):
        dst = qstg_r.bf16(tt * 512, (tt + 1) * 512)
        ev = p.comp("act", lambda e, dst=dst, bk=bk, dc=dc: e.activation(
            dst, psum[bk][:, 0:512], AF.Identity,
            bias=SMALL[:, bcol + dc:bcol + dc + 1]),
            reads=[ev_mm, env["_ev_bq"]] + (prev_sp.get(0) or []), sig=True)
        if tt == NTT - 1:
            ev_sp = p.dma("sp", lambda e, dc=dc: e.dma_start(
                out=scr["qt1"][dc], in_=qstg_r.bf16(0, T)), reads=[ev])
            prev_sp[0] = [ev_sp]
            qsp_evs[dc] = ev_sp
        return ev

    proj_fm("L0_attn_Wq", 1024, q_sink, f"q0b{b}")
    arena.free(qstg_r, [e for e in qsp_evs if e])

    kt_r = arena.alloc("l0_kt", EC * T // 2)

    def k_sink(dc, tt, bk, ev_mm, bcol):
        dst = kt_r.bf16(dc * T + tt * 512, dc * T + tt * 512 + 512)
        return p.comp("act", lambda e, dst=dst, bk=bk, dc=dc: e.activation(
            dst, psum[bk][:, 0:512], AF.Identity,
            bias=SMALL[:, bcol + dc:bcol + dc + 1]),
            reads=[ev_mm, env["_ev_bk"]] +
                  (kt_r.deps if (dc == 0 and tt == 0) else []),
            sig=True)

    k_evs = proj_fm("L0_attn_Wk", 1032, k_sink, f"k0b{b}")

    vt_r = arena.alloc("l0_vt", NTC * 1024 // 2)
    wt_r, wt_evs = _cast_load_weight_T(env, "L0_attn_Wv", EC, D, f"v0b{b}")
    rows_bv = env["arena"].live["l0_rows"]
    bv_row = rows_bv.f32(0, 1024)[0:1, :]
    v_evs = []
    for tc in range(NTC):
        for n in range(2):
            bk = (tc * 2 + n) % 2 + 2
            deps = banks.acquire(bk)
            ev_mm = p.comp("pe", lambda e, tc=tc, n=n, bk=bk, wt_r=wt_r: [
                e.matmul(psum[bk][:, 0:512],
                         xnb_r.bf16(j * T + tc * 128, j * T + (tc + 1) * 128),
                         wt_r.bf16(0, EC * D).rearrange(
                             "p (j m) -> p j m", m=D)[:, j,
                                                      n * 512:(n + 1) * 512],
                         start=(j == 0), stop=False)
                for j in range(EC)] + [
                e.matmul(psum[bk][:, 0:512], env["ones_row"],
                         bv_row[:, n * 512:(n + 1) * 512],
                         start=False, stop=True)],
                reads=wt_evs + ln_evs + deps + [env["ev_const"],
                                                env.get("_ev_bv")], sig=True)
            banks.wrote(bk, ev_mm)
            dst = vt_r.bf16(tc * 1024 + n * 512, tc * 1024 + (n + 1) * 512)
            ev_s = p.comp("act", lambda e, dst=dst, bk=bk: e.copy(
                dst, psum[bk][:, 0:512]),
                reads=[ev_mm] + (vt_r.deps if tc == 0 and n == 0 else []),
                sig=True)
            banks.read(bk, ev_s)
            v_evs.append(ev_s)
    arena.free(wt_r, v_evs)

    env["_l0_kt"] = kt_r
    env["_l0_vt"] = vt_r
    env["_l0_proj_evs"] = k_evs + v_evs + [e for e in qsp_evs if e]
    return [e for e in qsp_evs if e]


def _attn_l0(env, b, res_r, kt_r, vt_r, qt_spill_evs, ev_res):
    p, banks, arena = env["p"], env["banks"], env["arena"]
    psum, scr = env["psum"], env["scr"]
    SMALL = env["sbuf"]["SMALL"]
    pt_r = arena.alloc("l0_pt", NTC * 512 // 2)
    qt_r = arena.alloc("l0_qtile", 2 * EC * 512 // 2, side="high")
    zn_r = arena.alloc("l0_zn", 512, side="high")
    rbs_r = arena.alloc("l0_rbs", 512, side="high")
    res_state = {}
    att_done = []
    qslot_ev = {0: qt_r.deps, 1: qt_r.deps}
    for tt in range(NTT):
        sl = tt % 2
        qtile = qt_r.bf16(sl * EC * 512, (sl + 1) * EC * 512)
        ev_q = p.dma("sp", lambda e, tt=tt, qtile=qtile: e.dma_start(
            out=qtile.rearrange("p (j m) -> p j m", m=512),
            in_=scr["qt1"][:, :, tt * 512:(tt + 1) * 512].rearrange(
                "j p m -> p j m")),
            reads=qslot_ev[sl] + qt_spill_evs)
        deps_sum = banks.acquire(2)
        exp_evs = []
        for tkc in range(NTC):
            bk = tkc % 2
            deps = banks.acquire(bk)
            ev_mm = p.comp("pe", lambda e, tkc=tkc, bk=bk, qtile=qtile: [
                e.matmul(psum[bk][:, 0:512],
                         kt_r.bf16(j * T + tkc * 128, j * T + (tkc + 1) * 128),
                         qtile[:, j * 512:(j + 1) * 512],
                         start=(j == 0), stop=(j == EC - 1))
                for j in range(EC)],
                reads=[ev_q] + deps, sig=True)
            banks.wrote(bk, ev_mm)
            dst = pt_r.bf16(tkc * 512, (tkc + 1) * 512)
            ev_e = p.comp("act", lambda e, dst=dst, bk=bk: e.activation(
                dst, psum[bk][:, 0:512], AF.Exp, bias=env["cell"][-SHIFT_L0],
                scale=1.0 / 32.0),
                reads=[ev_mm, env["ev_cells"]] +
                      (pt_r.deps if tt == 0 and tkc == 0 else []),
                sig=True)
            banks.read(bk, ev_e)
            exp_evs.append(ev_e)
        ev_sums = p.comp("pe", lambda e: [
            e.matmul(psum[2][0:1, 0:512], env["ones_col_b"],
                     pt_r.bf16(tkc * 512, (tkc + 1) * 512),
                     start=(tkc == 0), stop=(tkc == NTC - 1))
            for tkc in range(NTC)],
            reads=exp_evs + [env["ev_constb"]] + deps_sum, sig=True)
        banks.wrote(2, ev_sums)
        rsum = SMALL[0:1, 0:512]
        ev_r = p.comp("dve", lambda e: e.reciprocal(
            rsum, psum[2][0:1, 0:512]),
            reads=[ev_sums] + env["rowchain"][-4:], sig=True)
        banks.read(2, ev_r)
        pv_evs = []
        for dc in range(EC):
            bk = 3 + dc % 2
            deps = banks.acquire(bk)
            ev_mm = p.comp("pe", lambda e, dc=dc, bk=bk: [
                e.matmul(psum[bk][:, 0:512],
                         vt_r.bf16(tkc * 1024 + dc * 128,
                                   tkc * 1024 + (dc + 1) * 128),
                         pt_r.bf16(tkc * 512, (tkc + 1) * 512),
                         start=(tkc == 0), stop=(tkc == NTC - 1))
                for tkc in range(NTC)],
                reads=exp_evs + deps, sig=True)
            banks.wrote(bk, ev_mm)
            pv_evs.append(ev_mm)
            if dc == 0:
                ev_rb = _bcast_row(env, 5, rsum, [ev_r])
                env["rowchain"] = [ev_rb]
                ev_rbs = p.comp("dve", lambda e: e.tensor_copy(
                    rbs_r.f32(0, 512), psum[5][:, 0:512]),
                    reads=[ev_rb] + rbs_r.deps + zn_r.deps, sig=True)
                banks.read(5, ev_rbs)
            ev_zn = p.comp("dve", lambda e, bk=bk: e.tensor_tensor(
                zn_r.f32(0, 512), psum[bk][:, 0:512], rbs_r.f32(0, 512),
                OP.mult), reads=[ev_mm, ev_rbs] + zn_r.deps, sig=True)
            banks.read(bk, ev_zn)
            key = (dc, tt)
            prior = res_state.get(key, ev_res)
            ev_add = p.comp("dve", lambda e, dc=dc, tt=tt: e.tensor_tensor(
                res_r.f32(dc * T + tt * 512, dc * T + tt * 512 + 512),
                res_r.f32(dc * T + tt * 512, dc * T + tt * 512 + 512),
                zn_r.f32(0, 512), OP.add), reads=[ev_zn, prior], sig=True)
            res_state[key] = ev_add
            att_done.append(ev_add)
        qslot_ev[sl] = [att_done[-1]] + exp_evs[-1:]
    env["_l0_att_done"] = att_done
    env["_l0_res_state"] = res_state
    arena.free(pt_r, att_done[-EC:])
    arena.free(qt_r, att_done[-EC:])
    arena.free(zn_r, att_done[-1:])
    arena.free(rbs_r, att_done[-1:])
    return att_done


FFN0_ITILES = [(0, 512), (512, 512), (1024, 512), (1536, 512),
               (2048, 512), (2560, 256)]


def _ffn0(env, b, res_r, res_evs, xq_evs):
    p, banks, arena = env["p"], env["banks"], env["arena"]
    psum, psum_t, scr = env["psum"], env["psum_t"], env["scr"]
    res_state = env["_l0_res_state"]

    xq_r = arena.alloc("ffn_xq", EC * T // 2)
    ev_xq = p.dma("sp", lambda e: e.dma_start(
        out=xq_r.bf16(0, EC * T).rearrange("p (j m) -> p j m", m=T),
        in_=scr["xq"].rearrange("j p m -> p j m")),
        reads=xq_r.deps + xq_evs, stream="w")

    wst_r = arena.alloc("ffn_wst", 4 * 2048, side="high")
    wdq_r = arena.alloc("ffn_wdq", 2 * IC0 * 128 // 2, side="high")
    h_r = arena.alloc("ffn_h", 4 * 256, side="high")
    hq_r = arena.alloc("ffn_hq", 4 * 256, side="high")
    hqt_r = arena.alloc("ffn_hqt", IC0 * 512 // 2)
    gs_r = arena.alloc("ffn_gs", 4 * 256, side="high")
    qs_r = arena.alloc("ffn_qs", 2048, side="high")
    wslot_ev = {0: wst_r.deps + env["ev_wprep"], 1: wst_r.deps}
    dslot_ev = {0: wdq_r.deps + env["ev_wprep"], 1: wdq_r.deps}
    h_last = {i: h_r.deps for i in range(4)}
    hq_last = {0: hq_r.deps + qs_r.deps, 1: hq_r.deps, 2: hq_r.deps,
               3: hq_r.deps}
    gs_last = {i: gs_r.deps for i in range(4)}
    hqt_last = hqt_r.deps
    ev_qprev = None
    out_evs = []
    for tt in range(NTT):
        hqt_evs = []
        for it, (i0, iw) in enumerate(FFN0_ITILES):
            sl = it % 2
            wg = wst_r.bf16(sl * 8192, sl * 8192 + 4096)[:, 0:EC * iw]
            wu = wst_r.bf16(sl * 8192 + 4096, (sl + 1) * 8192)[:, 0:EC * iw]
            ev_w = [
                p.dma("sp", lambda e, i0=i0, iw=iw, wg=wg: e.dma_start(
                    out=wg.rearrange("p (j m) -> p j m", m=iw),
                    in_=scr["wgq"][:, :, i0:i0 + iw].rearrange("j p m -> p j m")),
                    reads=wslot_ev[sl], stream="w"),
                p.dma("sp", lambda e, i0=i0, iw=iw, wu=wu: e.dma_start(
                    out=wu.rearrange("p (j m) -> p j m", m=iw),
                    in_=scr["wuq"][:, :, i0:i0 + iw].rearrange("j p m -> p j m")),
                    stream="w"),
            ]
            wsl_evs = []
            for tcw in range(4):
                tc = tt * 4 + tcw
                bkg = tcw % 2
                bku = 2 + tcw % 2
                hsl4 = tcw
                depsg = banks.acquire(bkg)
                depsu = banks.acquire(bku)
                ev_g = p.comp("pe", lambda e, tc=tc, iw=iw, wg=wg, bkg=bkg: [
                    e.matmul(psum[bkg][:, 0:iw],
                             xq_r.bf16(j * T + tc * 128, j * T + (tc + 1) * 128),
                             wg[:, j * iw:(j + 1) * iw],
                             start=(j == 0), stop=(j == EC - 1))
                    for j in range(EC)], reads=ev_w + [ev_xq] + depsg,
                    sig=True)
                banks.wrote(bkg, ev_g)
                ev_u = p.comp("pe", lambda e, tc=tc, iw=iw, wu=wu, bku=bku: [
                    e.matmul(psum[bku][:, 0:iw],
                             xq_r.bf16(j * T + tc * 128, j * T + (tc + 1) * 128),
                             wu[:, j * iw:(j + 1) * iw],
                             start=(j == 0), stop=(j == EC - 1))
                    for j in range(EC)], reads=ev_w + [ev_xq] + depsu,
                    sig=True)
                banks.wrote(bku, ev_u)
                wsl_evs += [ev_g, ev_u]
                hsl = tcw
                gs = gs_r.bf16(hsl * 512, hsl * 512 + iw)
                ev_si = p.comp("act", lambda e, gs=gs, bkg=bkg, iw=iw:
                               e.activation(gs, psum[bkg][:, 0:iw], AF.Silu),
                               reads=[ev_g] + gs_last[hsl], sig=True)
                banks.read(bkg, ev_si)
                h_t = h_r.bf16(hsl * 512, hsl * 512 + iw)
                ev_h = p.comp("dve", lambda e, h_t=h_t, gs=gs, bku=bku, iw=iw:
                              e.tensor_tensor(h_t, gs, psum[bku][:, 0:iw],
                                              OP.mult),
                              reads=[ev_si, ev_u] + h_last[hsl], sig=True)
                banks.read(bku, ev_h)
                gs_last[hsl] = [ev_h]
                hq_t = hq_r.bf16(hsl * 512, hsl * 512 + iw)
                ev_q = _quant_block_free(
                    env, h_t, iw, hq_t, qs_r,
                    [ev_h] + hq_last[hsl] + ([ev_qprev] if ev_qprev else []),
                    src_is_f32=False)
                ev_qprev = ev_q
                h_last[hsl] = [ev_q]
                bk7 = 7
                deps7 = banks.acquire(bk7)
                njj = iw // 128
                ev_tp = p.comp("pe", lambda e, hq_t=hq_t, njj=njj: [
                    e.transpose(psum_t[:, jj * 128:(jj + 1) * 128],
                                hq_t[:, jj * 128:(jj + 1) * 128],
                                env["idenb"])
                    for jj in range(njj)],
                    reads=[ev_q, env["ev_constb"]] + deps7, sig=True)
                banks.wrote(bk7, ev_tp)
                ev_he = p.comp("dve", lambda e, i0=i0, tcw=tcw, njj=njj:
                               e.tensor_copy(
                    hqt_r.bf16(0, IC0 * 512).rearrange(
                        "p (ic m) -> p ic m", m=512)[
                        :, i0 // 128:i0 // 128 + njj,
                        tcw * 128:(tcw + 1) * 128],
                    psum_t[:, 0:njj * 128].rearrange("p (jj m) -> p jj m",
                                                     m=128)),
                    reads=[ev_tp] + (hqt_last if it == 0 and tcw == 0 else []),
                    sig=True)
                banks.read(bk7, ev_he)
                hq_last[hsl] = [ev_he]
                hqt_evs.append(ev_he)
            wslot_ev[sl] = wsl_evs
        for dc in range(EC):
            dsl = dc % 2
            wd = wdq_r.bf16(dsl * IC0 * 128, (dsl + 1) * IC0 * 128)
            ev_wd = p.dma("sp", lambda e, dc=dc, wd=wd: e.dma_start(
                out=wd.rearrange("p (ic m) -> p ic m", m=128),
                in_=scr["wdq"][:, :, dc * 128:(dc + 1) * 128].rearrange(
                    "ic p m -> p ic m")),
                reads=dslot_ev[dsl], stream="w")
            bk = 4 + dc % 2
            deps = banks.acquire(bk)
            ev_mm = p.comp("pe", lambda e, wd=wd, bk=bk: [
                e.matmul(psum[bk][:, 0:512],
                         wd[:, ic * 128:(ic + 1) * 128],
                         hqt_r.bf16(ic * 512, (ic + 1) * 512),
                         start=(ic == 0), stop=(ic == IC0 - 1))
                for ic in range(IC0)],
                reads=[ev_wd] + hqt_evs + deps, sig=True)
            banks.wrote(bk, ev_mm)
            dslot_ev[dsl] = [ev_mm]
            key = (dc, tt)
            prior = res_state.get(key)
            ev_add = p.comp("dve", lambda e, dc=dc, tt=tt, bk=bk:
                            e.tensor_tensor(
                res_r.f32(dc * T + tt * 512, dc * T + tt * 512 + 512),
                res_r.f32(dc * T + tt * 512, dc * T + tt * 512 + 512),
                psum[bk][:, 0:512], OP.add),
                reads=[ev_mm, prior], sig=True)
            banks.read(bk, ev_add)
            res_state[key] = ev_add
            out_evs.append(ev_add)
        hqt_last = hqt_evs
    for r, evs in ((xq_r, out_evs[-2:]), (wst_r, out_evs[-2:]),
                   (wdq_r, out_evs[-2:]), (h_r, out_evs[-2:]),
                   (hq_r, out_evs[-2:]), (hqt_r, out_evs[-2:]),
                   (gs_r, out_evs[-2:]), (qs_r, out_evs[-2:])):
        arena.free(r, evs)
    env["_l0_res_state"] = res_state
    return out_evs


def _cast_load_rows_T(env, wname, chunks, nin, tag, reads=None):
    p, banks, arena, psum_t = env["p"], env["banks"], env["arena"], env["psum_t"]
    W = env["W"]
    njc = nin // 128
    nout = len(chunks)
    ncols = nout * 128
    nat_r = arena.alloc(f"wnat_{tag}", nin // 2, side="high")
    wt_r = arena.alloc(f"wt_{tag}", (njc * ncols) // 2, side="high")
    evs = []
    ev_prev = list(nat_r.deps)
    for c, plan in enumerate(chunks):
        ev_lds = []
        for (r0, p0, nr) in plan:
            ev_lds.append(p.dma("gp", lambda e, r0=r0, p0=p0, nr=nr:
                                e.dma_start(
                out=nat_r.bf16(0, nin)[p0:p0 + nr, :],
                in_=W[wname][r0:r0 + nr, :]),
                reads=ev_prev + (reads or []), stream="w"))
            ev_prev = []
        tevs = []
        for jb in range(0, njc, 8):
            jn = min(8, njc - jb)
            bk = 7
            deps = banks.acquire(bk)
            ev_tp = p.comp("pe", lambda e, jb=jb, jn=jn: [
                e.transpose(env["psum_t"][:, j * 128:(j + 1) * 128],
                            nat_r.bf16(0, nin)[:, (jb + j) * 128:
                                               (jb + j + 1) * 128],
                            env["idenb"])
                for j in range(jn)],
                reads=ev_lds + [env["ev_constb"]] + deps + wt_r.deps,
                sig=True)
            banks.wrote(bk, ev_tp)
            tevs.append(ev_tp)
            ev_ev = p.comp("dve", lambda e, c=c, jb=jb, jn=jn: e.tensor_copy(
                wt_r.bf16(0, njc * ncols).rearrange(
                    "p (j m) -> p j m", m=ncols)[:, jb:jb + jn,
                                                 c * 128:(c + 1) * 128],
                env["psum_t"][:, 0:jn * 128].rearrange("p (j m) -> p j m",
                                                       m=128)),
                reads=[ev_tp], sig=True)
            banks.read(bk, ev_ev)
            evs.append(ev_ev)
        ev_prev = tevs
    arena.free(nat_r, evs[-1:] + ev_prev)
    return wt_r, evs


TZ = T + 1


def _layer1(env, b, res_r, ev_res):
    p, banks, arena = env["p"], env["banks"], env["arena"]
    psum, scr, W = env["psum"], env["scr"], env["W"]
    SMALL = env["sbuf"]["SMALL"]

    xint_r = arena.alloc("xint", EC * TZ + 8)
    xintb_r = arena.alloc("xintb", (EC * TZ) // 2 + 8)
    ev_z0 = p.comp("dve", lambda e, xr=xint_r, xbr=xintb_r: [
        e.memset(xr.f32(0, EC * TZ).rearrange("p (c z) -> p c z", z=TZ)
                 [:, :, 0:1], 0.0),
        e.memset(xbr.bf16(0, EC * TZ).rearrange("p (c z) -> p c z", z=TZ)
                 [:, :, 0:1], 0.0)],
        reads=xint_r.deps + xintb_r.deps, sig=True)
    ln_evs = _layernorm_fast(
        env,
        lambda tt: (res_r.f32(0, EC * T).rearrange(
            "p (c m) -> p c m", m=T)[:, :, tt * 512:(tt + 1) * 512], ev_res),
        [("f32", lambda tt, xr=xint_r: xr.f32(0, EC * TZ).rearrange(
            "p (c m) -> p c m", m=TZ)[:, :, 1 + tt * 512:1 + tt * 512 + 512]),
         ("bf16", lambda tt, xbr=xintb_r: xbr.bf16(0, EC * TZ).rearrange(
             "p (c m) -> p c m", m=TZ)[:, :, 1 + tt * 512:1 + tt * 512 + 512])],
        f"l1b{b}")
    ln_evs = ln_evs + [ev_z0]

    sp_evs = []
    for ec in range(EC):
        sp_evs.append(p.dma("sp", lambda e, ec=ec: e.dma_start(
            out=scr["res"][ec], in_=res_r.f32(ec * T, (ec + 1) * T)),
            reads=ev_res + ln_evs[-1:]))
    arena.free(res_r, sp_evs)
    spx_evs = []
    for ec in range(EC):
        spx_evs.append(p.dma("sp", lambda e, ec=ec, xr=xint_r: e.dma_start(
            out=scr["xn1"][ec], in_=xr.f32(ec * TZ + 1, ec * TZ + 1 + T)),
            reads=ln_evs))
    arena.free(xint_r, spx_evs)

    def xb_ap(ec, c0, c1, shift=False):
        o = ec * TZ + (0 if shift else 1)
        return xintb_r.bf16(o + c0, o + c1)

    att_evs = _mha(env, b, xb_ap, ln_evs, xintb_r)
    xint2_r, ffn_evs = _l1ffn(env, b, ln_evs + spx_evs, att_evs)

    fin_r = arena.alloc("final", EC * T)
    stg_r = arena.alloc("fin_stg", 2 * 2048, side="high")
    stgb_r = arena.alloc("fin_stgb", 2 * 1024, side="high")
    slot_ev = {0: stg_r.deps + stgb_r.deps + sp_evs + att_evs,
               1: stg_r.deps + stgb_r.deps + sp_evs + att_evs}
    fin_evs = []
    for ec in range(EC):
        sl = ec % 2
        xs = stg_r.f32(sl * 2048, sl * 2048 + 2048)
        ab = stgb_r.bf16(sl * 2048, sl * 2048 + 2048)
        ev_l1 = p.dma("sp", lambda e, ec=ec, xs=xs: e.dma_start(
            out=xs, in_=scr["res"][ec]), reads=slot_ev[sl])
        ev_l2 = p.dma("gp", lambda e, ec=ec, ab=ab: e.dma_start(
            out=ab, in_=scr["att"][ec]), reads=slot_ev[sl])
        ev_a1 = p.comp("dve", lambda e, ec=ec, xs=xs, ab=ab: e.tensor_tensor(
            fin_r.f32(ec * T, (ec + 1) * T), xs, ab, OP.add),
            reads=[ev_l1, ev_l2] + fin_r.deps, sig=True)
        ev_a2 = p.comp("dve", lambda e, ec=ec: e.tensor_tensor(
            fin_r.f32(ec * T, (ec + 1) * T),
            fin_r.f32(ec * T, (ec + 1) * T),
            xint2_r.f32(ec * TZ + 1, ec * TZ + 1 + T), OP.add),
            reads=[ev_a1] + ffn_evs, sig=True)
        slot_ev[sl] = [ev_a1, ev_a2]
        fin_evs.append(ev_a2)
    arena.free(xint2_r, fin_evs)
    arena.free(stg_r, fin_evs)
    arena.free(stgb_r, fin_evs)
    env["final_r"] = fin_r
    return fin_evs


def _mha(env, b, xb_ap, ln_evs, xintb_r):
    p, banks, arena = env["p"], env["banks"], env["arena"]
    psum, scr, W = env["psum"], env["scr"], env["W"]
    SMALL = env["sbuf"]["SMALL"]

    bev = []
    for g in range(8):
        for part, r0 in ((0, 384 * g), (64, 384 * g + 192)):
            bev.append(p.dma("sp", lambda e, part=part, r0=r0, g=g:
                             e.dma_start(
                out=SMALL[part:part + 64, 1040 + g:1041 + g],
                in_=W["L1_attn_b_qkv"][r0:r0 + 64].rearrange(
                    "(a d) -> d a", a=1)), reads=ln_evs))
            bev.append(p.dma("sp", lambda e, part=part, r0=r0 + 64, g=g:
                             e.dma_start(
                out=SMALL[part:part + 64, 1048 + g:1049 + g],
                in_=W["L1_attn_b_qkv"][r0:r0 + 64].rearrange(
                    "(a d) -> d a", a=1)), reads=ln_evs))
    rbs_r = arena.alloc("mha_rbs", 1024, side="high")
    bv_row = rbs_r.f32(0, 1024)[0:1, :]
    ev_bv = p.dma("sp", lambda e: e.dma_start(
        out=bv_row.rearrange("a (h d) -> a h d", d=64),
        in_=W["L1_attn_b_qkv"].rearrange("(h x d) -> x h d", x=3, d=64)
            [2:3, :, :]),
        reads=rbs_r.deps)
    bev.append(ev_bv)
    ev_bo = p.dma("sp", lambda e: e.dma_start(
        out=SMALL[:, 1056:1064],
        in_=W["L1_attn_b_out"].rearrange("(c p) -> p c", p=128),
        allow_slow_non_contiguous=True), reads=ln_evs)

    def qk_chunks(off):
        return [[(384 * g + off, 0, 64), (384 * g + 192 + off, 64, 64)]
                for g in range(8)]

    kt_r = arena.alloc("mha_kt", EC * T // 2)
    wt_r, wt_evs = _cast_load_rows_T(env, "L1_attn_W_qkv", qk_chunks(64), D,
                                     f"mk{b}")
    k_evs = []
    for g in range(8):
        for tt in range(NTT):
            bk = (g * NTT + tt) % 2
            deps = banks.acquire(bk)
            ev_mm = p.comp("pe", lambda e, g=g, tt=tt, bk=bk, wt_r=wt_r: [
                e.matmul(psum[bk][:, 0:512],
                         wt_r.bf16(0, EC * D).rearrange(
                             "p (j m) -> p j m", m=D)[:, j,
                                                      g * 128:(g + 1) * 128],
                         xb_ap(j, tt * 512, (tt + 1) * 512),
                         start=(j == 0), stop=(j == EC - 1))
                for j in range(EC)],
                reads=wt_evs + ln_evs + deps, sig=True)
            banks.wrote(bk, ev_mm)
            dst = kt_r.bf16(g * T + tt * 512, g * T + tt * 512 + 512)
            ev_s = p.comp("act", lambda e, dst=dst, bk=bk, g=g: e.activation(
                dst, psum[bk][:, 0:512], AF.Identity,
                bias=SMALL[:, 1048 + g:1049 + g]),
                reads=[ev_mm] + bev +
                      (kt_r.deps if g == 0 and tt == 0 else []),
                sig=True)
            banks.read(bk, ev_s)
            k_evs.append(ev_s)
    arena.free(wt_r, k_evs[-1:])

    wt_r, wt_evs = _cast_load_rows_T(env, "L1_attn_W_qkv", qk_chunks(0), D,
                                     f"mq{b}")
    qstg_r = arena.alloc("mha_qstg", T // 2, side="high")
    q_sp = []
    prev_sp = list(qstg_r.deps)
    for g in range(8):
        g_evs = []
        for tt in range(NTT):
            bk = (g * NTT + tt) % 2
            deps = banks.acquire(bk)
            ev_mm = p.comp("pe", lambda e, g=g, tt=tt, bk=bk, wt_r=wt_r: [
                e.matmul(psum[bk][:, 0:512],
                         wt_r.bf16(0, EC * D).rearrange(
                             "p (j m) -> p j m", m=D)[:, j,
                                                      g * 128:(g + 1) * 128],
                         xb_ap(j, tt * 512, (tt + 1) * 512),
                         start=(j == 0), stop=(j == EC - 1))
                for j in range(EC)],
                reads=wt_evs + ln_evs + deps, sig=True)
            banks.wrote(bk, ev_mm)
            dst = qstg_r.bf16(tt * 512, (tt + 1) * 512)
            ev_s = p.comp("act", lambda e, dst=dst, bk=bk, g=g: e.activation(
                dst, psum[bk][:, 0:512], AF.Identity,
                bias=SMALL[:, 1040 + g:1041 + g]),
                reads=[ev_mm] + bev + prev_sp, sig=True)
            banks.read(bk, ev_s)
            g_evs.append(ev_s)
        ev_spill = p.dma("sp", lambda e, g=g: e.dma_start(
            out=scr["qt1"][g], in_=qstg_r.bf16(0, T)), reads=g_evs)
        prev_sp = [ev_spill]
        q_sp.append(ev_spill)
    arena.free(wt_r, q_sp[-1:])
    arena.free(qstg_r, q_sp[-1:])

    def v_chunks():
        return [[(384 * g + 128, 0, 64), (384 * g + 320, 64, 64)]
                for g in range(8)]

    vt_r = arena.alloc("mha_vt", NTC * 1040 // 2)

    def vt_chunk(tc):
        return vt_r.bf16(tc * 1040, (tc + 1) * 1040)

    wt_r, wt_evs = _cast_load_rows_T(env, "L1_attn_W_qkv", v_chunks(), D,
                                     f"mv{b}")
    ev_ones = p.comp("dve", lambda e: e.memset(
        vt_r.bf16(0, NTC * 1040).rearrange(
            "p (c k) -> p c k", k=65)[:, :, 64:65], 1.0),
        reads=vt_r.deps, sig=True)
    v_evs = []
    for tc in range(NTC):
        for n in range(2):
            bk = (tc * 2 + n) % 2 + 2
            deps = banks.acquire(bk)
            ev_mm = p.comp("pe", lambda e, tc=tc, n=n, bk=bk, wt_r=wt_r: [
                e.matmul(psum[bk][:, 0:512],
                         xb_ap(j, tc * 128, (tc + 1) * 128),
                         wt_r.bf16(0, EC * D).rearrange(
                             "p (j m) -> p j m", m=D)[:, j,
                                                      n * 512:(n + 1) * 512],
                         start=(j == 0), stop=False)
                for j in range(EC)] + [
                e.matmul(psum[bk][:, 0:512], env["ones_row"],
                         bv_row[:, n * 512:(n + 1) * 512],
                         start=False, stop=True)],
                reads=wt_evs + ln_evs + deps + [env["ev_const"], ev_bv],
                sig=True)
            banks.wrote(bk, ev_mm)
            dst = vt_chunk(tc).rearrange(
                "p (h k) -> p h k", k=65)[:, n * 8:(n + 1) * 8, 0:64]
            ev_s = p.comp("act", lambda e, dst=dst, bk=bk: e.copy(
                dst, psum[bk][:, 0:512].rearrange(
                    "p (h k) -> p h k", k=64)),
                reads=[ev_mm, ev_ones], sig=True)
            banks.read(bk, ev_s)
            v_evs.append(ev_s)
    arena.free(wt_r, v_evs[-1:])
    arena.free(xintb_r, k_evs[-1:] + q_sp[-1:] + v_evs[-1:])

    wo_r, wo_evs = _cast_load_weight_T(env, "L1_attn_W_out", EC, D, f"mo{b}")
    att_r = arena.alloc("mha_att", EC * 512 // 2)
    qt_r = arena.alloc("mha_qtile", 2 * EC * 512 // 2, side="high")
    pt0_r = arena.alloc("mha_pt0", 2 * NTC * 512 // 2, side="high")
    pt1_r = arena.alloc("mha_pt1", 2 * NTC * 512 // 2, side="high")
    ptslot = {0: pt0_r, 1: pt1_r}
    ob_r = arena.alloc("mha_ostg", EC * 512 // 2, side="high")
    qslot_ev = {0: list(qt_r.deps), 1: list(qt_r.deps)}
    attsp_evs = []
    ost_prev = list(ob_r.deps)
    att_prev = {g: list(att_r.deps) for g in range(8)}
    pt_prev = {0: list(pt0_r.deps), 1: list(pt1_r.deps)}
    p7f = env["psum_t"][:, :].bitcast(F32)
    rsA = SMALL[64:65, 0:512]
    rsB = SMALL[64:65, 512:1024]
    ones64 = SMALL[64:65, 1592:1656]
    ev_o64 = p.comp("dve", lambda e: e.memset(ones64, 1.0),
                    reads=env["rowchain"][-4:], sig=True)
    env["ev_ones64"] = ev_o64
    jobs = [(tt, g) for tt in range(NTT) for g in range(8)]
    qtile_of = {}
    qtile_ev = {}
    win_evs = []

    def plan_scores(j):
        tt, g = jobs[j]
        sl = j % 2
        if g == 0:
            qsl = tt % 2
            qtile = qt_r.bf16(qsl * EC * 512, (qsl + 1) * EC * 512)
            qtile_of[tt] = qtile
            qtile_ev[tt] = p.dma("sp", lambda e, tt=tt, qtile=qtile:
                                 e.dma_start(
                out=qtile.rearrange("p (g m) -> p g m", m=512),
                in_=scr["qt1"][:, :, tt * 512:(tt + 1) * 512].rearrange(
                    "g p m -> p g m")),
                reads=qslot_ev[qsl] + q_sp)
        qtile = qtile_of[tt]
        ev_q = qtile_ev[tt]
        exp_evs = []
        for tkc in range(NTC):
            bkA = 2 * (tkc % 2)
            bkB = bkA + 1
            pair_t = env["psum_pair"][tkc % 2]
            depsA = banks.acquire(bkA)
            depsB = banks.acquire(bkB)
            ev_mm = p.comp("pe", lambda e, g=g, tkc=tkc, bkA=bkA,
                           bkB=bkB, qtile=qtile: [
                e.matmul(psum[bkA][:, 0:512],
                         kt_r.bf16(g * T + tkc * 128,
                                   g * T + (tkc + 1) * 128)[0:64, :],
                         qtile[0:64, g * 512:(g + 1) * 512],
                         start=True, stop=True, tile_position=(0, 0)),
                e.matmul(psum[bkB][:, 0:512],
                         kt_r.bf16(g * T + tkc * 128,
                                   g * T + (tkc + 1) * 128)[64:128, :],
                         qtile[64:128, g * 512:(g + 1) * 512],
                         start=True, stop=True, tile_position=(64, 0))],
                reads=[ev_q] + k_evs[-1:] + depsA + depsB, sig=True)
            banks.wrote(bkA, ev_mm)
            banks.wrote(bkB, ev_mm)
            dstAB = ptslot[sl].bf16(tkc * 1024, (tkc + 1) * 1024)
            ev_e = p.comp("act", lambda e, dstAB=dstAB, pair_t=pair_t:
                          e.activation(
                dstAB, pair_t[:, 0:1024], AF.Exp,
                bias=env["cell"][-SHIFT_MHA], scale=0.125),
                reads=[ev_mm, env["ev_cells"]] + pt_prev[sl], sig=True)
            banks.read(bkA, ev_e)
            banks.read(bkB, ev_e)
            exp_evs.append(ev_e)
        return (None, None, exp_evs, sl)

    def _plan_wo(tt):
        nonlocal win_evs, ost_prev
        ob_evs = []
        for dc in range(EC):
            bk = dc % 2
            deps = banks.acquire(bk)
            ev_mm = p.comp("pe", lambda e, dc=dc, bk=bk: [
                e.matmul(psum[bk][:, 0:512],
                         wo_r.bf16(0, EC * D).rearrange(
                             "p (j m) -> p j m", m=D)[:, j,
                                                      dc * 128:(dc + 1) * 128],
                         att_r.bf16(j * 512, (j + 1) * 512),
                         start=(j == 0), stop=(j == EC - 1))
                for j in range(EC)],
                reads=win_evs + wo_evs + deps, sig=True)
            banks.wrote(bk, ev_mm)
            dst = ob_r.bf16(dc * 512, (dc + 1) * 512)
            ev_s = p.comp("act", lambda e, dst=dst, bk=bk, dc=dc:
                          e.activation(
                dst, psum[bk][:, 0:512], AF.Identity,
                bias=SMALL[:, 1056 + dc:1057 + dc]),
                reads=[ev_mm, ev_bo] + ost_prev, sig=True)
            banks.read(bk, ev_s)
            ob_evs.append(ev_s)
        ev_asp = p.dma("gp", lambda e, tt=tt: e.dma_start(
            out=scr["att"][:, :, tt * 512:(tt + 1) * 512].rearrange(
                "c p m -> p c m"),
            in_=ob_r.bf16(0, EC * 512).rearrange("p (c m) -> p c m", m=512)),
            reads=ob_evs)
        ost_prev = [ev_asp]
        attsp_evs.append(ev_asp)
        win_evs = []

    def _finish(rec):
        g, tt = rec["g"], rec["tt"]
        rbb = rec["i"] % 4
        deps_rb = banks.acquire(rbb)
        ev_rb = p.comp("pe", lambda e, rbb=rbb: [
            e.matmul(psum[rbb][0:64, 0:512], ones64, rsA,
                     start=True, stop=True, tile_position=(64, 0)),
            e.matmul(psum[rbb][64:128, 0:512], ones64, rsB,
                     start=True, stop=True, tile_position=(64, 64))],
            reads=[rec["ev_rA"], rec["ev_rB"], ev_o64] + deps_rb, sig=True)
        banks.wrote(rbb, ev_rb)
        env["rowchain"] = [ev_rb]
        rec["ev_rb"] = ev_rb
        ev_rbe = p.comp("act", lambda e, rbb=rbb: e.copy(
            rbs_r.f32(0, 512), psum[rbb][:, 0:512]),
            reads=[ev_rb] + rbs_r.deps + env.get("_rbs_last", []), sig=True)
        banks.read(rbb, ev_rbe)
        dst = att_r.bf16(g * 512, (g + 1) * 512)
        ev_nA = p.comp("dve", lambda e, dst=dst, rec=rec: e.tensor_tensor(
            dst[0:64, :], psum[rec["pvA"]][0:64, 0:512],
            rbs_r.f32(0, 512)[0:64, :], OP.mult),
            reads=[rec["ev_pv"], ev_rbe] + att_prev[g], sig=True)
        ev_n = p.comp("dve", lambda e, dst=dst, rec=rec: e.tensor_tensor(
            dst[64:128, :], rec["pvB_ap"][0:64, 0:512],
            rbs_r.f32(0, 512)[64:128, :], OP.mult),
            reads=[rec["ev_pv"], ev_rbe, ev_nA], sig=True)
        banks.read(rec["pvA"], ev_nA)
        banks.read(rec["pvB"], ev_n)
        env["_rbs_last"] = [ev_n, ev_nA]
        att_prev[g] = [ev_n, ev_nA]
        win_evs.append(ev_n)
        if g == 7:
            qslot_ev[tt % 2] = [rec["ev_pv"]]
            _plan_wo(tt)

    prev = None
    nxt = plan_scores(0)
    for i in range(len(jobs)):
        tt, g = jobs[i]
        _pa, _pb, exp_evs, sl = nxt
        if i + 1 < len(jobs):
            nxt = plan_scores(i + 1)
        pvA, pvB = (4, 5) if i % 2 == 0 else (6, 7)
        pvB_ap = p7f if pvB == 7 else psum[pvB]
        depsA = banks.acquire(pvA)
        depsB = banks.acquire(pvB)
        ptr = ptslot[sl]
        ev_pv = p.comp("pe", lambda e, g=g, ptr=ptr, pvA=pvA,
                       pvB_ap=pvB_ap: [
            mm for tkc in range(NTC) for mm in (
                e.matmul(psum[pvA][0:65, 0:512],
                         vt_chunk(tkc)[:, (2 * g) * 65:(2 * g) * 65 + 65],
                         ptr.bf16(tkc * 1024, tkc * 1024 + 512),
                         start=(tkc == 0), stop=(tkc == NTC - 1),
                         tile_position=(0, 0)),
                e.matmul(pvB_ap[0:65, 0:512],
                         vt_chunk(tkc)[
                             :, (2 * g + 1) * 65:(2 * g + 1) * 65 + 65],
                         ptr.bf16(tkc * 1024 + 512, (tkc + 1) * 1024),
                         start=(tkc == 0), stop=(tkc == NTC - 1),
                         tile_position=(0, 0)))],
            reads=exp_evs + depsA + depsB + v_evs[-1:], sig=True)
        banks.wrote(pvA, ev_pv)
        banks.wrote(pvB, ev_pv)
        if prev is not None:
            _finish(prev)
        ev_rA = p.comp("dve", lambda e, pvA=pvA: e.reciprocal(
            rsA, psum[pvA][64:65, 0:512]),
            reads=[ev_pv] + ([prev["ev_rb"]] if prev else []) +
                  env["rowchain"][-2:], sig=True)
        ev_rB = p.comp("dve", lambda e, pvB_ap=pvB_ap: e.reciprocal(
            rsB, pvB_ap[64:65, 0:512]), reads=[ev_rA, ev_pv], sig=True)
        banks.read(pvA, ev_rA)
        banks.read(pvB, ev_rB)
        prev = dict(i=i, tt=tt, g=g, pvA=pvA, pvB=pvB, pvB_ap=pvB_ap,
                    ev_pv=ev_pv, ev_rA=ev_rA, ev_rB=ev_rB)
        pt_prev[sl] = [ev_pv] + exp_evs[-1:]
    _finish(prev)
    for r in (kt_r, vt_r, wo_r, att_r, qt_r, pt0_r, pt1_r, ob_r,
              rbs_r):
        arena.free(r, attsp_evs[-1:] + env.get("_rbs_last", []))
    return attsp_evs


INV2PI = float(np.float32(1.0 / 6.283185307179586))
PI_F = float(np.float32(3.1415926535897932))
HALFPI_F = float(np.float32(1.5707963267948966))
TWOPI_F = float(np.float32(6.283185307179586))


def _wrap_pm_pi(env, out, in_, shift, reads, tmp):
    p = env["p"]
    ev = p.comp("dve", lambda e: e.tensor_single_scalar(
        out, in_, shift, OP.add), reads=reads, sig=True)
    ev = p.comp("dve", lambda e: e.tensor_scalar(
        tmp, out, -PI_F, TWOPI_F, OP.is_lt, OP.mult), reads=[ev], sig=True)
    ev = p.comp("dve", lambda e: e.tensor_tensor(
        out, out, tmp, OP.add), reads=[ev], sig=True)
    ev = p.comp("dve", lambda e: e.tensor_scalar(
        tmp, out, PI_F, -TWOPI_F, OP.is_gt, OP.mult), reads=[ev], sig=True)
    ev = p.comp("dve", lambda e: e.tensor_tensor(
        out, out, tmp, OP.add), reads=[ev], sig=True)
    ev = p.comp("dve", lambda e: e.tensor_scalar(
        out, out, -3.1415925, 3.1415925, OP.max, OP.min), reads=[ev],
        sig=True)
    return ev


def _rotary_tables(env, l, tb_r, qs_col, ks_col, ev_qs, reads):
    p, banks, arena = env["p"], env["banks"], env["arena"]
    psum = env["psum"]
    SMALL = env["sbuf"]["SMALL"]
    cap = env["cap"]
    ts_r = arena.alloc(f"tabsc{l}", 7 * 512 + 2 * T + 768, side="high")
    t_row = ts_r.f32(0, T)[0:1, :]
    pw = ts_r.f32(T, 2 * T)[0:1, :]
    idh = ts_r.f32(2 * T, 2 * T + 128)[0:1, :]
    b10k = ts_r.f32(2 * T + 128, 2 * T + 256)[0:1, 0:128]
    invf = ts_r.f32(2 * T + 256, 2 * T + 384)[0:1, 0:128]
    expn = ts_r.f32(2 * T + 384, 2 * T + 512)[0:1, 0:128]
    lnb = ts_r.f32(2 * T + 512, 2 * T + 640)[0:1, 0:128]
    o = 2 * T + 768
    kf = ts_r.f32(o, o + 512)
    rr = ts_r.f32(o + 512, o + 1024)
    rw = ts_r.f32(o + 1024, o + 1536)
    snc = ts_r.f32(o + 1536, o + 2048)
    ki = ts_r.f32(o + 2048, o + 2560)
    lsc = ts_r.f32(o + 2560, o + 3072)
    isc = ts_r.f32(o + 3072, o + 3584)
    ev_t = p.dma("sp", lambda e: e.dma_start(out=t_row, in_=cap["iota_t"]),
                 reads=ts_r.deps + (reads or []))
    ev_i = p.dma("sp", lambda e: e.dma_start(out=idh, in_=cap["iota_dh"]),
                 reads=ts_r.deps)
    ev1 = p.comp("dve", lambda e: [
        e.memset(b10k, 10000.0),
        e.tensor_scalar_mul(expn, idh, -1.0 / 32.0)],
        reads=[ev_i] + ts_r.deps, sig=True)
    ev1b = p.comp("act", lambda e: e.activation(
        b10k[0:1, 0:1], b10k[0:1, 0:1], AF.Ln), reads=[ev1], sig=True)
    ev1c = p.comp("dve", lambda e: e.tensor_scalar(
        expn, expn, b10k[0:1, 0:1], None, OP.mult), reads=[ev1b], sig=True)
    ev2 = p.comp("act", lambda e: e.activation(invf, expn, AF.Exp),
                 reads=[ev1c], sig=True)
    ev3 = p.comp("dve", lambda e: e.tensor_scalar(
        expn, idh, 2.0, 25.6, OP.mult, OP.add), reads=[ev2], sig=True)
    ev4 = p.comp("act", lambda e: e.activation(
        lnb, expn, AF.Ln, scale=float(np.float32(1.0 / 89.6))),
        reads=[ev3], sig=True)
    ev5 = p.comp("dve", lambda e: e.tensor_scalar(
        pw, t_row, 1.0 / 256.0, -4.0, OP.mult, OP.add),
        reads=[ev_t], sig=True)
    tabs = {"CSq": 0, "SSq": T // 2, "CSk": T, "SSk": 3 * T // 2}
    out_evs = []
    ev_prev = None
    for tt in range(NTT):
        deps6 = banks.acquire(6)
        ev_f = p.comp("pe", lambda e, tt=tt: e.matmul(
            psum[6][:, 0:512], invf, t_row[:, tt * 512:(tt + 1) * 512],
            start=True, stop=True), reads=[ev2, ev_t, env["ev_const"]] + deps6,
            sig=True)
        banks.wrote(6, ev_f)
        deps5 = banks.acquire(5)
        ev_l = p.comp("pe", lambda e, tt=tt: e.matmul(
            psum[5][:, 0:512], lnb, pw[:, tt * 512:(tt + 1) * 512],
            start=True, stop=True), reads=[ev4, ev5, env["ev_const"]] + deps5,
            sig=True)
        banks.wrote(5, ev_l)
        ki_i32 = ki.bitcast(I32)
        ev_k1 = p.comp("dve", lambda e: e.tensor_scalar(
            ki_i32, psum[6][:, 0:512], INV2PI, 0.5, OP.mult, OP.add),
            reads=[ev_f] + ([ev_prev] if ev_prev else []), sig=True)
        ev_k2 = p.comp("dve", lambda e: e.tensor_copy(kf, ki_i32),
                       reads=[ev_k1], sig=True)
        ev_r = p.comp("dve", lambda e: e.scalar_tensor_tensor(
            rr, kf, -CW1, psum[6][:, 0:512], OP.mult, OP.add),
            reads=[ev_k2, ev_f], sig=True)
        banks.read(6, ev_r)
        ev_r = p.comp("dve", lambda e: e.scalar_tensor_tensor(
            rr, kf, -CW2, rr, OP.mult, OP.add), reads=[ev_r], sig=True)
        ev_r = p.comp("dve", lambda e: e.scalar_tensor_tensor(
            rr, kf, -CW3, rr, OP.mult, OP.add), reads=[ev_r], sig=True)
        ev_c1 = _wrap_pm_pi(env, rw, rr, 0.0, [ev_r], kf)
        ev_s1 = p.comp("act", lambda e: e.activation(snc, rw, AF.Sin),
                       reads=[ev_c1], sig=True)
        ev_sc = p.comp("act", lambda e: e.activation(
            lsc, psum[5][:, 0:512], AF.Exp), reads=[ev_l, ev_s1], sig=True)
        ev_isc = p.comp("act", lambda e: e.activation(
            isc, psum[5][:, 0:512], AF.Exp, scale=-1.0),
            reads=[ev_l, ev_sc], sig=True)
        banks.read(5, ev_isc)
        banks.read(5, ev_sc)
        ev_m1 = p.comp("dve", lambda e: e.tensor_tensor(
            rw, snc, lsc, OP.mult), reads=[ev_s1, ev_sc], sig=True)
        ev_m2 = p.comp("dve", lambda e, tt=tt: e.tensor_scalar(
            tb_r.bf16(tabs["SSq"] * 2 + tt * 512,
                      tabs["SSq"] * 2 + (tt + 1) * 512),
            rw, SMALL[:, qs_col + 1:qs_col + 2], None, OP.mult),
            reads=[ev_m1, ev_qs] + (tb_r.deps if tt == 0 else []), sig=True)
        ev_m3 = p.comp("dve", lambda e: e.tensor_tensor(
            rw, snc, isc, OP.mult), reads=[ev_m2, ev_isc], sig=True)
        ev_m4 = p.comp("dve", lambda e, tt=tt: e.tensor_scalar(
            tb_r.bf16(tabs["SSk"] * 2 + tt * 512,
                      tabs["SSk"] * 2 + (tt + 1) * 512),
            rw, SMALL[:, ks_col + 1:ks_col + 2], None, OP.mult),
            reads=[ev_m3, ev_qs], sig=True)
        ev_c2 = _wrap_pm_pi(env, rw, rr, HALFPI_F, [ev_m4, ev_r], kf)
        ev_s2 = p.comp("act", lambda e: e.activation(snc, rw, AF.Sin),
                       reads=[ev_c2, ev_m4], sig=True)
        ev_m5 = p.comp("dve", lambda e: e.tensor_tensor(
            rw, snc, lsc, OP.mult), reads=[ev_s2, ev_sc], sig=True)
        ev_m6 = p.comp("dve", lambda e, tt=tt: e.tensor_scalar(
            tb_r.bf16(tabs["CSq"] * 2 + tt * 512,
                      tabs["CSq"] * 2 + (tt + 1) * 512),
            rw, SMALL[:, qs_col:qs_col + 1], None, OP.mult),
            reads=[ev_m5, ev_qs], sig=True)
        ev_m7 = p.comp("dve", lambda e: e.tensor_tensor(
            rw, snc, isc, OP.mult), reads=[ev_m6, ev_isc], sig=True)
        ev_m8 = p.comp("dve", lambda e, tt=tt: e.tensor_scalar(
            tb_r.bf16(tabs["CSk"] * 2 + tt * 512,
                      tabs["CSk"] * 2 + (tt + 1) * 512),
            rw, SMALL[:, ks_col:ks_col + 1], None, OP.mult),
            reads=[ev_m7, ev_qs], sig=True)
        ev_prev = ev_m8
        out_evs += [ev_m2, ev_m4, ev_m6, ev_m8]
    arena.free(ts_r, out_evs[-4:])
    return tabs, out_evs


def _l1ffn(env, b, ln_evs, att_evs):
    p, banks, arena = env["p"], env["banks"], env["arena"]
    psum, psum_t, scr, W = env["psum"], env["psum_t"], env["scr"], env["W"]
    SMALL = env["sbuf"]["SMALL"]
    xint_r = arena.alloc("xint2", EC * TZ + 8)
    ev_z0 = p.comp("dve", lambda e: e.memset(
        xint_r.f32(0, EC * TZ).rearrange("p (c z) -> p c z", z=TZ)
        [:, :, 0:1], 0.0), reads=xint_r.deps, sig=True)
    ld_evs = [ev_z0]
    for ec in range(EC):
        ld_evs.append(p.dma("sp", lambda e, ec=ec: e.dma_start(
            out=xint_r.f32(ec * TZ + 1, ec * TZ + 1 + T),
            in_=scr["xn1"][ec]), reads=ln_evs + xint_r.deps))
    state = ld_evs

    def xf_ap(ec, c0, c1, shift=False):
        o = ec * TZ + (0 if shift else 1)
        return xint_r.f32(o + c0, o + c1)

    for l in range(2):
        state = _l1ffn_layer(env, b, l, xint_r, xf_ap, state)
    return xint_r, state


def _l1ffn_layer(env, b, l, xint_r, xf_ap, state):
    p, banks, arena = env["p"], env["banks"], env["arena"]
    psum, psum_t, scr, W = env["psum"], env["psum_t"], env["scr"], env["W"]
    SMALL = env["sbuf"]["SMALL"]
    NG = 4

    bev = []
    for (col, wname) in ((1100, "L1_ffn_q_scale"), (1102, "L1_ffn_k_scale")):
        for rep in range(2):
            bev.append(p.dma("sp", lambda e, col=col, wname=wname, rep=rep:
                             e.dma_start(
                out=SMALL[rep * 64:(rep + 1) * 64, col:col + 1],
                in_=W[wname][l].rearrange("(a d) -> d a", a=1)),
                reads=state))
            bev.append(p.dma("sp", lambda e, col=col, wname=wname, rep=rep:
                             e.dma_start(
                out=SMALL[rep * 64:rep * 64 + 32, col + 1:col + 2],
                in_=W[wname][l, 32:64].rearrange("(a d) -> d a", a=1)),
                reads=state))
            bev.append(p.dma("sp", lambda e, col=col, wname=wname, rep=rep:
                             e.dma_start(
                out=SMALL[rep * 64 + 32:rep * 64 + 64, col + 1:col + 2],
                in_=W[wname][l, 0:32].rearrange("(a d) -> d a", a=1)),
                reads=state))
    bev.append(p.dma("sp", lambda e: e.dma_start(
        out=SMALL[:, 1104:1112],
        in_=W["L1_ffn_ln_g"][l].rearrange("(c p) -> p c", p=128),
        allow_slow_non_contiguous=True), reads=state))
    bev.append(p.dma("sp", lambda e: e.dma_start(
        out=SMALL[:, 1112:1120],
        in_=W["L1_ffn_ln_b"][l].rearrange("(c p) -> p c", p=128),
        allow_slow_non_contiguous=True), reads=state))
    bev.append(p.dma("sp", lambda e: e.dma_start(
        out=SMALL[:, 1120:1141],
        in_=W["L1_ffn_b1"][l, 0:2688].rearrange("(c p) -> p c", p=128),
        allow_slow_non_contiguous=True), reads=state))
    bev.append(p.dma("sp", lambda e: e.dma_start(
        out=SMALL[0:42, 1141:1142],
        in_=W["L1_ffn_b1"][l, 2688:2730].rearrange("(a d) -> d a", a=1)), reads=state))
    bev.append(p.dma("sp", lambda e: e.dma_start(
        out=SMALL[:, 1142:1163],
        in_=W["L1_ffn_b1"][l, 2730:5418].rearrange("(c p) -> p c", p=128),
        allow_slow_non_contiguous=True), reads=state))
    bev.append(p.dma("sp", lambda e: e.dma_start(
        out=SMALL[0:42, 1163:1164],
        in_=W["L1_ffn_b1"][l, 5418:5460].rearrange("(a d) -> d a", a=1)), reads=state))
    bev.append(p.dma("sp", lambda e: e.dma_start(
        out=SMALL[:, 1164:1172],
        in_=W["L1_ffn_b2"][l].rearrange("(c p) -> p c", p=128),
        allow_slow_non_contiguous=True), reads=state))
    ev_qs = bev[:12]

    tb_r = arena.alloc(f"tabs{l}", 2 * T)
    tabs, tab_evs = _rotary_tables(env, l, tb_r, 1100, 1102, bev, state)

    xb_r = arena.alloc(f"ixb{l}", (EC * TZ) // 2 + 8)
    xb_evs = [p.comp("dve", lambda e: e.memset(
        xb_r.bf16(0, EC * TZ).rearrange("p (c z) -> p c z", z=TZ)
        [:, :, 0:1], 0.0), reads=state + xb_r.deps, sig=True)]
    for ec in range(EC):
        xb_evs.append(p.comp("act", lambda e, ec=ec: e.copy(
            xb_r.bf16(ec * TZ + 1, ec * TZ + 1 + T),
            xf_ap(ec, 0, T)), reads=state + xb_r.deps, sig=True))
    state = state + xb_evs

    def xb_ap(ec, c0, c1, shift=False):
        o = ec * TZ + (0 if shift else 1)
        return xb_r.bf16(o + c0, o + c1)

    def tab(name, tt):
        o = tabs[name] * 2
        return tb_r.bf16(o + tt * 512, o + (tt + 1) * 512)

    wq_r = arena.alloc("iq_w", 2048, side="high")
    qr_r = arena.alloc("iq_q", NG * T // 2)
    kr_r = arena.alloc("iq_k", NG * T // 2)
    vt_r = arena.alloc("iq_v", NTC * 520 // 2)
    ev_vones = p.comp("dve", lambda e: e.memset(
        vt_r.bf16(0, NTC * 520).rearrange(
            "p (c k) -> p c k", k=65)[:, :, 64:65], 1.0),
        reads=vt_r.deps, sig=True)
    proj_evs = []
    wslot = [list(wq_r.deps)]

    def proj(part, sink_r, tag):
        ev_w = p.dma("gp", lambda e: e.dma_start(
            out=wq_r.bf16(0, EC * 512).rearrange("p (j m) -> p j m", m=512),
            in_=W["L1_ffn_wqkv"][l][:, part * 512:(part + 1) * 512].rearrange(
                "(j p) m -> p j m", p=128)),
            reads=wslot[0] + state, stream="w")
        evs = []
        if part < 2:
            for g in range(NG):
                for tt in range(NTT):
                    bk = (g * NTT + tt) % 2
                    deps = banks.acquire(bk)
                    ev_mm = p.comp("pe", lambda e, g=g, tt=tt, bk=bk: [
                        e.matmul(psum[bk][:, 0:512],
                                 wq_r.bf16(0, EC * 512).rearrange(
                                     "p (j m) -> p j m", m=512)[
                                     :, j, g * 128:(g + 1) * 128],
                                 xb_ap(j, tt * 512, (tt + 1) * 512,
                                       shift=(j >= 4)),
                                 start=(j == 0), stop=(j == EC - 1))
                        for j in range(EC)],
                        reads=[ev_w] + state + deps, sig=True)
                    banks.wrote(bk, ev_mm)
                    dst = sink_r.bf16(g * T + tt * 512, g * T + tt * 512 + 512)
                    ev_s = p.comp("act", lambda e, dst=dst, bk=bk: e.copy(
                        dst, psum[bk][:, 0:512]),
                        reads=[ev_mm] + (sink_r.deps if g == 0 and tt == 0
                                         else []), sig=True)
                    banks.read(bk, ev_s)
                    evs.append(ev_s)
        else:
            for tc in range(NTC):
                bk = tc % 2
                deps = banks.acquire(bk)
                ev_mm = p.comp("pe", lambda e, tc=tc, bk=bk: [
                    e.matmul(psum[bk][:, 0:512],
                             xb_ap(j, tc * 128, (tc + 1) * 128, shift=(j >= 4)),
                             wq_r.bf16(0, EC * 512).rearrange(
                                 "p (j m) -> p j m", m=512)[:, j, :],
                             start=(j == 0), stop=(j == EC - 1))
                    for j in range(EC)],
                    reads=[ev_w] + state + deps, sig=True)
                banks.wrote(bk, ev_mm)
                dst = sink_r.bf16(tc * 520, (tc + 1) * 520).rearrange(
                    "p (h k) -> p h k", k=65)[:, :, 0:64]
                ev_s = p.comp("act", lambda e, dst=dst, bk=bk: e.copy(
                    dst, psum[bk][:, 0:512].rearrange(
                        "p (h k) -> p h k", k=64)),
                    reads=[ev_mm, ev_vones] + (sink_r.deps if tc == 0 else []),
                    sig=True)
                banks.read(bk, ev_s)
                evs.append(ev_s)
        wslot[0] = evs[-2:]
        return evs

    q_evs = proj(0, qr_r, "q")
    k_evs = proj(1, kr_r, "k")
    v_evs = proj(2, vt_r, "v")
    proj_evs = q_evs + k_evs + v_evs
    arena.free(xb_r, proj_evs[-1:] + q_evs[-1:] + k_evs[-1:])

    ws_r = arena.alloc("iq_s", 1024, side="high")
    rot_evs = []
    nrm = SMALL[0:16, 0:512]
    for tt in range(NTT):
        deps4 = banks.acquire(4)
        ev_nm = None
        sq_evs = []
        for s, src_r in ((0, qr_r), (1, kr_r)):
            for g in range(NG):
                q2 = ws_r.bf16(0, 512)
                ev_sq = p.comp("act", lambda e, src_r=src_r, g=g, tt=tt,
                               q2=q2: e.activation(
                    q2, src_r.bf16(g * T + tt * 512, g * T + tt * 512 + 512),
                    AF.Square),
                    reads=proj_evs[-1:] + ws_r.deps +
                          ([ev_nm] if ev_nm else []), sig=True)
                ev_nm = p.comp("pe", lambda e, g=g, s=s, q2=q2: e.matmul(
                    psum[4][0:16, 0:512],
                    env["e16_b"][:, (g * 2 + s) * 16:(g * 2 + s) * 16 + 16],
                    q2, start=(s == 0 and g == 0),
                    stop=(s == 1 and g == NG - 1)),
                    reads=[ev_sq, env["ev_constb"]] +
                          (deps4 if s == 0 and g == 0 else []),
                    sig=True)
                sq_evs.append(ev_sq)
        banks.wrote(4, ev_nm)
        ev_sr = p.comp("act", lambda e: e.activation(
            nrm, psum[4][0:16, 0:512], AF.Sqrt),
            reads=[ev_nm] + env["rowchain"][-4:], sig=True)
        banks.read(4, ev_sr)
        ev_cl = p.comp("dve", lambda e: e.tensor_scalar(
            nrm, nrm, 1e-12, None, OP.max), reads=[ev_sr], sig=True)
        ev_rc = p.comp("dve", lambda e: e.reciprocal(nrm, nrm),
                       reads=[ev_cl], sig=True)
        nrmb = SMALL[0:16, 1172:1428].bitcast(BF16)
        ev_rc = p.comp("dve", lambda e: e.tensor_copy(nrmb, nrm),
                       reads=[ev_rc], sig=True)
        for s, src_r in ((0, qr_r), (1, kr_r)):
            csn = "CSq" if s == 0 else "CSk"
            ssn = "SSq" if s == 0 else "SSk"
            for g in range(NG):
                deps5 = banks.acquire(5)
                ev_rb = p.comp("pe", lambda e, g=g, s=s: e.matmul(
                    psum[5][:, 0:512],
                    env["ebr_f"][0:16, (g * 2 + s) * 128:
                                 (g * 2 + s) * 128 + 128],
                    nrmb, start=True, stop=True),
                    reads=[ev_rc, env["ev_gconst"]] + deps5, sig=True)
                banks.wrote(5, ev_rb)
                env["rowchain"] = env["rowchain"][-3:] + [ev_rb]
                src = src_r.bf16(g * T + tt * 512, g * T + tt * 512 + 512)
                qn = ws_r.bf16(512, 1024)
                ev_qn = p.comp("dve", lambda e, src=src, qn=qn: e.tensor_tensor(
                    qn, src, psum[5][:, 0:512], OP.mult),
                    reads=[ev_rb] + sq_evs + rot_evs[-1:], sig=True)
                banks.read(5, ev_qn)
                deps6 = banks.acquire(6)
                ev_rh = p.comp("pe", lambda e, qn=qn: e.matmul(
                    psum[6][:, 0:512], env["rot_pi_b"], qn,
                    start=True, stop=True),
                    reads=[ev_qn, env["ev_constb"]] + deps6, sig=True)
                banks.wrote(6, ev_rh)
                t1 = ws_r.bf16(1024, 1536)
                ev_t1 = p.comp("dve", lambda e, t1=t1, ssn=ssn, tt=tt:
                               e.tensor_tensor(
                    t1, psum[6][:, 0:512], tab(ssn, tt), OP.mult),
                    reads=[ev_rh] + tab_evs[-1:] + rot_evs[-1:], sig=True)
                banks.read(6, ev_t1)
                t2 = ws_r.bf16(1536, 2048)
                ev_t2 = p.comp("dve", lambda e, t2=t2, qn=qn, csn=csn, tt=tt:
                               e.tensor_tensor(
                    t2, qn, tab(csn, tt), OP.mult),
                    reads=[ev_qn, ev_t1], sig=True)
                ev_o = p.comp("dve", lambda e, src=src, t1=t1, t2=t2:
                              e.tensor_tensor(src, t1, t2, OP.add),
                              reads=[ev_t2], sig=True)
                rot_evs.append(ev_o)

    wo_r = arena.alloc("iwo", 2048, side="high")
    ev_wo = p.dma("gp", lambda e: e.dma_start(
        out=wo_r.bf16(0, 4 * D).rearrange("p (j m) -> p j m", m=D),
        in_=W["L1_ffn_wo"][l].rearrange("(j p) m -> p j m", p=128)),
        reads=wo_r.deps + state, stream="w")
    pt_r = arena.alloc("ipt", 2 * 8 * 1024 // 2, side="high")
    ot_r = arena.alloc("iot", NG * 512 // 2, side="high")
    rbs_r = arena.alloc(f"irbs{l}", 512, side="high")
    res_upd = []
    pt_prev = {0: list(pt_r.deps), 1: list(pt_r.deps)}
    ot_prev = {g: list(ot_r.deps) for g in range(NG)}
    jobs = [(w, g) for w in range(NTT) for g in range(NG)]

    def plan_scores_loc(jid):
        w, g = jobs[jid]
        sl = jid % 2
        base = sl * 8 * 1024
        kc0 = 4 * (w - 1)
        tkcs = [c for c in range(8) if kc0 + c >= 0]
        exp_evs = []
        for ci, c in enumerate(tkcs):
            kc = kc0 + c
            bkA = 2 * (ci % 2)
            bkB = bkA + 1
            pair_t = env["psum_pair"][ci % 2]
            depsA = banks.acquire(bkA)
            depsB = banks.acquire(bkB)
            ev_mm = p.comp("pe", lambda e, g=g, w=w, kc=kc, bkA=bkA,
                           bkB=bkB: [
                e.matmul(psum[bkA][:, 0:512],
                         kr_r.bf16(g * T + kc * 128,
                                   g * T + (kc + 1) * 128)[0:64, :],
                         qr_r.bf16(g * T + w * 512,
                                   g * T + (w + 1) * 512)[0:64, :],
                         start=True, stop=True, tile_position=(0, 0)),
                e.matmul(psum[bkB][:, 0:512],
                         kr_r.bf16(g * T + kc * 128,
                                   g * T + (kc + 1) * 128)[64:128, :],
                         qr_r.bf16(g * T + w * 512,
                                   g * T + (w + 1) * 512)[64:128, :],
                         start=True, stop=True, tile_position=(64, 0))],
                reads=rot_evs[-1:] + depsA + depsB, sig=True)
            banks.wrote(bkA, ev_mm)
            banks.wrote(bkB, ev_mm)
            dstAB = pt_r.bf16(base + ci * 1024, base + (ci + 1) * 1024)
            ev_e = p.comp("act", lambda e, dstAB=dstAB, pair_t=pair_t:
                          e.activation(
                dstAB, pair_t[:, 0:1024], AF.Exp,
                bias=env["cell"][-SHIFT_LOC], scale=8.0),
                reads=[ev_mm, env["ev_cells"]] + pt_prev[sl], sig=True)
            banks.read(bkA, ev_e)
            banks.read(bkB, ev_e)
            dstAB = pt_r.bf16(base + ci * 1024,
                              base + (ci + 1) * 1024).rearrange(
                "p (n m) -> p n m", n=2)
            mka = env["loc_mask"][:, c * 512:(c + 1) * 512]
            ev_mk = p.comp("dve", lambda e, dstAB=dstAB, mka=mka:
                           e.tensor_tensor(
                dstAB, dstAB,
                mka.unsqueeze(1).broadcast_to([128, 2, 512]), OP.mult),
                reads=[ev_e, env["ev_constb"]], sig=True)
            exp_evs.append(ev_mk)
        return (sl, base, tuple(tkcs), exp_evs)

    rsA = SMALL[64:65, 0:512]
    rsB = SMALL[64:65, 512:1024]
    ones64 = SMALL[64:65, 1592:1656]
    p7f = env["psum_t"][:, :].bitcast(F32)
    win_evs = []
    res_upd = res_upd

    def _ifinish(rec):
        nonlocal win_evs
        g, w = rec["g"], rec["w"]
        rbb = rec["i"] % 4
        deps_rb = banks.acquire(rbb)
        ev_rb = p.comp("pe", lambda e, rbb=rbb: [
            e.matmul(psum[rbb][0:64, 0:512], ones64, rsA,
                     start=True, stop=True, tile_position=(64, 0)),
            e.matmul(psum[rbb][64:128, 0:512], ones64, rsB,
                     start=True, stop=True, tile_position=(64, 64))],
            reads=[rec["ev_rA"], rec["ev_rB"], env["ev_ones64"]] + deps_rb,
            sig=True)
        banks.wrote(rbb, ev_rb)
        env["rowchain"] = [ev_rb]
        rec["ev_rb"] = ev_rb
        ev_rbe = p.comp("act", lambda e, rbb=rbb: e.copy(
            rbs_r.f32(0, 512), psum[rbb][:, 0:512]),
            reads=[ev_rb] + rbs_r.deps + env.get("_irbs_last", []), sig=True)
        banks.read(rbb, ev_rbe)
        dst = ot_r.bf16(g * 512, (g + 1) * 512)
        ev_nA = p.comp("dve", lambda e, dst=dst, rec=rec: e.tensor_tensor(
            dst[0:64, :], psum[rec["pvA"]][0:64, 0:512],
            rbs_r.f32(0, 512)[0:64, :], OP.mult),
            reads=[rec["ev_pv"], ev_rbe] + ot_prev[g], sig=True)
        ev_n = p.comp("dve", lambda e, dst=dst, rec=rec: e.tensor_tensor(
            dst[64:128, :], rec["pvB_ap"][0:64, 0:512],
            rbs_r.f32(0, 512)[64:128, :], OP.mult),
            reads=[rec["ev_pv"], ev_rbe, ev_nA], sig=True)
        banks.read(rec["pvA"], ev_nA)
        banks.read(rec["pvB"], ev_n)
        env["_irbs_last"] = [ev_n, ev_nA]
        ot_prev[g] = [ev_n, ev_nA]
        win_evs.append(ev_n)
        if g == NG - 1:
            for dc in range(EC):
                bk = dc % 2
                deps = banks.acquire(bk)
                ev_mm = p.comp("pe", lambda e, dc=dc, bk=bk: [
                    e.matmul(psum[bk][:, 0:512],
                             wo_r.bf16(0, 4 * D).rearrange(
                                 "p (j m) -> p j m", m=D)[
                                 :, j, dc * 128:(dc + 1) * 128],
                             ot_r.bf16(j * 512, (j + 1) * 512),
                             start=(j == 0), stop=(j == 3))
                    for j in range(4)],
                    reads=win_evs + [ev_wo] + deps, sig=True)
                banks.wrote(bk, ev_mm)
                ev_add = p.comp("dve", lambda e, dc=dc, w=w, bk=bk:
                                e.tensor_tensor(
                    xf_ap(dc, w * 512, (w + 1) * 512),
                    xf_ap(dc, w * 512, (w + 1) * 512),
                    psum[bk][:, 0:512], OP.add),
                    reads=[ev_mm] + state, sig=True)
                banks.read(bk, ev_add)
                res_upd.append(ev_add)
            win_evs = []

    prev = None
    nxt = plan_scores_loc(0)
    for jid, (w, g) in enumerate(jobs):
        sl, base, tkcs, exp_evs = nxt
        if jid + 1 < len(jobs):
            nxt = plan_scores_loc(jid + 1)
        kc0 = 4 * (w - 1)
        pvA, pvB = (4, 5) if jid % 2 == 0 else (6, 7)
        pvB_ap = p7f if pvB == 7 else psum[pvB]
        depsA = banks.acquire(pvA)
        depsB = banks.acquire(pvB)
        ev_pv = p.comp("pe", lambda e, g=g, base=base, tkcs=tkcs, kc0=kc0,
                       pvA=pvA, pvB_ap=pvB_ap: [
            mm for ci, c in enumerate(tkcs) for mm in (
                e.matmul(psum[pvA][0:65, 0:512],
                         vt_r.bf16((kc0 + c) * 520 + (2 * g) * 65,
                                   (kc0 + c) * 520 + (2 * g) * 65 + 65),
                         pt_r.bf16(base + ci * 1024, base + ci * 1024 + 512),
                         start=(ci == 0), stop=(ci == len(tkcs) - 1),
                         tile_position=(0, 0)),
                e.matmul(pvB_ap[0:65, 0:512],
                         vt_r.bf16((kc0 + c) * 520 + (2 * g + 1) * 65,
                                   (kc0 + c) * 520 + (2 * g + 1) * 65 + 65),
                         pt_r.bf16(base + ci * 1024 + 512,
                                   base + (ci + 1) * 1024),
                         start=(ci == 0), stop=(ci == len(tkcs) - 1),
                         tile_position=(0, 0)))],
            reads=exp_evs + depsA + depsB + v_evs[-1:] + [ev_vones],
            sig=True)
        banks.wrote(pvA, ev_pv)
        banks.wrote(pvB, ev_pv)
        if prev is not None:
            _ifinish(prev)
        ev_rA = p.comp("dve", lambda e, pvA=pvA: e.reciprocal(
            rsA, psum[pvA][64:65, 0:512]),
            reads=[ev_pv] + ([prev["ev_rb"]] if prev else []) +
                  env["rowchain"][-2:], sig=True)
        ev_rB = p.comp("dve", lambda e, pvB_ap=pvB_ap: e.reciprocal(
            rsB, pvB_ap[64:65, 0:512]), reads=[ev_rA, ev_pv], sig=True)
        banks.read(pvA, ev_rA)
        banks.read(pvB, ev_rB)
        prev = dict(i=jid, w=w, g=g, pvA=pvA, pvB=pvB, pvB_ap=pvB_ap,
                    ev_pv=ev_pv, ev_rA=ev_rA, ev_rB=ev_rB)
        pt_prev[sl] = [ev_pv] + exp_evs[-1:]
    _ifinish(prev)
    state = state + res_upd
    if DBG.get("stop") == f"iatt{l}":
        raise StopBuild(_dump_xint(env, xf_ap, state))
    for r in (wq_r, qr_r, kr_r, vt_r, ws_r, wo_r, pt_r, ot_r, tb_r, rbs_r):
        arena.free(r, res_upd[-2:] + pt_prev[0] + pt_prev[1] +
                   env.get("_irbs_last", []))

    xh_r = arena.alloc("ixh", EC * T // 2)
    ln2 = _layernorm(
        env,
        lambda ec: (xf_ap(ec, 0, T, shift=(ec >= 4)), state),
        [("bf16", lambda ec, tt: xh_r.bf16(ec * T + tt * 512,
                                           ec * T + tt * 512 + 512))],
        f"iln{l}b{b}",
        gamma_beta=lambda ec: (SMALL[:, 1104 + ec:1105 + ec],
                               SMALL[:, 1112 + ec:1113 + ec]),
        gb_reads=bev)
    w1_r = arena.alloc("iw1", 2 * 2048, side="high")
    w2_r = arena.alloc("iw2", 2 * 2048, side="high")
    h1_r = arena.alloc("ih1", 22 * 512 // 2, side="high")
    ag_r = arena.alloc("iag", 512, side="high")
    w1slot = {0: list(w1_r.deps), 1: list(w1_r.deps)}
    w2slot = {0: list(w2_r.deps), 1: list(w2_r.deps)}
    h1_prev = {0: list(h1_r.deps) + list(ag_r.deps),
               1: list(h1_r.deps)}
    ffn_out = []
    CH = [(i * 128, 128) for i in range(21)] + [(2688, 42)]
    for tt in range(NTT):
        h_evs = []
        for pc, (r0, nr) in enumerate(CH):
            sl = pc % 2
            w1a = w1_r.bf16(sl * 4096, sl * 4096 + 2048)[:, 0:EC * nr]
            w1g = w1_r.bf16(sl * 4096 + 2048, (sl + 1) * 4096)[:, 0:EC * nr]
            ev_w1 = [
                p.dma("gp", lambda e, r0=r0, nr=nr, w1a=w1a: e.dma_start(
                    out=w1a.rearrange("p (j m) -> p j m", m=nr),
                    in_=W["L1_ffn_w1"][l][:, r0:r0 + nr].rearrange(
                        "(j p) m -> p j m", p=128)),
                    reads=w1slot[sl], stream="w"),
                p.dma("gp", lambda e, r0=r0, nr=nr, w1g=w1g: e.dma_start(
                    out=w1g.rearrange("p (j m) -> p j m", m=nr),
                    in_=W["L1_ffn_w1"][l][:, 2730 + r0:2730 + r0 + nr]
                        .rearrange("(j p) m -> p j m", p=128)),
                    stream="w"),
            ]
            bka = pc % 2
            bkg = 2 + pc % 2
            depsa = banks.acquire(bka)
            depsg = banks.acquire(bkg)
            ev_a = p.comp("pe", lambda e, tt=tt, nr=nr, w1a=w1a, bka=bka: [
                e.matmul(psum[bka][0:nr, 0:512],
                         w1a[:, j * nr:(j + 1) * nr],
                         xh_r.bf16(j * T + tt * 512, j * T + tt * 512 + 512),
                         start=(j == 0), stop=(j == EC - 1))
                for j in range(EC)],
                reads=ev_w1 + ln2 + depsa, sig=True)
            banks.wrote(bka, ev_a)
            ev_g = p.comp("pe", lambda e, tt=tt, nr=nr, w1g=w1g, bkg=bkg: [
                e.matmul(psum[bkg][0:nr, 0:512],
                         w1g[:, j * nr:(j + 1) * nr],
                         xh_r.bf16(j * T + tt * 512, j * T + tt * 512 + 512),
                         start=(j == 0), stop=(j == EC - 1))
                for j in range(EC)],
                reads=ev_w1 + ln2 + depsg, sig=True)
            banks.wrote(bkg, ev_g)
            w1slot[sl] = [ev_a, ev_g]
            acol = 1120 + pc if nr == 128 else 1141
            gcol = 1142 + pc if nr == 128 else 1163
            ap_t = ag_r.bf16(0, 512)[0:nr, :]
            ev_ap = p.comp("act", lambda e, ap_t=ap_t, bka=bka, nr=nr,
                           acol=acol: e.activation(
                ap_t, psum[bka][0:nr, 0:512], AF.Identity,
                bias=SMALL[0:nr, acol:acol + 1]),
                reads=[ev_a] + bev + h1_prev[0], sig=True)
            banks.read(bka, ev_ap)
            gp_t = ag_r.bf16(512, 1024)[0:nr, :]
            ev_gp = p.comp("act", lambda e, gp_t=gp_t, bkg=bkg, nr=nr,
                           gcol=gcol: e.activation(
                gp_t, psum[bkg][0:nr, 0:512], AF.Gelu,
                bias=SMALL[0:nr, gcol:gcol + 1]),
                reads=[ev_g] + bev + h1_prev[0], sig=True)
            banks.read(bkg, ev_gp)
            dst = h1_r.bf16(pc * 512, pc * 512 + 512)[0:nr, :]
            ev_h = p.comp("dve", lambda e, dst=dst, ap_t=ap_t, gp_t=gp_t:
                          e.tensor_tensor(dst, ap_t, gp_t, OP.mult),
                          reads=[ev_ap, ev_gp] + h1_prev[1], sig=True)
            h1_prev[0] = [ev_h]
            h_evs.append(ev_h)
        h1_prev[1] = h_evs[-2:]
        for dc in range(EC):
            sl = dc % 2
            w2s = w2_r.bf16(sl * 4096, sl * 4096 + 2816)
            ev_w2a = p.dma("gp", lambda e, dc=dc, w2s=w2s: e.dma_start(
                out=w2s[:, 0:21 * 128].rearrange("p (c m) -> p c m", m=128),
                in_=W["L1_ffn_w2"][l, 0:2688, dc * 128:(dc + 1) * 128]
                    .rearrange("(c p) m -> p c m", p=128)),
                reads=w2slot[sl], stream="w")
            ev_w2b = p.dma("gp", lambda e, dc=dc, w2s=w2s: e.dma_start(
                out=w2s[0:42, 21 * 128:22 * 128],
                in_=W["L1_ffn_w2"][l, 2688:2730, dc * 128:(dc + 1) * 128]),
                stream="w")
            ev_w2 = [ev_w2a, ev_w2b]
            bk = 4 + dc % 2
            deps = banks.acquire(bk)
            ev_mm = p.comp("pe", lambda e, dc=dc, bk=bk, w2s=w2s: [
                e.matmul(psum[bk][:, 0:512],
                         w2s[0:nr2, cc * 128:cc * 128 + 128],
                         h1_r.bf16(cc * 512, cc * 512 + 512)[0:nr2, :],
                         start=(cc == 0), stop=(cc == 21))
                for cc, (r02, nr2) in enumerate(CH)],
                reads=ev_w2 + h_evs + deps, sig=True)
            banks.wrote(bk, ev_mm)
            w2slot[sl] = [ev_mm]
            ev_add = p.comp("dve", lambda e, dc=dc, tt=tt, bk=bk:
                            e.scalar_tensor_tensor(
                xf_ap(dc, tt * 512, (tt + 1) * 512),
                psum[bk][:, 0:512], SMALL[:, 1164 + dc:1165 + dc],
                xf_ap(dc, tt * 512, (tt + 1) * 512), OP.add, OP.add),
                reads=[ev_mm] + state + bev, sig=True)
            banks.read(bk, ev_add)
            ffn_out.append(ev_add)
        h1_prev[1] = ffn_out[-2:]
    state = state + ln2 + ffn_out
    if DBG.get("stop") == f"iffn{l}":
        raise StopBuild(_dump_xint(env, xf_ap, state))
    for r in (xh_r, w1_r, w2_r, h1_r, ag_r):
        arena.free(r, ffn_out[-2:])
    return state
